# version 11 (final)
# baseline (speedup 1.0000x reference)
import numpy as np
import ml_dtypes
from contextlib import ExitStack
import concourse.bass as bass
import concourse.mybir as mybir
from concourse.bass_utils import run_bass_kernel_spmd

F32 = mybir.dt.float32
BF16 = mybir.dt.bfloat16
U8 = mybir.dt.uint8
I32 = mybir.dt.int32
AF = mybir.ActivationFunctionType
ALU = mybir.AluOpType
AX = mybir.AxisListType
BF = ml_dtypes.bfloat16

H = 4096
TB = 4096
TC = 1024
NCORE = 8
I_FFN = 11008
EPS_LN = 1e-5

ENGINES = ("sync", "vector", "scalar", "tensor", "gpsimd")


class Em:

    def __init__(self, name):
        self.nc = bass.Bass("TRN2", target_bir_lowering=False)
        self.es = ExitStack()
        self.q = {e: [] for e in ENGINES}
        self._sems = {}
        self._names = set()

    def _uname(self, base):
        i = 0
        nm = base
        while nm in self._names:
            i += 1
            nm = f"{base}_{i}"
        self._names.add(nm)
        return nm

    def dram_in(self, name, shape, dtype):
        return self.nc.dram_tensor(name, list(shape), dtype, kind="ExternalInput").ap()

    def dram_out(self, name, shape, dtype):
        return self.nc.dram_tensor(name, list(shape), dtype, kind="ExternalOutput").ap()

    def dram_tmp(self, name, shape, dtype, addr_space="Local"):
        return self.nc.dram_tensor(name, list(shape), dtype, addr_space=addr_space).ap()

    def sb(self, name, shape, dtype=F32):
        return self.es.enter_context(self.nc.sbuf_tensor(self._uname(name), list(shape), dtype))

    def ps(self, name, shape, dtype=F32):
        return self.es.enter_context(self.nc.psum_tensor(self._uname(name), list(shape), dtype))

    def ps_scope(self, name, shape, dtype=F32):
        cm = self.nc.psum_tensor(self._uname(name), list(shape), dtype)
        t = cm.__enter__()
        return t, (lambda cm=cm: cm.__exit__(None, None, None))

    def sb_scope(self, name, shape, dtype=F32):
        cm = self.nc.sbuf_tensor(self._uname(name), list(shape), dtype)
        t = cm.__enter__()
        return t, (lambda cm=cm: cm.__exit__(None, None, None))

    @staticmethod
    def _canon(name):
        if name.endswith("ct"):
            return "_T"
        if name.endswith("cs"):
            return "_S"
        if name.endswith("v"):
            return "_V"
        if name.endswith("gi"):
            return "_GI"
        return name

    def sem(self, name):
        name = self._canon(name)
        if name not in self._sems:
            h = self.es.enter_context(self.nc.semaphore(name))
            self._sems[name] = [h, 0]
        return self._sems[name]

    def count(self, name):
        return self.sem(name)[1]

    def do(self, eng, fn, wait=(), inc=None, incby=1):
        waits = []
        for w in wait:
            if isinstance(w, tuple):
                s = self.sem(w[0])
                waits.append((s[0], w[1]))
            else:
                s = self.sem(w)
                waits.append((s[0], s[1]))
        if inc is not None:
            s = self.sem(inc)
            s[1] += (incby if incby is not None else 1)
            hinc = s[0]
        else:
            hinc = None

        def cmd(engine, fn=fn, waits=waits, hinc=hinc, incby=incby):
            for h, v in waits:
                if v > 0:
                    engine.wait_ge(h, v)
            ins = fn(engine)
            if hinc is not None:
                if ins is None:
                    raise RuntimeError("inc on fn returning no instruction")
                if incby is None:
                    ins.then_inc(hinc)
                else:
                    ins.then_inc(hinc, incby)
        self.q[eng].append(cmd)

    def wait(self, eng, *sems):
        self.do(eng, lambda e: None, wait=sems)

    def phase_barrier(self, exclude=("ccA",)):
        snap = [(nm, s[1]) for nm, s in self._sems.items()
                if s[1] > 0 and nm not in exclude]
        for e in ENGINES:
            self.do(e, lambda _: None, wait=snap)

    def barrier_on(self, *sems):
        snap = [(nm, self.count(nm)) for nm in sems]
        for e in ENGINES:
            self.do(e, lambda _: None, wait=snap)

    def build(self):
        with self.nc.Block() as block:
            for e in ENGINES:
                cmds = self.q[e]
                if not cmds:
                    continue
                getattr(block, e)(lambda engine, cmds=cmds: [c(engine) for c in cmds])
        self.es.close()
        return self.nc


def _run(em, in_maps):
    nc = em.build()
    res = run_bass_kernel_spmd(nc, in_maps, core_ids=list(range(NCORE)))
    return res.results


PERM_H = np.concatenate([np.arange(0, H, 2), np.arange(1, H, 2)])
PERM_I = np.concatenate([np.arange(0, I_FFN, 2), np.arange(1, I_FFN, 2)])
IPERM_H = np.argsort(PERM_H)


def _c(a):
    return np.ascontiguousarray(a)


def host_prep(inp):
    pc = [dict() for _ in range(NCORE)]
    shared = {}

    x0 = np.asarray(inp["L0_attn_hidden_states"])
    for c in range(NCORE):
        b, q = divmod(c, 4)
        pc[c]["x0T"] = _c(x0[b, TC * q:TC * (q + 1), :].T[PERM_H, :])

    gate_pk = np.asarray(inp["L0_ffn_gate_Wq"]).astype(np.uint8)
    up_pk = np.asarray(inp["L0_ffn_up_Wq"]).astype(np.uint8)
    down_pk = np.asarray(inp["L0_ffn_down_Wq"]).astype(np.uint8)
    gate_pkT = gate_pk.T[:, PERM_I]
    up_pkT = up_pk.T[:, PERM_I]
    down_pkT = down_pk[PERM_H, :].T
    gsT = np.asarray(inp["L0_ffn_gate_scale"]).T[:, PERM_I]
    gzT = np.asarray(inp["L0_ffn_gate_zero"]).T[:, PERM_I]
    usT = np.asarray(inp["L0_ffn_up_scale"]).T[:, PERM_I]
    uzT = np.asarray(inp["L0_ffn_up_zero"]).T[:, PERM_I]
    dsT = np.asarray(inp["L0_ffn_down_scale"]).T[:, PERM_H]
    dzT = np.asarray(inp["L0_ffn_down_zero"]).T[:, PERM_H]
    ISL = I_FFN // 8
    HSL = H // 8

    def _pad1408(a):
        out = np.zeros((a.shape[0], 1408), a.dtype)
        out[:, :ISL] = a
        return out
    for c in range(NCORE):
        isl = slice(ISL * c, ISL * (c + 1))
        hsl = slice(HSL * c, HSL * (c + 1))
        pc[c]["gate_pk"] = _pad1408(gate_pkT[:, isl])
        pc[c]["up_pk"] = _pad1408(up_pkT[:, isl])
        pc[c]["down_pk"] = _c(down_pkT[:, hsl])
        pc[c]["gate_sT"] = _pad1408(gsT[:, isl])
        pc[c]["up_sT"] = _pad1408(usT[:, isl])
        pc[c]["down_sT"] = _c(dsT[:, hsl])
        pc[c]["gate_zT"] = _pad1408(gzT[:, isl])
        pc[c]["up_zT"] = _pad1408(uzT[:, isl])
        pc[c]["down_zT"] = _c(dzT[:, hsl])

    wq = np.asarray(inp["L0_attn_w_qkv"])
    bq = np.asarray(inp["L0_attn_b_qkv"])
    for c in range(NCORE):
        q = c % 4
        rows = np.arange(768 * 2 * q, 768 * (2 * q + 2))
        pc[c]["wqkvT"] = _c(wq[rows][:, PERM_H].T)
        pc[c]["bqkv"] = _c(bq[rows])
    sel = np.zeros((4, 128), np.float32)
    for g in range(4):
        sel[g, 32 * g:32 * (g + 1)] = 1.0
    shared["sel4"] = sel
    shared["invf"] = _c((1.0 / (1e6 ** (np.arange(0, 64) / 64.0))).astype(np.float32))

    tri = np.triu(np.full((128, 128), -1e30, np.float32), 1)
    shared["tri_mask"] = tri
    half = np.zeros((128, 64), np.float32)
    half[64:, :] = -1e30
    shared["half_mask"] = half
    shared["ident"] = np.eye(128, dtype=BF)
    shared["ident_f"] = np.eye(128, dtype=np.float32)

    wd = np.asarray(inp["L0_attn_w_dense"])[PERM_H, :]
    wdT = wd.T
    shared["wd_panels"] = _c(wdT.reshape(4096, 32, 128).transpose(1, 0, 2))
    shared["b_dense"] = _c(np.asarray(inp["L0_attn_b_dense"])[PERM_H].reshape(32, 128).T)
    for c in range(NCORE):
        q = c % 4
        pc[c]["wdTs"] = _c(wdT[1024 * q:1024 * (q + 1), :])

    ipw = np.asarray(inp["L1_attn_in_proj_w"])
    I_SSM = 4096
    GN = 1024
    for c in range(NCORE):
        q = c % 4
        hrows = np.arange(64 * 16 * q, 64 * 16 * (q + 1))
        gate_rows = hrows
        xbc_rows = I_SSM + hrows
        b_rows = I_SSM + I_SSM + 256 * q + np.arange(256)
        c_rows = I_SSM + I_SSM + GN + 256 * q + np.arange(256)
        dt_rows = I_SSM + I_SSM + 2 * GN + np.arange(16 * q, 16 * (q + 1))
        rows = np.concatenate([gate_rows, xbc_rows, b_rows, c_rows, dt_rows])
        wslice = ipw[rows][:, PERM_H].T.astype(np.float32)
        wpad = np.zeros((4096, 2688), np.float32)
        wpad[:, :2576] = wslice
        pc[c]["ipj_panels"] = _c(wpad.reshape(4096, 21, 128).transpose(1, 0, 2))
    cw = np.asarray(inp["L1_attn_conv_weight"])
    cb = np.asarray(inp["L1_attn_conv_bias"])
    dtb = np.asarray(inp["L1_attn_dt_bias"])
    alog = np.asarray(inp["L1_attn_A_log"])
    Dp = np.asarray(inp["L1_attn_D"])
    nw = np.asarray(inp["L1_attn_norm_weight"])
    for c in range(NCORE):
        q = c % 4
        hrows = np.arange(64 * 16 * q, 64 * 16 * (q + 1))
        b_rows = 4096 + 256 * q + np.arange(256)
        c_rows = 4096 + GN + 256 * q + np.arange(256)
        rows = np.concatenate([hrows, b_rows, c_rows])
        pc[c]["conv_w"] = _c(cw[rows])
        pc[c]["conv_b"] = _c(cb[rows])
        pc[c]["dt_bias"] = _c(dtb[16 * q:16 * (q + 1)])
        pc[c]["A_log"] = _c(alog[16 * q:16 * (q + 1)])
        pc[c]["D_ssm"] = _c(Dp[16 * q:16 * (q + 1)])
        pc[c]["norm_w"] = _c(nw[64 * 16 * q:64 * 16 * (q + 1)])

    opw = np.asarray(inp["L1_attn_out_proj_w"])[PERM_H, :]
    opT = opw.T
    shared["opj_panels"] = _c(opT.reshape(4096, 32, 128).transpose(1, 0, 2))
    for c in range(NCORE):
        q = c % 4
        pc[c]["opTs"] = _c(opT[1024 * q:1024 * (q + 1), :])
    shared["routerT"] = _c(np.asarray(inp["L1_ffn_router_weight"])[:, PERM_H].T)
    shared["router_b"] = _c(np.asarray(inp["L1_ffn_router_bias"]))
    shared["expert_b"] = _c(np.asarray(inp["L1_ffn_expert_bias"])[:, PERM_H])

    return pc, shared



def emit_ln(em, tag, src_hbm, xln_hbm, nt=TC, resid_in=None, pmajor=False):
    nc = em.nc
    CH = 512
    scopes = []
    def ssb(n, shp, dt=F32):
        t, f = em.sb_scope(n, shp, dt); scopes.append(f); return t
    sx = [ssb(f"{tag}_x", [128, CH]) for _ in range(2)]
    sbsq = [ssb(f"{tag}_bsq", [128, 2 * CH], BF16) for _ in range(2)]
    ones_b = ssb(f"{tag}_ones", [128, 1], BF16)
    ones_f = ssb(f"{tag}_onesf", [1, 128], F32)
    stat = ssb(f"{tag}_stat", [1, 3 * CH])
    rmr = ssb(f"{tag}_rmr", [1, 2 * CH])
    stmp = [ssb(f"{tag}_tmp", [128, CH]) for _ in range(2)]
    pscopes = [em.ps_scope(f"{tag}_psum", [1, CH]) for _ in range(2)] + \
              [em.ps_scope(f"{tag}_pbc", [128, CH]) for _ in range(2)]
    p_sum = [pscopes[0][0], pscopes[1][0]]
    p_bc = [pscopes[2][0], pscopes[3][0]]
    sxo = [ssb(f"{tag}_xo", [128, CH], BF16) for _ in range(2)]
    ld, cv, cs, ct, so = (f"{tag}{n}" for n in ("ld", "cv", "cs", "ct", "so"))
    em.do("vector", lambda e: e.memset(ones_b[:], 1.0), inc=cv)
    em.do("vector", lambda e: e.memset(ones_f[:], 1.0), inc=cv)
    n_chunk = nt // CH
    for t in range(n_chunk):
        lnmm = [0, 0]
        castm = [0, 0]
        for hh in range(32):
            buf = sx[hh % 2]
            bsq = sbsq[hh % 2]
            em.do("sync", lambda e, t=t, hh=hh, buf=buf: e.dma_start(
                out=buf[:], in_=src_hbm[128 * hh:128 * (hh + 1), CH * t:CH * (t + 1)]),
                wait=[(cv, castm[hh % 2])],
                inc=f"{tag}ld{hh % 2}", incby=16)
            em.do("vector", lambda e, buf=buf, bsq=bsq, hh=hh: e.tensor_copy(
                bsq[:, 0:CH], buf[:]),
                wait=[(f"{tag}ld{hh % 2}", em.count(f"{tag}ld{hh % 2}")),
                      (ct, lnmm[hh % 2]), (cs, em.count(cs))], inc=cv)
            castm[hh % 2] = em.count(cv)
            em.do("scalar", lambda e, bsq=bsq: e.activation(
                bsq[:, CH:2 * CH], bsq[:, 0:CH], AF.Square),
                wait=[(cv, em.count(cv))], inc=cs)
            for half in range(2):
                em.do("tensor", lambda e, hh=hh, half=half, bsq=bsq: e.matmul(
                    p_sum[half][0:1, :], ones_b[:], bsq[:, CH * half:CH * (half + 1)],
                    start=(hh == 0), stop=(hh == 31), skip_group_check=True),
                    wait=([(cs, em.count(cs))] if half == 0 else ()), inc=ct)
            lnmm[hh % 2] = em.count(ct)
        for half in range(2):
            em.do("vector", lambda e, half=half: e.tensor_scalar(
                stat[0:1, CH * half:CH * (half + 1)], p_sum[half][0:1, :],
                1.0 / H, None, ALU.mult),
                wait=[(ct, em.count(ct))], inc=cv)
        em.do("vector", lambda e: e.tensor_tensor(
            stat[0:1, 2 * CH:3 * CH], stat[0:1, 0:CH], stat[0:1, 0:CH], ALU.mult),
            wait=[(cv, em.count(cv))], inc=cv)
        em.do("vector", lambda e: e.tensor_tensor(
            stat[0:1, CH:2 * CH], stat[0:1, CH:2 * CH], stat[0:1, 2 * CH:3 * CH],
            ALU.subtract),
            wait=[(cv, em.count(cv))], inc=cv)
        em.do("vector", lambda e: e.tensor_scalar(
            stat[0:1, CH:2 * CH], stat[0:1, CH:2 * CH], EPS_LN, None, ALU.add),
            wait=[(cv, em.count(cv))], inc=cv)
        em.do("scalar", lambda e: e.activation(
            stat[0:1, CH:2 * CH], stat[0:1, CH:2 * CH], AF.Ln),
            wait=[(cv, em.count(cv))], inc=cs)
        em.do("scalar", lambda e: e.activation(
            rmr[0:1, 0:CH], stat[0:1, CH:2 * CH], AF.Exp, scale=-0.5),
            wait=[(cs, em.count(cs))], inc=cs)
        em.do("vector", lambda e: e.tensor_tensor(
            rmr[0:1, CH:2 * CH], rmr[0:1, 0:CH], stat[0:1, 0:CH], ALU.mult),
            wait=[(cs, em.count(cs))], inc=cv)
        for half in range(2):
            em.do("tensor", lambda e, half=half: e.matmul(
                p_bc[half][:, :], ones_f[0:1, :], rmr[0:1, CH * half:CH * (half + 1)],
                start=True, stop=True, skip_group_check=True),
                wait=[(cv, em.count(cv))], inc=ct)
        p2m = [0, 0]
        for hh in range(32):
            par = hh % 2
            buf = sx[par]
            em.do("sync", lambda e, t=t, hh=hh, buf=buf: e.dma_start(
                out=buf[:], in_=src_hbm[128 * hh:128 * (hh + 1), CH * t:CH * (t + 1)]),
                wait=[(cv, p2m[par]), (ct, em.count(ct))],
                inc=f"{tag}ld{par}", incby=16)
            em.do("vector", lambda e, buf=buf, par=par: e.tensor_tensor(
                stmp[par][:], buf[:], p_bc[0][:, :], ALU.mult),
                wait=[(f"{tag}ld{par}", em.count(f"{tag}ld{par}")),
                      (ct, em.count(ct))], inc=cv)
            em.do("vector", lambda e, par=par: e.tensor_tensor(
                sxo[par][:], stmp[par][:], p_bc[1][:, :], ALU.subtract),
                wait=[(cv, em.count(cv)), (so, em.count(so))], inc=cv)
            p2m[par] = em.count(cv)
            if pmajor:
                oap = xln_hbm[:, nt * hh + CH * t:nt * hh + CH * (t + 1)]
            else:
                oap = xln_hbm[128 * hh:128 * (hh + 1), CH * t:CH * (t + 1)]
            em.do("sync", lambda e, oap=oap, par=par: e.dma_start(
                out=oap, in_=sxo[par][:]),
                wait=[(cv, em.count(cv))], inc=so, incby=16)
    em.wait("sync", so)
    em.phase_barrier()
    for pf in reversed(pscopes):
        pf[1]()
    for f in reversed(scopes):
        f()


def _bcast_check(ap, shape):
    return ap



def emit_dequant(em, gate_pk, up_pk, down_pk, gate_sT, up_sT, down_sT,
                 gate_zT, up_zT, down_zT, gate_w, up_w, down_w,
                 ag_outs=None):
    sel4 = em.dram_in("sel4", [4, 128], F32)
    scopes = []
    def ssb(n, shp, dt=F32):
        t, f = em.sb_scope(n, shp, dt); scopes.append(f); return t
    spk = ssb("dq_pk", [128, 1408], U8)
    svhi = ssb("dq_vhi", [128, 1408], U8)
    svlo = ssb("dq_vlo", [128, 1408], U8)
    srow = ssb("dq_srow", [4, 1408])
    szrow = ssb("dq_zrow", [4, 1408])
    sself = ssb("dq_self", [4, 128], F32)
    sstile = ssb("dq_stile", [128, 1408])
    sztile = ssb("dq_ztile", [128, 1408])
    stmp = ssb("dq_tmp", [128, 1408])
    sw = ssb("dq_w", [128, 1408], BF16)
    zpad = ssb("dq_zpad", [32, 512], BF16)
    p_s, ps_free = em.ps_scope("dq_ps", [128, 512])
    dl, dv, ds, dtn, dso = ("dqld", "dqv", "dqs", "dqt", "dqso")
    em.do("sync", lambda e: e.dma_start(out=sself[:, :], in_=sel4), inc=dl, incby=16)
    em.do("vector", lambda e: e.memset(zpad[:, :], 0.0), inc=dv)
    _zpv = em.count(dv)
    for cc in range(8):
        em.do("sync", lambda e, cc=cc: e.dma_start(
            out=down_w[1408 * cc + 1376:1408 * cc + 1408, :], in_=zpad[:, :]),
            wait=[(dv, _zpv)], inc=dso, incby=16)

    def dq_matrix(pk_hbm, sT_hbm, zT_hbm, out_hbm, njt, ncol, ngrp,
                  tile_major=False):
        for jj in range(njt):
            g0 = 4 * jj
            em.do("sync", lambda e, jj=jj, pk_hbm=pk_hbm, ncol=ncol: e.dma_start(
                out=spk[:, 0:ncol], in_=pk_hbm[128 * jj:128 * (jj + 1), :]),
                wait=[(dv, em.count(dv))], inc=dl, incby=16)
            em.do("sync", lambda e, g0=g0, sT_hbm=sT_hbm, ncol=ncol: e.dma_start(
                out=srow[:, 0:ncol], in_=sT_hbm[g0:g0 + 4, :]),
                wait=[(dv, em.count(dv)), (dtn, em.count(dtn))], inc=dl, incby=16)
            em.do("sync", lambda e, g0=g0, zT_hbm=zT_hbm, ncol=ncol: e.dma_start(
                out=szrow[:, 0:ncol], in_=zT_hbm[g0:g0 + 4, :]),
                wait=[(dv, em.count(dv)), (dtn, em.count(dtn))], inc=dl, incby=16)
            for (row, tile) in ((srow, sstile), (szrow, sztile)):
                for n0 in range(0, ncol, 512):
                    n1 = min(n0 + 512, ncol)
                    em.do("tensor", lambda e, n0=n0, n1=n1, row=row: e.matmul(
                        p_s[:, 0:n1 - n0], sself[:, :], row[:, n0:n1],
                        start=True, stop=True, skip_group_check=True),
                        wait=[(dl, em.count(dl)), (dv, em.count(dv))], inc=dtn)
                    em.do("vector", lambda e, n0=n0, n1=n1, tile=tile: e.tensor_copy(
                        tile[:, n0:n1], p_s[:, 0:n1 - n0]),
                        wait=[(dtn, em.count(dtn))], inc=dv)
            em.do("vector", lambda e, ncol=ncol: e.tensor_single_scalar(
                svhi[:, 0:ncol], spk[:, 0:ncol], 4, ALU.logical_shift_right),
                wait=[(dl, em.count(dl))], inc=dv)
            em.do("vector", lambda e, ncol=ncol: e.tensor_single_scalar(
                svlo[:, 0:ncol], spk[:, 0:ncol], 15, ALU.bitwise_and),
                wait=[(dl, em.count(dl))], inc=dv)
            for (vv, rowbase) in ((svhi, 0), (svlo, njt)):
                em.do("vector", lambda e, vv=vv, ncol=ncol: e.tensor_tensor(
                    stmp[:, 0:ncol], vv[:, 0:ncol], sztile[:, 0:ncol], ALU.subtract),
                    wait=[(dv, em.count(dv)), (dso, em.count(dso))], inc=dv)
                em.do("vector", lambda e, ncol=ncol: e.tensor_tensor(
                    sw[:, 0:ncol], stmp[:, 0:ncol], sstile[:, 0:ncol], ALU.mult),
                    wait=[(dv, em.count(dv)), (dso, em.count(dso))], inc=dv)
                if tile_major:
                    bb = rowbase + jj
                    for s in range(11):
                        em.do("sync", lambda e, bb=bb, s=s, out_hbm=out_hbm:
                              e.dma_start(
                            out=out_hbm[128 * s:128 * (s + 1),
                                        128 * bb:128 * (bb + 1)],
                            in_=sw[:, 128 * s:128 * (s + 1)]),
                            wait=[(dv, em.count(dv))], inc=dso, incby=16)
                else:
                    q0 = 128 * (rowbase + jj)
                    c0, c1 = q0 // 1376, (q0 + 127) // 1376
                    runs = ([(q0, 128)] if c0 == c1 else
                            [(q0, 1376 * c1 - q0), (1376 * c1, q0 + 128 - 1376 * c1)])
                    for (qa, ln) in runs:
                        qq = qa + 32 * (qa // 1376)
                        em.do("sync", lambda e, qa=qa, ln=ln, qq=qq, q0=q0,
                              out_hbm=out_hbm, ncol=ncol: e.dma_start(
                            out=out_hbm[qq:qq + ln, :],
                            in_=sw[qa - q0:qa - q0 + ln, 0:ncol]),
                            wait=[(dv, em.count(dv))], inc=dso, incby=16)

    def _ag(src, dst):
        if ag_outs is None:
            return
        em.do("gpsimd", lambda e, src=src, dst=dst: e.collective_compute(
            "AllGather", ALU.bypass, replica_groups=[list(range(8))],
            ins=[src[:, :]], outs=[dst[:, :]]),
            wait=[(dso, em.count(dso))], inc="ccA", incby=None)

    dq_matrix(gate_pk, gate_sT, gate_zT, gate_w, 16, 1408, 64, tile_major=True)
    if ag_outs is not None:
        _ag(gate_w, ag_outs[0])
    dq_matrix(up_pk, up_sT, up_zT, up_w, 16, 1408, 64, tile_major=True)
    if ag_outs is not None:
        _ag(up_w, ag_outs[1])
    dq_matrix(down_pk, down_sT, down_zT, down_w, 43, 512, 172)
    if ag_outs is not None:
        _ag(down_w, ag_outs[2])
    em.wait("sync", dso)
    em.phase_barrier()
    ps_free()
    for f in reversed(scopes):
        f()


PI = float(np.pi)


def emit_qkv_rope(em, xg1p, wqT, bq, invf, qkvT, vout, ident):
    scopes = []
    def ssb(name, shape, dtype=F32):
        t, f = em.sb_scope(name, shape, dtype)
        scopes.append(f)
        return t
    swq = ssb("qk_w", [128, 32 * 1536], BF16)
    sxg = ssb("qk_xg", [128, 32 * 1024], BF16)
    sbq = ssb("qk_bq", [1, 1536], BF16)
    ones_b = ssb("qk_ones", [1, 128], BF16)
    stv = ssb("qk_tv", [1, 128])
    sinv = ssb("qk_inv", [1, 64])
    cos_all = ssb("qk_cos", [128, 2048])
    sin_all = ssb("qk_sin", [128, 2048])
    sfr = ssb("qk_fr", [128, 64])
    sk1 = ssb("qk_k1", [128, 64], I32)
    sk2 = ssb("qk_k2", [128, 64])
    scmp = ssb("qk_cmp", [128, 64])
    rope = ssb("qk_rope", [128, 5 * 128], BF16)
    t1 = ssb("qk_t1", [128, 64])
    t2 = ssb("qk_t2", [128, 64])
    vstage = ssb("qk_vst", [128, 128], BF16)
    pscopes = []
    def pps(name, shape, dtype=F32):
        t, f = em.ps_scope(name, shape, dtype)
        pscopes.append(f)
        return t
    p_fr = pps("qk_pfr", [128, 64])
    p_q = pps("qk_pq", [128, 512])
    p_kv = pps("qk_pkv", [128, 256])
    p_tr = pps("qk_ptr", [128, 128], BF16)
    sident = ssb("qk_id", [128, 128], BF16)
    gl, ld, cv, cs, ct, so = ("qkg", "qkld", "qkv", "qkcs", "qkct", "qkso")
    mm_mark = [0, 0]
    em.do("sync", lambda e: e.dma_start(out=sident[:, :], in_=ident), inc=ld, incby=16)

    em.do("gpsimd", lambda e: e.dma_start(
        out=swq.ap().rearrange("p (kk c) -> p kk c", c=1536),
        in_=wqT.rearrange("(kk p) c -> p kk c", p=128)), inc=gl, incby=16)
    em.do("gpsimd", lambda e: e.dma_start(out=sbq[:, :], in_=bq), inc=gl, incby=16)
    em.do("sync", lambda e: e.dma_start(out=sinv[:, :], in_=invf), inc=ld, incby=16)
    em.do("vector", lambda e: e.memset(ones_b[:], 1.0), inc=cv)
    for tt in range(32):
        em.do("gpsimd", lambda e, tt=tt: e.iota(
            stv[:, :], [[1, 128]], base=128 * tt, channel_multiplier=0,
            allow_small_or_imprecise_dtypes=True),
            wait=[(ct, em.count(ct))], inc="qkgi")
        em.do("tensor", lambda e: e.matmul(
            p_fr[:, :], stv[0:1, :], sinv[0:1, :],
            start=True, stop=True, skip_group_check=True),
            wait=[(gl, em.count(gl)), ("qkgi", em.count("qkgi")), (ld, em.count(ld)),
                  (cv, em.count(cv))], inc=ct)
        em.do("vector", lambda e: e.tensor_scalar(
            sk2[:, :], p_fr[:, :], 1.0 / (2 * PI), None, ALU.mult),
            wait=[(ct, em.count(ct))], inc=cv)
        em.do("vector", lambda e: e.tensor_copy(sk1[:, :], sk2[:, :]),
            wait=[(cv, em.count(cv))], inc=cv)
        em.do("vector", lambda e: e.tensor_copy(sk2[:, :], sk1[:, :]),
            wait=[(cv, em.count(cv))], inc=cv)
        em.do("vector", lambda e: e.scalar_tensor_tensor(
            sfr[:, :], sk2[:, :], -2 * PI, p_fr[:, :], ALU.mult, ALU.add),
            wait=[(cv, em.count(cv))], inc=cv)
        for _ in range(2):
            em.do("vector", lambda e: e.tensor_scalar(
                scmp[:, :], sfr[:, :], PI, None, ALU.is_gt),
                wait=[(cv, em.count(cv))], inc=cv)
            em.do("vector", lambda e: e.scalar_tensor_tensor(
                sfr[:, :], scmp[:, :], -2 * PI, sfr[:, :], ALU.mult, ALU.add),
                wait=[(cv, em.count(cv))], inc=cv)
            em.do("vector", lambda e: e.tensor_scalar(
                scmp[:, :], sfr[:, :], -PI, None, ALU.is_lt),
                wait=[(cv, em.count(cv))], inc=cv)
            em.do("vector", lambda e: e.scalar_tensor_tensor(
                sfr[:, :], scmp[:, :], 2 * PI, sfr[:, :], ALU.mult, ALU.add),
                wait=[(cv, em.count(cv))], inc=cv)
        em.do("scalar", lambda e, tt=tt: e.activation(
            sin_all[:, 64 * tt:64 * (tt + 1)], sfr[:, :], AF.Sin),
            wait=[(cv, em.count(cv))], inc=cs)
        em.do("scalar", lambda e: e.activation(t1[:, :], sfr[:, :], AF.Abs),
            wait=[(cv, em.count(cv))], inc=cs)
        em.do("vector", lambda e: e.tensor_scalar(
            t2[:, :], t1[:, :], -1.0, PI / 2, ALU.mult, ALU.add),
            wait=[(cs, em.count(cs))], inc=cv)
        em.do("scalar", lambda e, tt=tt: e.activation(
            cos_all[:, 64 * tt:64 * (tt + 1)], t2[:, :], AF.Sin),
            wait=[(cv, em.count(cv))], inc=cs)

    p_q2 = [p_q, pps("qk_pq2", [128, 512])]
    p_kv2 = [p_kv, pps("qk_pkv2", [128, 256])]
    rope2 = [rope, ssb("qk_rope2", [128, 5 * 128], BF16)]
    tiles = [(r, p, t2) for r in range(4) for p in range(2) for t2 in range(8)]
    rp_mark = [0, 0]
    ts_mark = [0, 0]
    st_mark = [0, 0]

    def emit_mm(idx):
        r, p, t2 = tiles[idx]
        par = idx % 2
        if p == 0 and t2 == 0:
            em.do("sync", lambda e, r=r: e.dma_start(
                out=sxg.ap(), in_=xg1p[128 * r:128 * (r + 1), :]),
                wait=[(ct, em.count(ct))], inc="qksl", incby=16)
        slv = em.count("qksl")
        for kk in range(32):
            lap = sxg[:, 1024 * kk + 128 * t2:1024 * kk + 128 * (t2 + 1)]
            em.do("tensor", lambda e, lap=lap, kk=kk, p=p, par=par: e.matmul(
                p_q2[par][:, :], lap,
                swq[:, 1536 * kk + 768 * p:1536 * kk + 768 * p + 512],
                start=(kk == 0), stop=False, skip_group_check=True),
                wait=([("qksl", slv), (gl, em.count(gl)), (cv, rp_mark[par]),
                       (cs, ts_mark[par])] if kk == 0 else ()), inc=ct)
            em.do("tensor", lambda e, lap=lap, kk=kk, p=p, par=par: e.matmul(
                p_kv2[par][:, :], lap,
                swq[:, 1536 * kk + 768 * p + 512:1536 * kk + 768 * p + 768],
                start=(kk == 0), stop=False, skip_group_check=True), inc=ct)
        em.do("tensor", lambda e, p=p, par=par: e.matmul(
            p_q2[par][:, :], ones_b[:, :], sbq[0:1, 768 * p:768 * p + 512],
            start=False, stop=True, skip_group_check=True), inc=ct)
        em.do("tensor", lambda e, p=p, par=par: e.matmul(
            p_kv2[par][:, :], ones_b[:, :], sbq[0:1, 768 * p + 512:768 * p + 768],
            start=False, stop=True, skip_group_check=True), inc=ct)
        mm_mark[par] = em.count(ct)

    def emit_rope(idx):
        r, p, u2 = tiles[idx]
        tt = 8 * r + u2
        par = idx % 2
        mmv = mm_mark[par]
        for s in range(5):
            ps_ = p_q2[par] if s < 4 else p_kv2[par]
            off = 128 * s if s < 4 else 0
            cosap = cos_all[:, 64 * tt:64 * (tt + 1)]
            sinap = sin_all[:, 64 * tt:64 * (tt + 1)]
            rp = rope2[par]
            em.do("vector", lambda e, ps_=ps_, off=off, cosap=cosap: e.tensor_tensor(
                t1[:, :], ps_[:, off:off + 64], cosap, ALU.mult),
                wait=[(ct, mmv), (cs, em.count(cs)), (cv, em.count(cv)),
                      (so, st_mark[idx % 2])], inc=cv)
            em.do("vector", lambda e, ps_=ps_, off=off, sinap=sinap: e.tensor_tensor(
                t2[:, :], ps_[:, off + 64:off + 128], sinap, ALU.mult),
                wait=[(cv, em.count(cv))], inc=cv)
            em.do("vector", lambda e, s=s, rp=rp: e.tensor_tensor(
                rp[:, 128 * s:128 * s + 64], t1[:, :], t2[:, :], ALU.subtract),
                wait=[(cv, em.count(cv))], inc=cv)
            em.do("vector", lambda e, ps_=ps_, off=off, cosap=cosap: e.tensor_tensor(
                t1[:, :], ps_[:, off + 64:off + 128], cosap, ALU.mult),
                wait=[(cv, em.count(cv))], inc=cv)
            em.do("vector", lambda e, ps_=ps_, off=off, sinap=sinap: e.tensor_tensor(
                t2[:, :], ps_[:, off:off + 64], sinap, ALU.mult),
                wait=[(cv, em.count(cv))], inc=cv)
            em.do("vector", lambda e, s=s, rp=rp: e.tensor_tensor(
                rp[:, 128 * s + 64:128 * (s + 1)], t1[:, :], t2[:, :], ALU.add),
                wait=[(cv, em.count(cv))], inc=cv)
        rp_mark[par] = em.count(cv)
        em.do("scalar", lambda e, par=par: e.activation(
            vstage[:, :], p_kv2[par][:, 128:256], AF.Copy),
            wait=[(ct, mmv), (so, em.count(so))], inc=cs)
        em.do("sync", lambda e, p=p, tt=tt: e.dma_start(
            out=vout[p, 128 * tt:128 * (tt + 1), :], in_=vstage[:, :]),
            wait=[(cs, em.count(cs))], inc=so, incby=16)

    def emit_tr(idx):
        r, p, u2 = tiles[idx]
        tt = 8 * r + u2
        par = idx % 2
        rp = rope2[par]
        for s in range(5):
            em.do("tensor", lambda e, s=s, rp=rp: e.transpose(
                p_tr[:, :], rp[:, 128 * s:128 * (s + 1)], sident[:, :]),
                wait=[(cv, rp_mark[par]), (cs, em.count(cs))], inc=ct)
            em.do("scalar", lambda e, s=s, rp=rp: e.activation(
                rp[:, 128 * s:128 * (s + 1)], p_tr[:, :], AF.Copy),
                wait=[(ct, em.count(ct))], inc=cs)
        ts_mark[par] = em.count(cs)
        em.do("sync", lambda e, p=p, tt=tt, rp=rp: e.dma_start(
            out=qkvT[p].rearrange("s d t -> d s t")[:, :, 128 * tt:128 * (tt + 1)],
            in_=rp.ap().rearrange("d (s t) -> d s t", t=128)),
            wait=[(cs, em.count(cs))], inc=so, incby=16)
        st_mark[par] = em.count(so)

    for idx in range(len(tiles) + 1):
        if idx < len(tiles):
            emit_mm(idx)
        if idx >= 1:
            emit_rope(idx - 1)
            emit_tr(idx - 1)
    em.wait("sync", so)
    em.phase_barrier()
    for f in reversed(scopes + pscopes):
        f()


def attn_regions(a):
    js = list(range(7, 2 * a - 15, 8))
    va = len(js)
    ks = 128 * max(0, a - 8)
    ke = 128 * (a + 1)
    L = ke - ks
    dead = 64 if a >= 8 else 0
    Wp = L - dead - 128
    p1 = min(512, Wp)
    p2 = Wp - p1
    need_half = (a >= 8) and ((2 * a - 14) % 8 != 0)
    return js, va, ks, ke, L, dead, Wp, p1, p2, need_half


def emit_attention(em, qkvT, vin, ctxT, tri_mask, half_mask, ident):
    scopes = []
    def asb(n, shp, dt=F32):
        t, f = em.sb_scope(n, shp, dt); scopes.append(f); return t
    skt = asb("at_kt", [128, 4096], BF16)
    sq = asb("at_qt", [128, 4096], BF16)
    sv = asb("at_v", [128, 32 * 128], BF16)
    ssq = asb("at_sq", [128, 4096], BF16)
    snrm = asb("at_nrm", [1, 4096])
    sm1 = asb("at_m1", [1, 8])
    ones_b = asb("at_ob", [128, 1], BF16)
    ones_f = asb("at_of", [1, 128])
    stri = asb("at_tri", [128, 128])
    shalf = asb("at_half", [128, 64])
    sident = asb("at_id", [128, 128], BF16)
    nb = asb("at_nb", [128, 1])
    P = [asb("at_P", [128, 2048], BF16) for _ in range(2)]
    acc = [asb("at_acc", [128, 4]) for _ in range(2)]
    ssum = [asb("at_ssum", [128, 1]) for _ in range(2)]
    rs = [asb("at_rs", [128, 1]) for _ in range(2)]
    pt_sb = [asb("at_pt", [128, 128], BF16) for _ in range(2)]
    svg = asb("at_vg", [128, 3 * 128], BF16)
    ctx_sb = asb("at_ctx", [128, 4096], BF16)
    psc, psc_free = zip(*[em.ps_scope("at_psc", [128, 1536]) for _ in range(2)])
    p_tr2, ptr_free = em.ps_scope("at_ptr", [128, 256], BF16)
    p_tr = [p_tr2[:, 0:128], p_tr2[:, 128:256]]
    pctx, pctx_free = em.ps_scope("at_pctx", [128, 128])
    ld, cv, cs, ct, so = ("atld", "atv", "atcs", "atct", "atso")
    em.do("sync", lambda e: e.dma_start(out=stri[:, :], in_=tri_mask), inc=ld, incby=16)
    em.do("sync", lambda e: e.dma_start(out=shalf[:, :], in_=half_mask), inc=ld, incby=16)
    em.do("sync", lambda e: e.dma_start(out=sident[:, :], in_=ident), inc=ld, incby=16)
    em.do("vector", lambda e: e.memset(ones_b[:], 1.0), inc=cv)
    em.do("vector", lambda e: e.memset(ones_f[:], 1.0), inc=cv)

    def norms_max(src_bf, out_idx):
        em.do("scalar", lambda e: e.activation(ssq[:, :], src_bf[:, :], AF.Square),
              wait=[(ld, em.count(ld)), (cv, em.count(cv)), (ct, em.count(ct))], inc=cs)
        for ch in range(8):
            em.do("tensor", lambda e, ch=ch: e.matmul(
                psc[0][0:1, 0:512], ones_b[:, :], ssq[:, 512 * ch:512 * (ch + 1)],
                start=True, stop=True, skip_group_check=True),
                wait=[(cs, em.count(cs)), (cv, em.count(cv))], inc=ct)
            em.do("vector", lambda e, ch=ch: e.tensor_copy(
                snrm[0:1, 512 * ch:512 * (ch + 1)], psc[0][0:1, 0:512]),
                wait=[(ct, em.count(ct))], inc=cv)
        em.do("vector", lambda e, out_idx=out_idx: e.reduce_max(
            sm1[0:1, out_idx:out_idx + 1], snrm[0:1, :], axis=AX.X),
            wait=[(cv, em.count(cv))], inc=cv)

    mask_mark = [0, 0]; exp_mark = [0, 0]; scl_mark = [0, 0]
    tr_mark = [0, 0]; sc_mark = [0, 0]; red_mark = [0, 0]
    ctx_evac_mark = [0]

    def emit_scores(a, par):
        js, va, ks, ke, L, dead, Wp, p1, p2, need_half = attn_regions(a)
        qsl = sq[:, 128 * a:128 * (a + 1)]
        mwait = [(cv, mask_mark[par]), (cs, exp_mark[par]), (ld, em.count(ld))]
        if va:
            vap = skt[:, 448:448 + 512 * va].rearrange(
                "p (m x) -> p m x", x=512)[:, :, 0:64]
            em.do("tensor", lambda e, qsl=qsl, vap=vap, va=va, par=par: e.matmul(
                psc[par][:, 0:64 * va], qsl, vap, start=True, stop=True,
                skip_group_check=True), wait=mwait, inc=ct)
            mwait = ()
        em.do("tensor", lambda e, qsl=qsl, ke=ke, par=par: e.matmul(
            psc[par][:, 384:512], qsl, skt[:, ke - 128:ke],
            start=True, stop=True, skip_group_check=True), wait=mwait, inc=ct)
        mwait = ()
        if p1:
            em.do("tensor", lambda e, qsl=qsl, ks=ks, dead=dead, p1=p1, par=par: e.matmul(
                psc[par][:, 512:512 + p1], qsl, skt[:, ks + dead:ks + dead + p1],
                start=True, stop=True, skip_group_check=True), wait=mwait, inc=ct)
        if p2:
            em.do("tensor", lambda e, qsl=qsl, ks=ks, dead=dead, p1=p1, p2=p2,
                  par=par: e.matmul(
                psc[par][:, 1024:1024 + p2], qsl,
                skt[:, ks + dead + p1:ks + dead + p1 + p2],
                start=True, stop=True, skip_group_check=True), wait=mwait, inc=ct)
        sc_mark[par] = em.count(ct)

    def emit_softmax(a, par):
        js, va, ks, ke, L, dead, Wp, p1, p2, need_half = attn_regions(a)
        em.do("vector", lambda e, par=par: e.tensor_tensor(
            psc[par][:, 384:512], psc[par][:, 384:512], stri[:, :], ALU.add),
            wait=[(ct, sc_mark[par])], inc=cv)
        if need_half:
            em.do("vector", lambda e, par=par: e.tensor_tensor(
                psc[par][:, 512:576], psc[par][:, 512:576], shalf[:, :], ALU.add),
                wait=[(cv, em.count(cv))], inc=cv)
        mask_mark[par] = em.count(cv)
        if dead:
            em.do("vector", lambda e, dead=dead, par=par: e.memset(
                P[par][:, 512:512 + dead], 0.0),
                wait=[(ct, tr_mark[par]), (cv, em.count(cv))], inc=cv)
        regions = []
        if va:
            regions.append((0, 0, 64 * va))
        if p1:
            regions.append((512, 512 + dead, p1))
        if p2:
            regions.append((1024, 512 + dead + p1, p2))
        regions.append((384, 512 + L - 128, 128))
        nacc = 0
        for (src_off, dst_off, w) in regions:
            em.do("scalar", lambda e, src_off=src_off, dst_off=dst_off, w=w,
                  nacc=nacc, par=par: e.activation(
                P[par][:, dst_off:dst_off + w], psc[par][:, src_off:src_off + w],
                AF.Exp, bias=nb[:, 0:1], scale=1.0 / 128.0,
                accum_out=acc[par][:, nacc:nacc + 1]),
                wait=[(cv, mask_mark[par]), (ct, tr_mark[par]),
                      (cv, red_mark[par])], inc=cs)
            nacc += 1
        exp_mark[par] = em.count(cs)
        em.do("vector", lambda e, nacc=nacc, par=par: e.reduce_sum(
            ssum[par][:, 0:1], acc[par][:, 0:nacc], axis=AX.X),
            wait=[(cs, exp_mark[par]), (cv, scl_mark[par])], inc=cv)
        red_mark[par] = em.count(cv)
        em.do("vector", lambda e, par=par: e.reciprocal(rs[par][:, 0:1],
              ssum[par][:, 0:1]), wait=[(cv, em.count(cv))], inc=cv)
        if va:
            em.do("vector", lambda e, va=va, par=par: e.tensor_scalar(
                P[par][:, 0:64 * va], P[par][:, 0:64 * va], rs[par][:, 0:1],
                None, ALU.mult),
                wait=[(cv, em.count(cv))], inc=cv)
        em.do("vector", lambda e, L=L, par=par: e.tensor_scalar(
            P[par][:, 512:512 + L], P[par][:, 512:512 + L], rs[par][:, 0:1],
            None, ALU.mult),
            wait=[(cv, em.count(cv))], inc=cv)
        scl_mark[par] = em.count(cv)

    def emit_pv(a, par):
        js, va, ks, ke, L, dead, Wp, p1, p2, need_half = attn_regions(a)
        chunks = []
        coff = 0
        while coff < 64 * va:
            w = min(128, 64 * va - coff)
            chunks.append(("v", coff, w))
            coff += w
        for c in range(L // 128):
            chunks.append(("l", 512 + 128 * c, 128))
        total_mm = len(chunks)
        cp_mark = [("atcs", em.count(cs)), ("atv", em.count(cv))]
        mmi = 0
        for ci, (kind, poff, w) in enumerate(chunks):
            cpar = ci % 2
            em.do("tensor", lambda e, poff=poff, w=w, par=par, cpar=cpar: e.transpose(
                p_tr[cpar][0:w, :], P[par][:, poff:poff + w], sident[:, :]),
                wait=[(cv, scl_mark[par]), cp_mark[cpar]], inc=ct)
            if cpar == 0:
                em.do("scalar", lambda e, w=w, cpar=cpar: e.activation(
                    pt_sb[cpar][0:w, :], p_tr[cpar][0:w, :], AF.Copy),
                    wait=[(ct, em.count(ct))], inc=cs)
                cp_mark[cpar] = ("atcs", em.count(cs))
            else:
                em.do("vector", lambda e, w=w, cpar=cpar: e.tensor_copy(
                    pt_sb[cpar][0:w, :], p_tr[cpar][0:w, :]),
                    wait=[(ct, em.count(ct))], inc=cv)
                cp_mark[cpar] = ("atv", em.count(cv))
            if kind == "l":
                tt = (poff - 512) // 128 + ks // 128
                em.do("tensor", lambda e, tt=tt, mmi=mmi, total_mm=total_mm,
                      cpar=cpar: e.matmul(
                    pctx[:, :], sv[:, 128 * tt:128 * (tt + 1)], pt_sb[cpar][:, :],
                    start=(mmi == 0), stop=(mmi == total_mm - 1),
                    skip_group_check=True),
                    wait=[cp_mark[cpar]] + ([(cs, ctx_evac_mark[0])]
                                            if mmi == 0 else []), inc=ct)
            else:
                vt = poff // 128
                em.do("tensor", lambda e, vt=vt, w=w, mmi=mmi, total_mm=total_mm,
                      cpar=cpar: e.matmul(
                    pctx[:, :], svg[0:w, 128 * vt:128 * (vt + 1)],
                    pt_sb[cpar][0:w, :],
                    start=(mmi == 0), stop=(mmi == total_mm - 1),
                    skip_group_check=True),
                    wait=[cp_mark[cpar], ("atgl", em.count("atgl"))] +
                         ([(cs, ctx_evac_mark[0])] if mmi == 0 else []), inc=ct)
            mmi += 1
        tr_mark[par] = em.count(ct)
        em.do("scalar", lambda e, a=a: e.activation(
            ctx_sb[:, 128 * a:128 * (a + 1)], pctx[:, :], AF.Copy),
            wait=[(ct, tr_mark[par]), (so, em.count(so))], inc=cs)
        ctx_evac_mark[0] = em.count(cs)

    for p in range(2):
        em.do("sync", lambda e, p=p: e.dma_start(out=skt[:, :], in_=qkvT[p, 4]),
              wait=[(ct, em.count(ct)), (cs, em.count(cs))], inc=ld, incby=16)
        em.do("sync", lambda e, p=p: e.dma_start(
            out=sv.ap().rearrange("q (tt d) -> q tt d", d=128),
            in_=vin[p].rearrange("(tt q) d -> q tt d", q=128)),
            wait=[(ct, em.count(ct)), (cs, em.count(cs))], inc=ld, incby=16)
        for i, j in enumerate((7, 15, 23, 31, 39)):
            em.do("gpsimd", lambda e, i=i, j=j: e.dma_start(
                out=svg[64 * (i % 2):64 * (i % 2) + 64,
                        128 * (i // 2):128 * (i // 2) + 128],
                in_=sv[64:128, 128 * (j // 2):128 * (j // 2) + 128]),
                wait=[(ld, em.count(ld)), (ct, em.count(ct))], inc="atgl", incby=16)
        norms_max(skt, 0)
        for s in range(4):
            em.do("sync", lambda e, p=p, s=s: e.dma_start(out=sq[:, :], in_=qkvT[p, s]),
                  wait=[(ct, em.count(ct)), (cs, em.count(cs))], inc=ld, incby=16)
            norms_max(sq, 1)
            em.do("vector", lambda e: e.tensor_tensor(
                sm1[0:1, 2:3], sm1[0:1, 0:1], sm1[0:1, 1:2], ALU.mult),
                wait=[(cv, em.count(cv))], inc=cv)
            em.do("scalar", lambda e: e.activation(sm1[0:1, 3:4], sm1[0:1, 2:3],
                  AF.Sqrt), wait=[(cv, em.count(cv))], inc=cs)
            em.do("vector", lambda e: e.tensor_scalar(
                sm1[0:1, 4:5], sm1[0:1, 3:4], -1.02 / 128.0, None, ALU.mult),
                wait=[(cs, em.count(cs))], inc=cv)
            em.do("tensor", lambda e: e.matmul(
                psc[0][:, 0:1], ones_f[0:1, :], sm1[0:1, 4:5], start=True, stop=True,
                skip_group_check=True), wait=[(cv, em.count(cv))], inc=ct)
            em.do("vector", lambda e: e.tensor_copy(nb[:, :], psc[0][:, 0:1]),
                  wait=[(ct, em.count(ct)), (cs, em.count(cs))], inc=cv)
            mask_mark[0] = mask_mark[1] = em.count(cv)
            for a in range(33):
                if a < 32:
                    emit_scores(a, a % 2)
                    emit_softmax(a, a % 2)
                if a >= 1:
                    emit_pv(a - 1, (a - 1) % 2)
            em.do("sync", lambda e, p=p, s=s: e.dma_start(
                out=ctxT[p, 128 * s:128 * (s + 1), :], in_=ctx_sb[:, :]),
                wait=[(cs, em.count(cs))], inc=so, incby=16)
    em.wait("sync", so)
    em.phase_barrier()
    pctx_free()
    ptr_free()
    for f in reversed(list(psc_free)):
        f()
    for f in reversed(scopes):
        f()



def emit_partial_gemm(em, tag, wTs, src_rows, rs_in):
    scopes = []
    def ssb(n, shp, dt=BF16):
        t, f = em.sb_scope(n, shp, dt); scopes.append(f); return t
    wsb = ssb(f"{tag}_w", [128, 8 * 4096])
    rsb = [ssb(f"{tag}_r", [128, 8 * 512]) for _ in range(2)]
    ev = [ssb(f"{tag}_ev", [128, 512]) for _ in range(2)]
    pp, pp_free = zip(*[em.ps_scope(f"{tag}_pp", [128, 512]) for _ in range(8)])
    ld, gl, cv, cs, ct, so = (f"{tag}{n}" for n in ("ld", "gl", "cv", "cs", "ct", "so"))
    em.do("gpsimd", lambda e: e.dma_start(
        out=wsb.ap().rearrange("p (kc c) -> p kc c", c=4096),
        in_=wTs.rearrange("(kc p) c -> p kc c", p=128)), inc=gl, incby=16)
    rm = [0, 0]
    evm = [0, 0]
    tcs = [(tq, tch) for tq in range(4) for tch in range(2)]
    for ti, (tq, tch) in enumerate(tcs):
        tpar = ti % 2
        for kc in range(8):
            em.do("sync", lambda e, kc=kc, tq=tq, tch=tch, tpar=tpar: e.dma_start(
                out=rsb[tpar][:, 512 * kc:512 * (kc + 1)], in_=src_rows(kc, tq, tch)),
                wait=[(ct, rm[tpar])], inc=f"{tag}ld{tpar}", incby=16)
        ldv = em.count(f"{tag}ld{tpar}")
        for hg in range(8):
            hpar = hg % 2
            for kc in range(8):
                for i in range(4):
                    hh = 4 * hg + i
                    em.do("tensor", lambda e, kc=kc, hh=hh, i=i, hpar=hpar,
                          tpar=tpar: e.matmul(
                        pp[4 * hpar + i][:, :],
                        wsb[:, 4096 * kc + 128 * hh:4096 * kc + 128 * (hh + 1)],
                        rsb[tpar][:, 512 * kc:512 * (kc + 1)],
                        start=(kc == 0), stop=(kc == 7), skip_group_check=True),
                        wait=([(f"{tag}ld{tpar}", ldv), (gl, 16),
                               (cv, evm[hpar])] if (kc == 0 and i == 0) else ()),
                        inc=ct)
            mmv = em.count(ct)
            for i in range(4):
                hh = 4 * hg + i
                epar = i % 2
                em.do("vector", lambda e, i=i, hpar=hpar, epar=epar: e.tensor_copy(
                    ev[epar][:, :], pp[4 * hpar + i][:, :]),
                    wait=[(ct, mmv), (so, em.count(so))], inc=cv)
                em.do("sync", lambda e, tq=tq, tch=tch, hh=hh, epar=epar: e.dma_start(
                    out=rs_in[tq, 128 * hh:128 * (hh + 1),
                              512 * tch:512 * (tch + 1)], in_=ev[epar][:, :]),
                    wait=[(cv, em.count(cv))], inc=so, incby=16)
            evm[hpar] = em.count(cv)
        rm[tpar] = em.count(ct)
    em.wait("sync", so)
    em.phase_barrier()
    for f in reversed(list(pp_free)):
        f()
    for f in reversed(scopes):
        f()


def emit_add_resid_ln(em, tag, rs_out, resid_in, bias, resid_out, xbf_all):
    scopes = []
    def ssb(n, shp, dt=F32):
        t, f = em.sb_scope(n, shp, dt); scopes.append(f); return t
    ra = [ssb(f"{tag}_ra", [128, 512], BF16) for _ in range(2)]
    rb = [ssb(f"{tag}_rb", [128, 512]) for _ in range(2)]
    ev = [ssb(f"{tag}_ev", [128, 512]) for _ in range(2)]
    sbias = ssb(f"{tag}_bias", [128, 32]) if bias is not None else None
    ld, cv, so = (f"{tag}{n}" for n in ("ld", "cv", "so"))
    if bias is not None:
        em.do("sync", lambda e: e.dma_start(out=sbias[:, :], in_=bias),
              inc=f"{tag}lb", incby=16)
    evm = [0, 0]
    for hh in range(32):
        for th in range(2):
            par = (2 * hh + th) % 2
            em.do("sync", lambda e, hh=hh, th=th, par=par: e.dma_start(
                out=ra[par][:, :], in_=rs_out[128 * hh:128 * (hh + 1),
                                              512 * th:512 * (th + 1)]),
                wait=[(cv, evm[par])], inc=f"{tag}ld{par}", incby=16)
            em.do("sync", lambda e, hh=hh, th=th, par=par: e.dma_start(
                out=rb[par][:, :], in_=resid_in[128 * hh:128 * (hh + 1),
                                                512 * th:512 * (th + 1)]),
                wait=[(cv, evm[par])], inc=f"{tag}ld{par}", incby=16)
            ldv = em.count(f"{tag}ld{par}")
            if bias is not None:
                em.do("vector", lambda e, hh=hh, par=par: e.tensor_scalar(
                    ev[par][:, :], ra[par][:, :], sbias[:, hh:hh + 1], None, ALU.add),
                    wait=[(f"{tag}ld{par}", ldv), (f"{tag}lb", 16),
                          (so, em.count(so))],
                    inc=cv)
                em.do("vector", lambda e, par=par: e.tensor_tensor(
                    ev[par][:, :], ev[par][:, :], rb[par][:, :], ALU.add),
                    wait=[(cv, em.count(cv))], inc=cv)
            else:
                em.do("vector", lambda e, par=par: e.tensor_tensor(
                    ev[par][:, :], ra[par][:, :], rb[par][:, :], ALU.add),
                    wait=[(f"{tag}ld{par}", ldv), (so, em.count(so))], inc=cv)
            em.do("sync", lambda e, hh=hh, th=th, par=par: e.dma_start(
                out=resid_out[128 * hh:128 * (hh + 1), 512 * th:512 * (th + 1)],
                in_=ev[par][:, :]), wait=[(cv, em.count(cv))], inc=so, incby=16)
            em.do("vector", lambda e, hh=hh, th=th, par=par: e.tensor_copy(
                xbf_all[:, 1024 * hh + 512 * th:1024 * hh + 512 * th + 512],
                ev[par][:, :]), wait=[(cv, em.count(cv))], inc=cv)
            evm[par] = em.count(cv)
    em.wait("sync", so)
    em.phase_barrier()
    for f in reversed(scopes):
        f()



def emit_gemm_resid(em, tag, panels, actT, bias, resid_in, resid_out, xbf_all,
                    w_f32=True):
    scopes = []
    def ssb(n, shp, dt=BF16):
        t, f = em.sb_scope(n, shp, dt); scopes.append(f); return t
    act = ssb(f"{tag}_act", [128, 32 * 1024])
    wt = [ssb(f"{tag}_wt", [128, 32 * 128]) for _ in range(4)]
    sbias = ssb(f"{tag}_bias", [128, 32], F32)
    rt = [ssb(f"{tag}_rt", [128, 512], F32) for _ in range(2)]
    ev = ssb(f"{tag}_ev", [128, 512], F32)
    pd, pd_free = zip(*[em.ps_scope(f"{tag}_pd", [128, 512]) for i in range(4)])
    ld, gl, cv, cs, ct, so, la = (f"{tag}{n}" for n in
                                  ("ld", "gl", "cv", "cs", "ct", "so", "la"))
    em.do("sync", lambda e: e.dma_start(
        out=act.ap().rearrange("p (kk t) -> p kk t", t=1024),
        in_=actT.rearrange("(kk p) t -> p kk t", p=128)), inc=la, incby=16)
    em.do("sync", lambda e: e.dma_start(out=sbias[:, :], in_=bias), inc=la, incby=16)
    for hg in range(8):
        for i in range(4):
            if w_f32:
                em.do("gpsimd", lambda e, hg=hg, i=i: e.dma_start(
                    out=wt[i].ap().rearrange("p (kk c) -> p kk c", c=128),
                    in_=panels[4 * hg + i].rearrange("(kk p) c -> p kk c", p=128)),
                    wait=[(ct, em.count(ct))], inc=gl, incby=16)
            else:
                em.do("sync", lambda e, hg=hg, i=i: e.dma_start(
                    out=wt[i].ap().rearrange("p (kk c) -> p kk c", c=128),
                    in_=panels[4 * hg + i].rearrange("(kk p) c -> p kk c", p=128)),
                    wait=[(ct, em.count(ct))], inc=ld, incby=16)
        for th in range(2):
            mwait = [(la, em.count(la)), (ld, em.count(ld)), (gl, em.count(gl)),
                     (cv, em.count(cv))]
            for kk in range(32):
                for i in range(4):
                    em.do("tensor", lambda e, kk=kk, i=i, th=th: e.matmul(
                        pd[i][:, :], wt[i][:, 128 * kk:128 * (kk + 1)],
                        act[:, 1024 * kk + 512 * th:1024 * kk + 512 * th + 512],
                        start=(kk == 0), stop=(kk == 31), skip_group_check=True),
                        wait=(mwait if kk == 0 else ()), inc=ct)
            for i in range(4):
                hh = 4 * hg + i
                buf = rt[i % 2]
                em.do("sync", lambda e, hh=hh, th=th, buf=buf: e.dma_start(
                    out=buf[:, :], in_=resid_in[128 * hh:128 * (hh + 1),
                                             512 * th:512 * (th + 1)]),
                    wait=[(cv, em.count(cv))], inc=ld, incby=16)
                em.do("vector", lambda e, i=i, hh=hh: e.tensor_scalar(
                    ev[:, :], pd[i][:, :], sbias[:, hh:hh + 1], None, ALU.add),
                    wait=[(ct, em.count(ct)), (so, em.count(so)), (ld, em.count(ld))],
                    inc=cv)
                em.do("vector", lambda e, buf=buf: e.tensor_tensor(
                    ev[:, :], ev[:, :], buf[:, :], ALU.add),
                    wait=[(cv, em.count(cv))], inc=cv)
                em.do("sync", lambda e, hh=hh, th=th: e.dma_start(
                    out=resid_out[128 * hh:128 * (hh + 1), 512 * th:512 * (th + 1)],
                    in_=ev[:, :]),
                    wait=[(cv, em.count(cv))], inc=so, incby=16)
                em.do("vector", lambda e, hh=hh, th=th: e.tensor_copy(
                    xbf_all[:, 1024 * hh + 512 * th:1024 * hh + 512 * th + 512],
                    ev[:, :]), wait=[(cv, em.count(cv))], inc=cv)
    em.wait("sync", so)
    em.phase_barrier()
    for f in reversed(list(pd_free)):
        f()
    for f in reversed(scopes):
        f()


def emit_ln_sbuf(em, tag, xbf_all, xln_hbm):
    scopes = []
    def ssb(n, shp, dt=F32):
        t, f = em.sb_scope(n, shp, dt); scopes.append(f); return t
    sq = [ssb(f"{tag}_sq", [128, 512], BF16) for _ in range(2)]
    ones_b = ssb(f"{tag}_ones", [128, 1], BF16)
    ones_f = ssb(f"{tag}_onesf", [1, 128])
    stat = ssb(f"{tag}_stat", [1, 3 * 512])
    rmr = ssb(f"{tag}_rmr", [1, 2 * 512])
    stmp = [ssb(f"{tag}_tmp", [128, 512]) for _ in range(2)]
    sxo = [ssb(f"{tag}_xo", [128, 512], BF16) for _ in range(2)]
    p_sum = [em.ps_scope(f"{tag}_ps", [1, 512]) for _ in range(2)]
    p_bc = [em.ps_scope(f"{tag}_pbc", [128, 512]) for _ in range(2)]
    CH = 512
    ld, cv, cs, ct, so = (f"{tag}{n}" for n in ("ld", "cv", "cs", "ct", "so"))
    em.do("vector", lambda e: e.memset(ones_b[:], 1.0), inc=cv)
    em.do("vector", lambda e: e.memset(ones_f[:], 1.0), inc=cv)
    for t in range(2):
        sqm = [0, 0]
        for hh in range(32):
            par = hh % 2
            xap = xbf_all[:, 1024 * hh + CH * t:1024 * hh + CH * t + CH]
            em.do("scalar", lambda e, xap=xap, par=par: e.activation(
                sq[par][:, :], xap, AF.Square),
                wait=[(cv, em.count(cv)), (ct, sqm[par])], inc=cs)
            em.do("tensor", lambda e, xap=xap, hh=hh: e.matmul(
                p_sum[0][0][0:1, :], ones_b[:], xap,
                start=(hh == 0), stop=(hh == 31), skip_group_check=True),
                wait=[(cs, em.count(cs)), (cv, em.count(cv))], inc=ct)
            em.do("tensor", lambda e, hh=hh, par=par: e.matmul(
                p_sum[1][0][0:1, :], ones_b[:], sq[par][:, :],
                start=(hh == 0), stop=(hh == 31), skip_group_check=True), inc=ct)
            sqm[par] = em.count(ct)
        for half in range(2):
            em.do("vector", lambda e, half=half: e.tensor_scalar(
                stat[0:1, CH * half:CH * (half + 1)], p_sum[half][0][0:1, :],
                1.0 / H, None, ALU.mult),
                wait=[(ct, em.count(ct))], inc=cv)
        em.do("vector", lambda e: e.tensor_tensor(
            stat[0:1, 2 * CH:3 * CH], stat[0:1, 0:CH], stat[0:1, 0:CH], ALU.mult),
            wait=[(cv, em.count(cv))], inc=cv)
        em.do("vector", lambda e: e.tensor_tensor(
            stat[0:1, CH:2 * CH], stat[0:1, CH:2 * CH], stat[0:1, 2 * CH:3 * CH],
            ALU.subtract), wait=[(cv, em.count(cv))], inc=cv)
        em.do("vector", lambda e: e.tensor_scalar(
            stat[0:1, CH:2 * CH], stat[0:1, CH:2 * CH], EPS_LN, None, ALU.add),
            wait=[(cv, em.count(cv))], inc=cv)
        em.do("scalar", lambda e: e.activation(
            stat[0:1, CH:2 * CH], stat[0:1, CH:2 * CH], AF.Ln),
            wait=[(cv, em.count(cv))], inc=cs)
        em.do("scalar", lambda e: e.activation(
            rmr[0:1, 0:CH], stat[0:1, CH:2 * CH], AF.Exp, scale=-0.5),
            wait=[(cs, em.count(cs))], inc=cs)
        em.do("vector", lambda e: e.tensor_tensor(
            rmr[0:1, CH:2 * CH], rmr[0:1, 0:CH], stat[0:1, 0:CH], ALU.mult),
            wait=[(cs, em.count(cs))], inc=cv)
        for half in range(2):
            em.do("tensor", lambda e, half=half: e.matmul(
                p_bc[half][0][:, :], ones_f[0:1, :], rmr[0:1, CH * half:CH * (half + 1)],
                start=True, stop=True, skip_group_check=True),
                wait=[(cv, em.count(cv))], inc=ct)
        for hh in range(32):
            par = hh % 2
            xap = xbf_all[:, 1024 * hh + CH * t:1024 * hh + CH * t + CH]
            em.do("vector", lambda e, xap=xap, par=par: e.tensor_tensor(
                stmp[par][:], xap, p_bc[0][0][:, :], ALU.mult),
                wait=[(ct, em.count(ct))], inc=cv)
            em.do("vector", lambda e, par=par: e.tensor_tensor(
                sxo[par][:], stmp[par][:], p_bc[1][0][:, :], ALU.subtract),
                wait=[(cv, em.count(cv)), (so, em.count(so))], inc=cv)
            em.do("sync", lambda e, t=t, hh=hh, par=par: e.dma_start(
                out=xln_hbm[128 * hh:128 * (hh + 1), CH * t:CH * (t + 1)],
                in_=sxo[par][:]),
                wait=[(cv, em.count(cv))], inc=so, incby=16)
    em.wait("sync", so)
    em.phase_barrier()
    for pf in reversed(p_sum + p_bc):
        pf[1]()
    for f in reversed(scopes):
        f()


def emit_ffn(em, xln2_hbm, gate_tile, up_tile, down_pan, hp_hbm,
             resid2, resid3, xbf3_all):
    scopes = []
    def ssb(n, shp, dt=BF16):
        t, f = em.sb_scope(n, shp, dt); scopes.append(f); return t
    act = ssb("ff_act", [128, 32 * 1024])
    gp = [ssb("ff_gp", [128, 32 * 128]) for _ in range(2)]
    up = [ssb("ff_up", [128, 32 * 128]) for _ in range(2)]
    sig = ssb("ff_sig", [128, 512], F32)
    t2 = ssb("ff_t2", [128, 512], F32)
    hp = [ssb("ff_hp", [128, 512]) for _ in range(4)]
    pg, pg_free = zip(*[em.ps_scope("ff_pg", [128, 512]) for _ in range(4)])
    pu, pu_free = zip(*[em.ps_scope("ff_pu", [128, 512]) for _ in range(4)])
    ld, gl, cv, cs, ct, so = ("ffld", "ffgl", "ffv", "ffcs", "ffct", "ffso")
    em.do("sync", lambda e: e.dma_start(
        out=act.ap().rearrange("p (kk t) -> p kk t", t=1024),
        in_=xln2_hbm.rearrange("(kk p) t -> p kk t", p=128)), inc="ffla", incby=16)
    mm_mark = [0, 0]
    ev_mark = [0, 0]
    ld_mark = [0, 0]
    so_mark = [0, 0, 0, 0]
    for ii in range(88):
        par = ii % 2
        ldp = f"ffld{par}"
        em.do("sync", lambda e, sap=gate_tile(ii), par=par: e.dma_start(
            out=gp[par].ap(), in_=sap),
            wait=[(ct, mm_mark[par])], inc=ldp, incby=16)
        em.do("sync", lambda e, sap=up_tile(ii), par=par: e.dma_start(
            out=up[par].ap(), in_=sap),
            wait=[(ct, mm_mark[par])], inc=ldp, incby=16)
        ld_mark[par] = em.count(ldp)
        for kk in range(32):
            for th in range(2):
                rap = act[:, 1024 * kk + 512 * th:1024 * kk + 512 * th + 512]
                em.do("tensor", lambda e, kk=kk, th=th, rap=rap, par=par: e.matmul(
                    pg[2 * par + th][:, :], gp[par][:, 128 * kk:128 * (kk + 1)],
                    rap, start=(kk == 0), stop=(kk == 31), skip_group_check=True),
                    wait=([(f"ffld{par}", ld_mark[par]), ("ffla", 16),
                           (cv, ev_mark[par])]
                          if (kk == 0 and th == 0) else ()), inc=ct)
            for th in range(2):
                rap = act[:, 1024 * kk + 512 * th:1024 * kk + 512 * th + 512]
                em.do("tensor", lambda e, kk=kk, th=th, rap=rap, par=par: e.matmul(
                    pu[2 * par + th][:, :], up[par][:, 128 * kk:128 * (kk + 1)],
                    rap, start=(kk == 0), stop=(kk == 31), skip_group_check=True),
                    inc=ct)
        mm_mark[par] = em.count(ct)
        for th in range(2):
            em.do("scalar", lambda e, th=th, par=par: e.activation(
                sig[:, :], pg[2 * par + th][:, :], AF.Sigmoid),
                wait=[(ct, mm_mark[par]), (cv, em.count(cv))], inc=cs)
            em.do("vector", lambda e, th=th, par=par: e.tensor_tensor(
                t2[:, :], sig[:, :], pg[2 * par + th][:, :], ALU.mult),
                wait=[(cs, em.count(cs)), (cv, em.count(cv))], inc=cv)
            em.do("vector", lambda e, th=th, par=par: e.tensor_tensor(
                hp[2 * par + th][:, :], t2[:, :], pu[2 * par + th][:, :], ALU.mult),
                wait=[(cv, em.count(cv)), (so, em.count(so))], inc=cv)
            em.do("sync", lambda e, ii=ii, th=th, par=par: e.dma_start(
                out=hp_hbm[ii, :, 512 * th:512 * (th + 1)], in_=hp[2 * par + th][:, :]),
                wait=[(cv, em.count(cv))], inc=so, incby=16)
            so_mark[2 * par + th] = em.count(so)
        ev_mark[par] = em.count(cv)
    em.wait("sync", so)
    em.phase_barrier()
    for f in reversed(list(pg_free) + list(pu_free)):
        f()
    for f in reversed(scopes[1:]):
        f()
    scopes[0]()

    scopes2 = []
    def ssb2(n, shp, dt=BF16):
        t, f = em.sb_scope(n, shp, dt); scopes2.append(f); return t
    NDB = 4
    wdt = [ssb2("ff_wdt", [128, 512]) for _ in range(NDB)]
    hpc = [ssb2("ff_hpc", [128, 1024]) for _ in range(NDB)]
    rt = [ssb2("ff_rt", [128, 512], F32) for _ in range(2)]
    ev = [ssb2("ff_ev", [128, 512], F32) for _ in range(2)]
    pdn, pdn_free = zip(*[em.ps_scope("ff_pdn", [128, 512]) for _ in range(8)])
    mmk = [0] * NDB
    dso_mark = [0, 0]
    for hp8 in range(8):
        ldk = [0] * NDB
        for kk in range(88):
            par = kk % NDB
            ldp = f"ffldd{par}"
            em.do("sync", lambda e, hp8=hp8, kk=kk, par=par: e.dma_start(
                out=wdt[par][:, :], in_=down_pan[hp8, 128 * kk:128 * (kk + 1), :]),
                wait=[(ct, mmk[par])], inc=ldp, incby=16)
            em.do("sync", lambda e, kk=kk, par=par: e.dma_start(
                out=hpc[par][:, :], in_=hp_hbm[kk, :, :]),
                wait=[(ct, mmk[par])], inc=ldp, incby=16)
            ldk[par] = em.count(ldp)
            for i in range(4):
                for th in range(2):
                    em.do("tensor", lambda e, kk=kk, i=i, th=th, par=par: e.matmul(
                        pdn[4 * th + i][:, :], wdt[par][:, 128 * i:128 * (i + 1)],
                        hpc[par][:, 512 * th:512 * (th + 1)],
                        start=(kk == 0), stop=(kk == 87), skip_group_check=True),
                        wait=([(f"ffldd{par}", ldk[par]), (cv, em.count(cv))]
                              if (th == 0 and i == 0) else ()), inc=ct)
            mmk[par] = em.count(ct)
        for th in range(2):
            for i in range(4):
                hh = 4 * hp8 + i
                par = i % 2
                em.do("sync", lambda e, hh=hh, th=th, par=par: e.dma_start(
                    out=rt[par][:, :], in_=resid2[128 * hh:128 * (hh + 1),
                                                  512 * th:512 * (th + 1)]),
                    wait=[(cv, em.count(cv))], inc="ffrt", incby=16)
                em.do("vector", lambda e, i=i, th=th, par=par: e.tensor_tensor(
                    ev[par][:, :], pdn[4 * th + i][:, :], rt[par][:, :], ALU.add),
                    wait=[(ct, em.count(ct)), (so, em.count(so)),
                          ("ffrt", em.count("ffrt"))],
                    inc=cv)
                em.do("sync", lambda e, hh=hh, th=th, par=par: e.dma_start(
                    out=resid3[128 * hh:128 * (hh + 1), 512 * th:512 * (th + 1)],
                    in_=ev[par][:, :]),
                    wait=[(cv, em.count(cv))], inc=so, incby=16)
                dso_mark[par] = em.count(so)
                em.do("vector", lambda e, hh=hh, th=th, par=par: e.tensor_copy(
                    xbf3_all[:, 1024 * hh + 512 * th:1024 * hh + 512 * th + 512],
                    ev[par][:, :]), wait=[(cv, em.count(cv))], inc=cv)
        mmk = [em.count(ct)] * NDB
    em.wait("sync", so)
    em.phase_barrier()
    for f in reversed(list(pdn_free)):
        f()
    for f in reversed(scopes2):
        f()



def emit_inproj(em, xg3_tile, panels, gateT, hbc_out, dt_out, ident):
    scopes = []
    def ssb(n, shp, dt=BF16):
        t, f = em.sb_scope(n, shp, dt); scopes.append(f); return t
    sxg = [ssb("ip_xg", [128, 32 * 512]) for _ in range(2)]
    wt = [ssb("ip_wt", [128, 32 * 128]) for _ in range(2)]
    ev = [ssb("ip_ev", [128, 512], F32) for _ in range(2)]
    evb = [ssb("ip_evb", [128, 512]) for _ in range(2)]
    sident = ssb("ip_id", [128, 128])
    pj, pj_free = zip(*[em.ps_scope("ip_pj", [128, 512]) for _ in range(2)])
    ptr, ptr_free = em.ps_scope("ip_ptr", [128, 128], BF16)
    ld, gl, cv, cs, ct, so = ("ipld", "ipgl", "ipv", "ipcs", "ipct", "ipso")
    em.do("sync", lambda e: e.dma_start(out=sident[:, :], in_=ident), inc="ipid", incby=16)
    wm = [0, 0]
    xm = [0, 0]
    em_ = [0, 0]
    evm_cv = [0, 0]
    evm_cs = [0, 0]
    it = 0
    for ts in range(8):
        xpar = ts % 2
        em.do("sync", lambda e, ts=ts, xpar=xpar: e.dma_start(
            out=sxg[xpar].ap().rearrange("p (kk t) -> p kk t", t=512),
            in_=xg3_tile(ts)),
            wait=[(ct, xm[xpar])], inc=f"ipldx{xpar}", incby=16)
        xldv = em.count(f"ipldx{xpar}")
        for cc in range(21):
            par = it % 2
            wbuf = wt[par]
            em.do("gpsimd", lambda e, cc=cc, wbuf=wbuf: e.dma_start(
                out=wbuf.ap().rearrange("p (kk c) -> p kk c", c=128),
                in_=panels[cc].rearrange("(kk p) c -> p kk c", p=128)),
                wait=[(ct, wm[par])], inc=f"ipgl{par}", incby=16)
            wldv = em.count(f"ipgl{par}")
            for kk in range(32):
                em.do("tensor", lambda e, kk=kk, wbuf=wbuf, xpar=xpar, par=par: e.matmul(
                    pj[par][:, :], wbuf[:, 128 * kk:128 * (kk + 1)],
                    sxg[xpar][:, 512 * kk:512 * (kk + 1)],
                    start=(kk == 0), stop=(kk == 31), skip_group_check=True),
                    wait=([(f"ipldx{xpar}", xldv), (f"ipgl{par}", wldv),
                           (cv, evm_cv[par]), (cs, evm_cs[par]),
                           ("ipid", 16)] if kk == 0 else ()), inc=ct)
            wm[par] = em.count(ct)
            if ts == 7:
                xm[xpar] = em.count(ct)
            mmv = em.count(ct)
            if cc < 8:
                em.do("scalar", lambda e, par=par: e.activation(
                    evb[par][:, :], pj[par][:, :], AF.Copy),
                    wait=[(ct, mmv), (so, em.count(so)), (cs, em.count(cs))], inc=cs)
                for j in range(4):
                    em.do("tensor", lambda e, j=j, par=par: e.transpose(
                        ptr[:, :], evb[par][:, 128 * j:128 * (j + 1)], sident[:, :]),
                        wait=[(cs, em.count(cs))], inc=ct)
                    em.do("scalar", lambda e, j=j, par=par: e.activation(
                        evb[par][:, 128 * j:128 * (j + 1)], ptr[:, :], AF.Copy),
                        wait=[(ct, em.count(ct))], inc=cs)
                    em.do("sync", lambda e, ts=ts, cc=cc, j=j, par=par: e.dma_start(
                        out=gateT[512 * ts + 128 * j:512 * ts + 128 * (j + 1),
                                  128 * cc:128 * (cc + 1)],
                        in_=evb[par][:, 128 * j:128 * (j + 1)]),
                        wait=[(cs, em.count(cs))], inc=so, incby=16)
            elif cc < 20:
                rr = 128 * (cc - 8)
                em.do("vector", lambda e, par=par: e.tensor_copy(
                    ev[par][:, :], pj[par][:, :]),
                    wait=[(ct, mmv), (so, em.count(so))], inc=cv)
                em.do("sync", lambda e, ts=ts, rr=rr, par=par: e.dma_start(
                    out=hbc_out[rr:rr + 128, 3 + 512 * ts:3 + 512 * (ts + 1)],
                    in_=ev[par][:, :]), wait=[(cv, em.count(cv))], inc=so, incby=16)
            else:
                em.do("vector", lambda e, par=par: e.tensor_copy(
                    ev[par][0:16, :], pj[par][0:16, :]),
                    wait=[(ct, mmv), (so, em.count(so))], inc=cv)
                em.do("sync", lambda e, ts=ts, par=par: e.dma_start(
                    out=dt_out[0:16, 512 * ts:512 * (ts + 1)], in_=ev[par][0:16, :]),
                    wait=[(cv, em.count(cv))], inc=so, incby=16)
            evm_cv[par] = em.count(cv)
            evm_cs[par] = em.count(cs)
            it += 1
    em.wait("sync", so)
    em.phase_barrier()
    ptr_free()
    for f in reversed(list(pj_free)):
        f()
    for f in reversed(scopes):
        f()



def emit_ssm(em, hbc_pad, dt_in, gateT, conv_w, conv_b, dtb, alog, Dp, nw,
             yngT, ident, identf, ynT_loc=None, rowscr=None):
    if rowscr is None:
        rowscr = em.dram_tmp("sm_rowscr", [2, 16, 128], F32)
    """Conv + SSD scan + group-RMS gate (head shard, pipelined heads)."""
    NH, NCH = 16, 32
    scopes = []
    def ssb(n, shp, dt=BF16):
        t, f = em.sb_scope(n, shp, dt); scopes.append(f); return t
    xbc = ssb("sm_xbc", [128, 12 * 4096])
    stg = [ssb("sm_stg", [128, 515], F32) for _ in range(2)]
    acc = [ssb("sm_acc", [128, 512], F32) for _ in range(2)]
    scw = ssb("sm_cw", [128, 12 * 4], F32)
    scb = ssb("sm_cb", [128, 12], F32)
    sdt = ssb("sm_dt", [16, 4096], F32)
    sd2 = ssb("sm_d2", [16, 4096], F32)
    cum = ssb("sm_cum", [16, 4096], F32)
    s16 = ssb("sm_s16", [16, 4], F32)
    cumT = ssb("sm_cumT", [128, 32 * 16], F32)
    ncumT = ssb("sm_ncumT", [128, 16], F32)
    dtpT = ssb("sm_dtpT", [128, 32 * 16], F32)
    fT = ssb("sm_fT", [128, 16], F32)
    wT = ssb("sm_wT", [128, 16], F32)
    dtot = ssb("sm_dtot", [128, 16], F32)
    Dbc = ssb("sm_Dbc", [128, 16], F32)
    rowcat = [ssb("sm_rowcat", [1, 2048], F32) for _ in range(2)]
    rrow0 = ssb("sm_rrow0", [1, 16], F32)
    sident = ssb("sm_id", [128, 128])
    sidf = ssb("sm_idf", [128, 128], F32)
    tri01 = ssb("sm_tri", [128, 128])
    xT = ssb("sm_xT", [128, 8 * 128])
    Bt = ssb("sm_Bt", [128, 2 * 128])
    dtx = [ssb("sm_dtx", [128, 64]) for _ in range(2)]
    wdtx = [ssb("sm_wdtx", [128, 64]) for _ in range(2)]
    Gs = [ssb("sm_Gs", [128, 128]) for _ in range(2)]
    Ls = [ssb("sm_Ls", [128, 128], F32) for _ in range(2)]
    Ms = [ssb("sm_Ms", [128, 128]) for _ in range(2)]
    Ssb = ssb("sm_S", [128, 16 * 64], F32)
    Sbf = ssb("sm_Sbf", [128, 16 * 64])
    ysb = ssb("sm_y", [128, 16 * 64], F32)
    yev = [ssb("sm_yev", [128, 64], F32) for _ in range(2)]
    ysq = ssb("sm_ysq", [128, 1024], F32)
    var2 = ssb("sm_var2", [128, 2], F32)
    nwbc = ssb("sm_nwbc", [128, 1024])
    nwrow = ssb("sm_nwrow", [1, 1024], F32)
    gt = ssb("sm_gt", [128, 1024])
    gsil = ssb("sm_gsil", [128, 1024])
    yng = ssb("sm_yng", [128, 1024])
    ytr = ssb("sm_ytr", [128, 8 * 128])
    ones_f = ssb("sm_onef", [1, 128], F32)
    pG, pG_f = em.ps_scope("sm_pG", [128, 128])
    pL, pL_f = zip(*[em.ps_scope("sm_pL", [128, 128]) for _ in range(2)])
    pY, pY_f = em.ps_scope("sm_pY", [128, 128])
    pS, pS_f = em.ps_scope("sm_pS", [128, 64])
    pT2, pT_f = em.ps_scope("sm_pT", [128, 256], BF16)
    pT = [pT2[:, 0:128], pT2[:, 128:256]]
    pTf, pTf_f = em.ps_scope("sm_pTf", [128, 16])
    pR, pR_f = em.ps_scope("sm_pR", [16, 128])
    ld, cv, cs, ct, so, gl = ("smld", "smv", "smcs", "smct", "smso", "smgl")

    em.do("sync", lambda e: e.dma_start(out=sident[:, :], in_=ident), inc=ld, incby=16)
    em.do("sync", lambda e: e.dma_start(out=sidf[:, :], in_=identf), inc=ld, incby=16)
    em.do("sync", lambda e: e.dma_start(out=scw[:, :], in_=conv_w), inc=ld, incby=16)
    em.do("sync", lambda e: e.dma_start(out=scb[:, :], in_=conv_b), inc=ld, incby=16)
    em.do("sync", lambda e: e.dma_start(out=sdt[:, :], in_=dt_in), inc=ld, incby=16)
    em.do("sync", lambda e: e.dma_start(out=s16[:, 0:1], in_=dtb), inc=ld, incby=16)
    em.do("sync", lambda e: e.dma_start(out=s16[:, 1:2], in_=alog), inc=ld, incby=16)
    em.do("sync", lambda e: e.dma_start(out=s16[:, 2:3], in_=Dp), inc=ld, incby=16)
    em.do("sync", lambda e: e.dma_start(out=nwrow[:, :], in_=nw), inc=ld, incby=16)
    em.do("gpsimd", lambda e: e.iota(ysq[:, 0:128], [[1, 128]], base=0,
        channel_multiplier=0, allow_small_or_imprecise_dtypes=True), inc=gl)
    em.do("gpsimd", lambda e: e.iota(ysq[:, 128:256], [[0, 128]], base=0,
        channel_multiplier=1, allow_small_or_imprecise_dtypes=True), inc=gl)
    em.do("vector", lambda e: e.tensor_tensor(tri01[:, :], ysq[:, 0:128],
        ysq[:, 128:256], ALU.is_ge), wait=[(gl, em.count(gl))], inc=cv)
    em.do("vector", lambda e: e.memset(ones_f[:, :], 1.0), inc=cv)

    cvm = [0, 0]
    for it, (ct12, ts) in enumerate([(a, b) for a in range(12) for b in range(8)]):
        par = it % 2
        em.do("sync", lambda e, ct12=ct12, ts=ts, par=par: e.dma_start(
            out=stg[par][:, :], in_=hbc_pad[128 * ct12:128 * (ct12 + 1),
                                            512 * ts:512 * ts + 515]),
            wait=[(cv, cvm[par])], inc=f"smld{par}", incby=16)
        ldv = em.count(f"smld{par}")
        em.do("vector", lambda e, ct12=ct12, par=par: e.tensor_scalar(
            acc[par][:, :], stg[par][:, 0:512], scw[:, 4 * ct12:4 * ct12 + 1],
            scb[:, ct12:ct12 + 1], ALU.mult, ALU.add),
            wait=[(f"smld{par}", ldv), (ld, em.count(ld)), (cs, em.count(cs))], inc=cv)
        for k in range(1, 4):
            em.do("vector", lambda e, ct12=ct12, k=k, par=par: e.scalar_tensor_tensor(
                acc[par][:, :], stg[par][:, k:k + 512],
                scw[:, 4 * ct12 + k:4 * ct12 + k + 1],
                acc[par][:, :], ALU.mult, ALU.add),
                wait=[(cv, em.count(cv))], inc=cv)
        cvm[par] = em.count(cv)
        em.do("scalar", lambda e, ct12=ct12, ts=ts, par=par: e.activation(
            xbc[:, 4096 * ct12 + 512 * ts:4096 * ct12 + 512 * (ts + 1)],
            acc[par][:, :], AF.Silu), wait=[(cv, cvm[par])], inc=cs)

    rcuse_mark = [0, 0]
    cum_ready = [0]
    em.do("vector", lambda e: e.tensor_scalar(
        sdt[:, :], sdt[:, :], s16[:, 0:1], None, ALU.add),
        wait=[(ld, em.count(ld))], inc=cv)
    em.do("scalar", lambda e: e.activation(sd2[:, :], sdt[:, :], AF.Abs),
          wait=[(cv, em.count(cv)), (cs, em.count(cs))], inc=cs)
    em.do("scalar", lambda e: e.activation(sd2[:, :], sd2[:, :], AF.Exp, scale=-1.0),
          wait=[(cs, em.count(cs))], inc=cs)
    em.do("vector", lambda e: e.tensor_scalar(sd2[:, :], sd2[:, :], 1.0, None, ALU.add),
          wait=[(cs, em.count(cs))], inc=cv)
    em.do("scalar", lambda e: e.activation(sd2[:, :], sd2[:, :], AF.Ln),
          wait=[(cv, em.count(cv))], inc=cs)
    em.do("vector", lambda e: e.tensor_scalar(sdt[:, :], sdt[:, :], 0.0, None, ALU.max),
          wait=[(cs, em.count(cs))], inc=cv)
    em.do("vector", lambda e: e.tensor_tensor(sdt[:, :], sd2[:, :], sdt[:, :], ALU.add),
          wait=[(cv, em.count(cv)), (cs, em.count(cs))], inc=cv)
    em.do("scalar", lambda e: e.activation(s16[:, 3:4], s16[:, 1:2], AF.Exp),
          wait=[(ld, em.count(ld))], inc=cs)
    em.do("vector", lambda e: e.tensor_scalar(
        s16[:, 3:4], s16[:, 3:4], -1.0, None, ALU.mult),
        wait=[(cs, em.count(cs))], inc=cv)
    em.do("vector", lambda e: e.tensor_scalar(
        sd2[:, :], sdt[:, :], s16[:, 3:4], None, ALU.mult),
        wait=[(cv, em.count(cv))], inc=cv)
    cur, oth = sd2, cum
    v3 = lambda t: t.ap().rearrange("p (c x) -> p c x", x=128)
    for st_ in (1, 2, 4, 8, 16, 32, 64):
        em.do("vector", lambda e, st_=st_, cur=cur, oth=oth: e.tensor_copy(
            v3(oth)[:, :, 0:st_], v3(cur)[:, :, 0:st_]),
            wait=[(cv, em.count(cv))], inc=cv)
        em.do("vector", lambda e, st_=st_, cur=cur, oth=oth: e.tensor_tensor(
            v3(oth)[:, :, st_:128], v3(cur)[:, :, st_:128],
            v3(cur)[:, :, 0:128 - st_], ALU.add),
            wait=[(cv, em.count(cv))], inc=cv)
        cur, oth = oth, cur
    assert cur is cum
    cum_ready[0] = em.count(cv)
    for c in range(NCH):
        em.do("tensor", lambda e, c=c: e.transpose(
            pTf[:, :], sdt[:, 128 * c:128 * (c + 1)], sidf[0:16, 0:16]),
            wait=[(cv, em.count(cv)), (cs, em.count(cs)), (ld, em.count(ld))], inc=ct)
        em.do("scalar", lambda e, c=c: e.activation(
            dtpT[:, 16 * c:16 * (c + 1)], pTf[:, :], AF.Copy),
            wait=[(ct, em.count(ct))], inc=cs)
    for c in range(NCH):
        em.do("tensor", lambda e, c=c: e.transpose(
            pTf[:, :], cum[:, 128 * c:128 * (c + 1)], sidf[0:16, 0:16]),
            wait=[(cv, em.count(cv)), (cs, em.count(cs))], inc=ct)
        em.do("scalar", lambda e, c=c: e.activation(
            cumT[:, 16 * c:16 * (c + 1)], pTf[:, :], AF.Copy),
            wait=[(ct, em.count(ct))], inc=cs)
    em.do("tensor", lambda e: e.transpose(pR[0:1, 0:16], s16[:, 2:3], sidf[0:16, 0:16]),
          wait=[(ld, em.count(ld)), (cs, em.count(cs))], inc=ct)
    em.do("scalar", lambda e: e.activation(rrow0[0:1, :], pR[0:1, 0:16], AF.Copy),
          wait=[(ct, em.count(ct))], inc=cs)
    em.do("tensor", lambda e: e.matmul(pG[:, 0:16], ones_f[0:1, :],
          rrow0[0:1, :], start=True, stop=True, skip_group_check=True),
          wait=[(cs, em.count(cs)), (cv, em.count(cv))], inc=ct)
    em.do("vector", lambda e: e.tensor_copy(Dbc[:, :], pG[:, 0:16]),
          wait=[(ct, em.count(ct))], inc=cv)
    for seg in range(8):
        em.do("tensor", lambda e, seg=seg: e.matmul(
            pG[:, :], ones_f[0:1, :], nwrow[0:1, 128 * seg:128 * (seg + 1)],
            start=True, stop=True, skip_group_check=True),
            wait=[(cv, em.count(cv))], inc=ct)
        em.do("vector", lambda e, seg=seg: e.tensor_copy(
            nwbc[:, 128 * seg:128 * (seg + 1)], pG[:, :]),
            wait=[(ct, em.count(ct))], inc=cv)
    em.do("vector", lambda e: e.memset(Ssb[:, :], 0.0), inc=cv)
    em.do("vector", lambda e: e.memset(Sbf[:, :], 0.0), inc=cv)

    lb_mark = [0, 0]
    ms_mark = [0, 0]
    dtx_mark = [0, 0]
    yev_mark = [0, 0]
    msw_mark = [0, 0]
    yevD_mark = [0, 0]
    pY_cs = [0]
    pY_cv = [0]
    pS_cv = [0]
    for c in range(NCH):
        tsl = slice(128 * c, 128 * (c + 1))
        ctsl = slice(16 * c, 16 * (c + 1))
        for j in range(8):
            jpar = j % 2
            em.do("tensor", lambda e, j=j, tsl=tsl, jpar=jpar: e.transpose(
                pT[jpar][:, :], xbc[:, 4096 * j:4096 * (j + 1)][:, tsl], sident[:, :]),
                wait=[(cv, em.count(cv)), (cs, em.count(cs))], inc=ct)
            em.do("scalar", lambda e, j=j, jpar=jpar: e.activation(
                xT[:, 128 * j:128 * (j + 1)], pT[jpar][:, :], AF.Copy),
                wait=[(ct, em.count(ct))], inc=cs)
        for g in range(2):
            gpar = g % 2
            em.do("tensor", lambda e, g=g, tsl=tsl, gpar=gpar: e.transpose(
                pT[gpar][:, :], xbc[:, 4096 * (8 + g):4096 * (9 + g)][:, tsl],
                sident[:, :]),
                wait=[(cv, em.count(cv)), (cs, em.count(cs))], inc=ct)
            em.do("scalar", lambda e, g=g, gpar=gpar: e.activation(
                Bt[:, 128 * g:128 * (g + 1)], pT[gpar][:, :], AF.Copy),
                wait=[(ct, em.count(ct))], inc=cs)
        for g in range(2):
            em.do("tensor", lambda e, g=g, tsl=tsl: e.matmul(
                pG[:, :], xbc[:, 4096 * (8 + g):4096 * (9 + g)][:, tsl],
                xbc[:, 4096 * (10 + g):4096 * (11 + g)][:, tsl],
                start=True, stop=True, skip_group_check=True),
                wait=[(cv, em.count(cv)), (cs, em.count(cs))], inc=ct)
            em.do("vector", lambda e, g=g: e.tensor_tensor(
                Gs[g][:, :], pG[:, :], tri01[:, :], ALU.mult),
                wait=[(ct, em.count(ct))], inc=cv)
        em.do("vector", lambda e, ctsl=ctsl: e.tensor_scalar(
            ncumT[:, :], cumT[:, ctsl], -1.0, None, ALU.mult),
            wait=[(cs, em.count(cs)), (cv, em.count(cv))], inc=cv)
        em.do("scalar", lambda e, ctsl=ctsl: e.activation(
            fT[:, :], cumT[:, ctsl], AF.Exp),
            wait=[(cv, em.count(cv))], inc=cs)
        cpar = c % 2
        em.do("sync", lambda e, tsl=tsl, cpar=cpar: e.dma_start(
            out=rowscr[cpar], in_=cum[:, tsl]),
            wait=[(ct, rcuse_mark[cpar]), (cv, cum_ready[0])],
            inc="smrb", incby=16)
        em.do("sync", lambda e, cpar=cpar: e.dma_start(
            out=rowcat[cpar].ap().rearrange("p (h t) -> p h t", t=128),
            in_=rowscr[cpar].unsqueeze(0)),
            wait=[("smrb", em.count("smrb"))], inc="smrc", incby=16)
        rcv = em.count("smrc")
        em.do("tensor", lambda e, cpar=cpar: e.matmul(
            pG[:, 0:16], ones_f[0:1, :],
            rowcat[cpar][0:1, 127:2048:128],
            start=True, stop=True, skip_group_check=True),
            wait=[("smrc", rcv), (cv, em.count(cv))], inc=ct)

        em.do("vector", lambda e, ctsl=ctsl: e.scalar_tensor_tensor(
            wT[:, :], cumT[:, ctsl], -1.0, pG[:, 0:16], ALU.mult, ALU.add),
            wait=[(ct, em.count(ct))], inc=cv)
        em.do("scalar", lambda e: e.activation(wT[:, :], wT[:, :], AF.Exp),
              wait=[(cv, em.count(cv))], inc=cs)
        em.do("scalar", lambda e: e.activation(dtot[:, :], pG[:, 0:16], AF.Exp),
              wait=[(cv, em.count(cv)), (ct, em.count(ct))], inc=cs)
        pre_cs = em.count(cs)
        pre_cv = em.count(cv)

        def stage1(h):
            par = h % 2
            g = h // 8
            em.do("tensor", lambda e, h=h, par=par, cpar=c % 2: e.matmul(
                pL[par][:, :], ones_f[0:1, :],
                rowcat[cpar][0:1, 128 * h:128 * (h + 1)],
                start=True, stop=True, skip_group_check=True),
                wait=[("smrc", rcv), (cv, lb_mark[par])], inc=ct)
            lbv = em.count(ct)
            em.do("vector", lambda e, h=h, par=par: e.tensor_scalar(
                Ls[par][:, :], pL[par][:, :], ncumT[:, h:h + 1], 0.0,
                ALU.add, ALU.min),
                wait=[(ct, lbv), (cv, msw_mark[par])], inc=cv)
            lb_mark[par] = em.count(cv)
            em.do("scalar", lambda e, par=par: e.activation(
                Ls[par][:, :], Ls[par][:, :], AF.Exp),
                wait=[(cv, lb_mark[par])], inc=cs)
            lse = em.count(cs)
            em.do("vector", lambda e, g=g, par=par: e.tensor_tensor(
                Ms[par][:, :], Ls[par][:, :], Gs[g][:, :], ALU.mult),
                wait=[(cs, lse), (ct, ms_mark[par])], inc=cv)
            em.do("vector", lambda e, h=h, c=c, par=par: e.tensor_scalar(
                dtx[par][:, :], xT[:, 64 * h:64 * h + 64],
                dtpT[:, 16 * c + h:16 * c + h + 1], None, ALU.mult),
                wait=[(ct, dtx_mark[par]), (cs, pre_cs)], inc=cv)
            em.do("vector", lambda e, h=h, par=par: e.tensor_scalar(
                wdtx[par][:, :], dtx[par][:, :], wT[:, h:h + 1], None, ALU.mult),
                wait=[(cv, em.count(cv))], inc=cv)
            msw_mark[par] = em.count(cv)

        def stage2(h):
            par = h % 2
            g = h // 8
            em.do("tensor", lambda e, par=par: e.matmul(
                pY[:, 0:64], Ms[par][:, :], dtx[par][:, :],
                start=True, stop=True, skip_group_check=True),
                wait=[(cv, msw_mark[par]), (cs, pY_cs[0]), (cv, pY_cv[0])], inc=ct)
            em.do("tensor", lambda e, g=g, tsl=tsl, h=h: e.matmul(
                pY[:, 64:128], xbc[:, 4096 * (10 + g):4096 * (11 + g)][:, tsl],
                Sbf[:, 64 * h:64 * (h + 1)],
                start=True, stop=True, skip_group_check=True), inc=ct)
            em.do("tensor", lambda e, g=g, par=par: e.matmul(
                pS[:, :], Bt[:, 128 * g:128 * (g + 1)], wdtx[par][:, :],
                start=True, stop=True, skip_group_check=True),
                wait=[(cv, pS_cv[0])], inc=ct)
            ms_mark[par] = em.count(ct)
            dtx_mark[par] = em.count(ct)
            ymm = em.count(ct)
            em.do("scalar", lambda e, par=par: e.activation(
                yev[par][:, :], pY[:, 0:64], AF.Copy),
                wait=[(ct, ymm), (cv, yevD_mark[par])], inc=cs)
            yev_mark[par] = em.count(cs)
            pY_cs[0] = em.count(cs)
            em.do("vector", lambda e, h=h, par=par: e.scalar_tensor_tensor(
                yev[par][:, :], pY[:, 64:128], fT[:, h:h + 1], yev[par][:, :],
                ALU.mult, ALU.add),
                wait=[(cs, yev_mark[par])], inc=cv)
            pY_cv[0] = em.count(cv)
            em.do("vector", lambda e, h=h, par=par: e.scalar_tensor_tensor(
                ysb[:, 64 * h:64 * (h + 1)], xT[:, 64 * h:64 * h + 64],
                Dbc[:, h:h + 1], yev[par][:, :], ALU.mult, ALU.add),
                wait=[(cv, em.count(cv))], inc=cv)
            yevD_mark[par] = em.count(cv)
            em.do("vector", lambda e, h=h: e.scalar_tensor_tensor(
                Ssb[:, 64 * h:64 * (h + 1)], Ssb[:, 64 * h:64 * (h + 1)],
                dtot[:, h:h + 1], pS[:, :], ALU.mult, ALU.add),
                wait=[(ct, ymm), (cv, pS_cv[0])], inc=cv)
            pS_cv[0] = em.count(cv)

        for h in range(NH + 1):
            if h < NH:
                stage1(h)
            if h >= 1:
                stage2(h - 1)
        rcuse_mark[cpar] = em.count(ct)
        em.do("vector", lambda e: e.tensor_copy(Sbf[:, :], Ssb[:, :]),
              wait=[(cv, em.count(cv)), (ct, em.count(ct))], inc=cv)
        em.do("sync", lambda e, tsl=tsl: e.dma_start(out=gt[:, :], in_=gateT[tsl, :]),
              wait=[(cs, em.count(cs)), (cv, em.count(cv))], inc="smgt", incby=16)
        em.do("scalar", lambda e: e.activation(ysq[:, :], ysb[:, :], AF.Square),
              wait=[(cv, em.count(cv))], inc=cs)
        for g in range(2):
            em.do("vector", lambda e, g=g: e.reduce_sum(
                var2[:, g:g + 1], ysq[:, 512 * g:512 * (g + 1)], axis=AX.X),
                wait=[(cs, em.count(cs))], inc=cv)
        em.do("vector", lambda e: e.tensor_scalar(
            var2[:, :], var2[:, :], 1.0 / 512.0, EPS_LN, ALU.mult, ALU.add),
            wait=[(cv, em.count(cv))], inc=cv)
        em.do("scalar", lambda e: e.activation(var2[:, :], var2[:, :], AF.Ln),
              wait=[(cv, em.count(cv))], inc=cs)
        em.do("scalar", lambda e: e.activation(var2[:, :], var2[:, :], AF.Exp,
              scale=-0.5), wait=[(cs, em.count(cs))], inc=cs)
        for g in range(2):
            em.do("vector", lambda e, g=g: e.tensor_scalar(
                ysq[:, 512 * g:512 * (g + 1)], ysb[:, 512 * g:512 * (g + 1)],
                var2[:, g:g + 1], None, ALU.mult),
                wait=[(cs, em.count(cs))], inc=cv)
        em.do("vector", lambda e: e.tensor_tensor(ysq[:, :], ysq[:, :], nwbc[:, :],
              ALU.mult), wait=[(cv, em.count(cv))], inc=cv)
        em.do("scalar", lambda e: e.activation(gsil[:, :], gt[:, :], AF.Silu),
              wait=[("smgt", em.count("smgt")), (cv, em.count(cv))], inc=cs)
        em.do("vector", lambda e: e.tensor_tensor(yng[:, :], ysq[:, :], gsil[:, :],
              ALU.mult), wait=[(cs, em.count(cs)), (so, em.count(so))], inc=cv)
        if yngT is not None:
            em.do("sync", lambda e, tsl=tsl: e.dma_start(out=yngT[tsl, :],
                  in_=yng[:, :]), wait=[(cv, em.count(cv))], inc=so, incby=16)
        if ynT_loc is not None:
            for j in range(8):
                jpar = j % 2
                em.do("tensor", lambda e, j=j, jpar=jpar: e.transpose(
                    pT[jpar][:, :], yng[:, 128 * j:128 * (j + 1)], sident[:, :]),
                    wait=[(cv, em.count(cv)), (cs, em.count(cs))], inc=ct)
                em.do("scalar", lambda e, j=j, jpar=jpar: e.activation(
                    ytr[:, 128 * j:128 * (j + 1)], pT[jpar][:, :], AF.Copy),
                    wait=[(ct, em.count(ct))], inc=cs)
            em.do("sync", lambda e, c=c: e.dma_start(
                out=ynT_loc.rearrange("(j p) t -> p j t", p=128)[:, :,
                     128 * c:128 * (c + 1)],
                in_=ytr.ap().rearrange("p (j t) -> p j t", t=128)),
                wait=[(cs, em.count(cs))], inc=so, incby=16)
    em.wait("sync", so)
    em.phase_barrier()
    for f in (pR_f, pTf_f, pT_f, pS_f, pY_f) + tuple(reversed(pL_f)) + (pG_f,):
        f()
    for f in reversed(scopes):
        f()



def emit_l1ffn(em, xln4_hbm, routerT, router_b, expert_b, resid4, out_final,
               ident, identf):
    scopes = []
    def ssb(n, shp, dt=BF16):
        t, f = em.sb_scope(n, shp, dt); scopes.append(f); return t
    act = ssb("lf_act", [128, 32 * 1024])
    rsb = ssb("lf_r", [128, 32 * 64])
    ebT = ssb("lf_eb", [64, 4096])
    rbb = ssb("lf_rbb", [128, 64], F32)
    ones_f = ssb("lf_onef", [1, 128], F32)
    lg = ssb("lf_lg", [128, 64], F32)
    wrk = ssb("lf_wrk", [128, 64], F32)
    mwrk = ssb("lf_mwrk", [128, 64], F32)
    m1 = ssb("lf_m1", [128, 4], F32)
    m1f = ssb("lf_m1f", [128, 1], F32)
    msk = ssb("lf_msk", [128, 64])
    mskT = ssb("lf_mskT", [64, 1024])
    wsr = ssb("lf_wsr", [1, 1024], F32)
    sident = ssb("lf_id", [128, 128])
    sidf = ssb("lf_idf", [128, 128], F32)
    rt = ssb("lf_rt", [128, 512], F32)
    ev = ssb("lf_ev", [128, 512], F32)
    ev2 = ssb("lf_ev2", [128, 512], F32)
    plg, plg_f = em.ps_scope("lf_plg", [128, 64])
    pt_, pt_f = em.ps_scope("lf_pt", [128, 128], BF16)
    ptf, ptf_f = em.ps_scope("lf_ptf", [1, 128])
    pwb = [em.ps_scope("lf_pwb", [128, 512]) for _ in range(2)]
    pmx = [em.ps_scope("lf_pmx", [128, 512]) for _ in range(2)]
    ld, gl, cv, cs, ct, so = ("lfld", "lfgl", "lfv", "lfcs", "lfct", "lfso")
    em.do("sync", lambda e: e.dma_start(
        out=act.ap().rearrange("p (kk t) -> p kk t", t=1024),
        in_=xln4_hbm.rearrange("(kk p) t -> p kk t", p=128)), inc=ld, incby=16)
    em.do("sync", lambda e: e.dma_start(out=sident[:, :], in_=ident), inc=ld, incby=16)
    em.do("sync", lambda e: e.dma_start(out=sidf[:, :], in_=identf), inc=ld, incby=16)
    em.do("gpsimd", lambda e: e.dma_start(
        out=rsb.ap().rearrange("p (kk c) -> p kk c", c=64),
        in_=routerT.rearrange("(kk p) c -> p kk c", p=128)), inc=gl, incby=16)
    em.do("gpsimd", lambda e: e.dma_start(out=ebT[:, :], in_=expert_b), inc=gl, incby=16)
    em.do("gpsimd", lambda e: e.dma_start(out=lg[0:1, :], in_=router_b), inc=gl, incby=16)
    em.do("vector", lambda e: e.memset(ones_f[:, :], 1.0), inc=cv)
    em.do("tensor", lambda e: e.matmul(plg[:, :], ones_f[0:1, :], lg[0:1, :],
          start=True, stop=True, skip_group_check=True),
          wait=[(gl, em.count(gl)), (cv, em.count(cv))], inc=ct)
    em.do("vector", lambda e: e.tensor_copy(rbb[:, :], plg[:, :]),
          wait=[(ct, em.count(ct))], inc=cv)
    for tt in range(8):
        for kk in range(32):
            em.do("tensor", lambda e, kk=kk, tt=tt: e.matmul(
                plg[:, :], act[:, 1024 * kk + 128 * tt:1024 * kk + 128 * tt + 128],
                rsb[:, 64 * kk:64 * (kk + 1)],
                start=(kk == 0), stop=(kk == 31), skip_group_check=True),
                wait=([(ld, em.count(ld)), (gl, em.count(gl)), (cv, em.count(cv))]
                      if kk == 0 else ()), inc=ct)
        em.do("vector", lambda e: e.tensor_tensor(lg[:, :], plg[:, :], rbb[:, :],
              ALU.add), wait=[(ct, em.count(ct)), (cs, em.count(cs))], inc=cv)
        em.do("vector", lambda e: e.reduce_max(m1[:, 0:1], lg[:, :], axis=AX.X),
              wait=[(cv, em.count(cv))], inc=cv)
        em.do("vector", lambda e: e.tensor_scalar(m1[:, 1:2], m1[:, 0:1], -1.0, None,
              ALU.mult), wait=[(cv, em.count(cv))], inc=cv)
        em.do("scalar", lambda e: e.activation(lg[:, :], lg[:, :], AF.Exp,
              bias=m1[:, 1:2], accum_out=m1[:, 2:3]),
              wait=[(cv, em.count(cv))], inc=cs)
        em.do("vector", lambda e: e.reciprocal(m1[:, 3:4], m1[:, 2:3]),
              wait=[(cs, em.count(cs))], inc=cv)
        em.do("vector", lambda e: e.tensor_scalar(lg[:, :], lg[:, :], m1[:, 3:4],
              None, ALU.mult), wait=[(cv, em.count(cv))], inc=cv)
        em.do("vector", lambda e: e.tensor_copy(wrk[:, :], lg[:, :]),
              wait=[(cv, em.count(cv))], inc=cv)
        for it in range(8):
            em.do("vector", lambda e: e.reduce_max(m1[:, 0:1], wrk[:, :], axis=AX.X),
                  wait=[(cv, em.count(cv))], inc=cv)
            if it < 7:
                em.do("vector", lambda e: e.tensor_scalar(
                    mwrk[:, :], wrk[:, :], m1[:, 0:1], None, ALU.is_ge),
                    wait=[(cv, em.count(cv))], inc=cv)
                em.do("vector", lambda e: e.scalar_tensor_tensor(
                    wrk[:, :], mwrk[:, :], -4.0, wrk[:, :], ALU.mult, ALU.add),
                    wait=[(cv, em.count(cv))], inc=cv)
        em.do("vector", lambda e: e.tensor_scalar(msk[:, :], lg[:, :], m1[:, 0:1],
              None, ALU.is_ge), wait=[(cv, em.count(cv))], inc=cv)
        em.do("vector", lambda e: e.tensor_tensor(wrk[:, :], lg[:, :], msk[:, :],
              ALU.mult), wait=[(cv, em.count(cv))], inc=cv)
        em.do("vector", lambda e: e.reduce_sum(m1[:, 2:3], wrk[:, :], axis=AX.X),
              wait=[(cv, em.count(cv))], inc=cv)
        em.do("vector", lambda e: e.tensor_copy(msk[:, :], wrk[:, :]),
              wait=[(cv, em.count(cv))], inc=cv)
        em.do("vector", lambda e: e.tensor_copy(m1f[:, 0:1], m1[:, 2:3]),
              wait=[(cv, em.count(cv))], inc=cv)
        em.do("tensor", lambda e: e.transpose(pt_[0:64, :], msk[:, :], sident[:, :]),
              wait=[(cv, em.count(cv)), (cs, em.count(cs))], inc=ct)
        em.do("scalar", lambda e, tt=tt: e.activation(
            mskT[:, 128 * tt:128 * (tt + 1)], pt_[0:64, :], AF.Copy),
            wait=[(ct, em.count(ct))], inc=cs)
        em.do("tensor", lambda e: e.transpose(ptf[0:1, :], m1f[:, 0:1], sidf[:, :]),
              wait=[(cv, em.count(cv)), (cs, em.count(cs))], inc=ct)
        em.do("scalar", lambda e, tt=tt: e.activation(
            wsr[0:1, 128 * tt:128 * (tt + 1)], ptf[0:1, :], AF.Copy),
            wait=[(ct, em.count(ct))], inc=cs)
    for th in range(2):
        em.do("tensor", lambda e, th=th: e.matmul(
            pwb[th][0][:, :], ones_f[0:1, :], wsr[0:1, 512 * th:512 * (th + 1)],
            start=True, stop=True, skip_group_check=True),
            wait=[(cs, em.count(cs)), (cv, em.count(cv))], inc=ct)
    for hh in range(32):
        for th in range(2):
            em.do("tensor", lambda e, hh=hh, th=th: e.matmul(
                pmx[th][0][:, :], ebT[:, 128 * hh:128 * (hh + 1)],
                mskT[:, 512 * th:512 * (th + 1)],
                start=True, stop=True, skip_group_check=True),
                wait=[(cs, em.count(cs)), (cv, em.count(cv))], inc=ct)
            em.do("sync", lambda e, hh=hh, th=th: e.dma_start(
                out=rt[:, :], in_=resid4[128 * hh:128 * (hh + 1),
                                         512 * th:512 * (th + 1)]),
                wait=[(cv, em.count(cv))], inc="lfrt", incby=16)
            em.do("vector", lambda e, hh=hh, th=th: e.tensor_tensor(
                ev[:, :], act[:, 1024 * hh + 512 * th:1024 * hh + 512 * th + 512],
                pwb[th][0][:, :], ALU.mult),
                wait=[(ct, em.count(ct)), (so, em.count(so)), (cv, em.count(cv))], inc=cv)
            em.do("vector", lambda e, th=th: e.tensor_tensor(
                ev2[:, :], ev[:, :], pmx[th][0][:, :], ALU.add),
                wait=[(cv, em.count(cv))], inc=cv)
            em.do("vector", lambda e: e.tensor_tensor(
                ev[:, :], ev2[:, :], rt[:, :], ALU.add),
                wait=[(cv, em.count(cv)), ("lfrt", em.count("lfrt"))], inc=cv)
            em.do("sync", lambda e, hh=hh, th=th: e.dma_start(
                out=out_final[128 * hh:128 * (hh + 1), 512 * th:512 * (th + 1)],
                in_=ev[:, :]), wait=[(cv, em.count(cv))], inc=so, incby=16)
    em.wait("sync", so)
    em.phase_barrier()
    for pf in reversed([plg_f, pt_f, ptf_f] + [p[1] for p in pwb] + [p[1] for p in pmx]):
        pf()
    for f in reversed(scopes):
        f()



def _cc(em, kind, rg, in_ap, out_ap, sem):
    em.do("gpsimd", lambda e: e.collective_compute(
        kind, ALU.add if kind in ("AllReduce", "ReduceScatter") else ALU.bypass,
        replica_groups=rg, ins=[in_ap], outs=[out_ap]), inc=sem, incby=None)


G4 = [[0, 1, 2, 3], [4, 5, 6, 7]]
G8 = [list(range(8))]


def build_fused():
    em = Em("FU")
    x0 = em.dram_in("x0T", [H, TC], F32)
    gate_pk = em.dram_in("gate_pk", [2048, 1408], U8)
    up_pk = em.dram_in("up_pk", [2048, 1408], U8)
    down_pk = em.dram_in("down_pk", [5504, 512], U8)
    gate_sT = em.dram_in("gate_sT", [64, 1408], F32)
    up_sT = em.dram_in("up_sT", [64, 1408], F32)
    down_sT = em.dram_in("down_sT", [172, 512], F32)
    gate_zT = em.dram_in("gate_zT", [64, 1408], F32)
    up_zT = em.dram_in("up_zT", [64, 1408], F32)
    down_zT = em.dram_in("down_zT", [172, 512], F32)
    sel4 = None
    wqT = em.dram_in("wqT", [4096, 1536], F32)
    bq = em.dram_in("bq", [1, 1536], F32)
    invf = em.dram_in("invf", [1, 64], F32)
    ident = em.dram_in("ident", [128, 128], BF16)
    identf = em.dram_in("identf", [128, 128], F32)
    tri = em.dram_in("tri", [128, 128], F32)
    half = em.dram_in("half", [128, 64], F32)
    wdTs = em.dram_in("wdTs", [1024, 4096], F32)
    bd = em.dram_in("bd", [128, 32], F32)
    ipp = em.dram_in("ipp", [21, 4096, 128], F32)
    conv_w = em.dram_in("conv_w", [128, 48], F32)
    conv_b = em.dram_in("conv_b", [128, 12], F32)
    dtb = em.dram_in("dtb", [16, 1], F32)
    alog = em.dram_in("alog", [16, 1], F32)
    Dp = em.dram_in("Dp", [16, 1], F32)
    nw = em.dram_in("nw", [1, 1024], F32)
    opTs = em.dram_in("opTs", [1024, 4096], F32)
    routerT = em.dram_in("routerT", [4096, 64], F32)
    router_b = em.dram_in("router_b", [1, 64], F32)
    expert_b = em.dram_in("expert_b", [64, 4096], F32)
    out_final = em.dram_out("out_final", [H, TC], F32)
    xln1 = em.dram_tmp("xln1", [128, 32 * TC], BF16)
    gate_w = em.dram_tmp("gate_w", [1408, 4096], BF16)
    up_w = em.dram_tmp("up_w", [1408, 4096], BF16)
    down_w = em.dram_tmp("down_w", [11264, 512], BF16)
    xg1 = em.dram_tmp("xg1", [512, 32 * TC], BF16)
    gate_ag = em.dram_tmp("gate_ag", [8 * 1408, 4096], BF16)
    up_ag = em.dram_tmp("up_ag", [8 * 1408, 4096], BF16)
    down_ag = em.dram_tmp("down_ag", [8 * 11264, 512], BF16)
    qkvT = em.dram_tmp("qkvT", [2, 5, 128, 4096], BF16)
    vout = em.dram_tmp("vout", [2, 4096, 128], BF16)
    ctxT = em.dram_tmp("ctxT", [2, 512, 4096], BF16)
    rs1_in = em.dram_tmp("rs1_in", [4, 4096, 1024], BF16)
    rs1_out = em.dram_tmp("rs1_out", [4096, 1024], BF16)
    resid2 = em.dram_tmp("resid2", [H, TC], F32)
    xln2 = em.dram_tmp("xln2", [H, TC], BF16)
    hp_hbm = em.dram_tmp("hp", [88, 128, 1024], BF16)
    resid3 = em.dram_tmp("resid3", [H, TC], F32)
    xln3 = em.dram_tmp("xln3", [H, TC], BF16)
    xg3 = em.dram_tmp("xg3", [4 * H, TC], BF16)
    gateT = em.dram_tmp("gateT", [4096, 1024], BF16)
    hbc = em.dram_tmp("hbc", [1536, 4099], F32)
    dt_o = em.dram_tmp("dt_o", [16, 4096], F32)
    ynT_loc = em.dram_tmp("ynT_loc", [1024, 4096], BF16)
    rs2_in = em.dram_tmp("rs2_in", [4, 4096, 1024], BF16)
    rs2_out = em.dram_tmp("rs2_out", [4096, 1024], BF16)
    resid4 = em.dram_tmp("resid4", [H, TC], F32)
    xln4 = em.dram_tmp("xln4", [H, TC], BF16)

    emit_ln(em, "ln1", x0, xln1, pmajor=True)
    em.do("gpsimd", lambda e: e.collective_compute(
        "AllGather", ALU.bypass, replica_groups=G4,
        ins=[xln1[:, :]], outs=[xg1[:, :]]),
        wait=[("ln1so", em.count("ln1so"))], inc="ccX", incby=None)
    emit_dequant(em, gate_pk, up_pk, down_pk, gate_sT, up_sT, down_sT,
                 gate_zT, up_zT, down_zT, gate_w, up_w, down_w,
                 ag_outs=(gate_ag, up_ag, down_ag))
    em.do("gpsimd", lambda e: None, wait=[("ccX", em.count("ccX"))])
    em.phase_barrier()

    emit_qkv_rope(em, xg1, wqT, bq, invf, qkvT, vout, ident)
    emit_attention(em, qkvT, vout, ctxT, tri, half, ident)

    def ctx_rows(kc, tq, tch):
        pair, c2 = kc // 4, kc % 4
        return ctxT[pair, 128 * c2:128 * (c2 + 1),
                    1024 * tq + 512 * tch:1024 * tq + 512 * (tch + 1)]
    emit_partial_gemm(em, "dn", wdTs, ctx_rows, rs1_in)
    _cc(em, "ReduceScatter", G4, rs1_in.rearrange("a b c -> (a b) c"),
        rs1_out[:, :], "ccB")
    em.do("gpsimd", lambda e: None, wait=[("ccB", em.count("ccB"))])
    em.phase_barrier()

    xbf_all, xbf_free = em.sb_scope("xbf", [128, 32 * 1024], BF16)
    emit_add_resid_ln(em, "r2", rs1_out, x0, bd, resid2, xbf_all)
    emit_ln_sbuf(em, "l2", xbf_all, xln2)
    em.barrier_on("ccA")

    def _wtile(ag, ii):
        r0 = 128 * ii
        return ag[r0:r0 + 128, :]
    emit_ffn(em, xln2, lambda ii: _wtile(gate_ag, ii), lambda ii: _wtile(up_ag, ii),
             down_ag.rearrange("(r i) c -> r i c", i=11264), hp_hbm,
             resid2, resid3, xbf_all)
    emit_ln_sbuf(em, "l3", xbf_all, xln3)
    xbf_free()
    em.phase_barrier()
    _cc(em, "AllGather", G4, xln3[:, :], xg3[:, :], "ccC")
    em.do("gpsimd", lambda e: None, wait=[("ccC", em.count("ccC"))])
    em.phase_barrier()

    zi, zi_free = em.sb_scope("zi", [128, 3], F32)
    em.do("vector", lambda e: e.memset(zi[:, :], 0.0), inc="zv")
    _zvc = em.count("zv")
    for rr in range(12):
        em.do("sync", lambda e, rr=rr: e.dma_start(
            out=hbc[128 * rr:128 * (rr + 1), 0:3], in_=zi[:, :]),
            wait=[("zv", _zvc)], inc="zs", incby=16)

    def xg3_tile(ts):
        r, t2 = ts // 2, ts % 2
        return xg3[4096 * r:4096 * (r + 1), :].rearrange(
            "(kk p) t -> p kk t", p=128)[:, :, 512 * t2:512 * (t2 + 1)]
    emit_inproj(em, xg3_tile, ipp, gateT, hbc, dt_o, ident)
    emit_ssm(em, hbc, dt_o, gateT, conv_w, conv_b, dtb, alog, Dp, nw,
             None, ident, identf, ynT_loc=ynT_loc)

    def yn_rows(kc, tq, tch):
        return ynT_loc[128 * kc:128 * (kc + 1),
                       1024 * tq + 512 * tch:1024 * tq + 512 * (tch + 1)]
    emit_partial_gemm(em, "op", opTs, yn_rows, rs2_in)
    _cc(em, "ReduceScatter", G4, rs2_in.rearrange("a b c -> (a b) c"),
        rs2_out[:, :], "ccD")
    em.do("gpsimd", lambda e: None, wait=[("ccD", em.count("ccD"))])
    em.phase_barrier()
    zi_free()

    xbf2, xbf2_free = em.sb_scope("xbf2", [128, 32 * 1024], BF16)
    emit_add_resid_ln(em, "r4", rs2_out, resid3, None, resid4, xbf2)
    emit_ln_sbuf(em, "l4", xbf2, xln4)
    xbf2_free()
    emit_l1ffn(em, xln4, routerT, router_b, expert_b, resid4, out_final,
               ident, identf)
    return em


def kernel(**inputs):
    pc, shared = host_prep(inputs)
    em = build_fused()
    in_maps = []
    for c in range(NCORE):
        q = c % 4
        in_maps.append({
            "x0T": pc[c]["x0T"],
            "gate_pk": pc[c]["gate_pk"], "up_pk": pc[c]["up_pk"],
            "down_pk": pc[c]["down_pk"],
            "gate_sT": pc[c]["gate_sT"], "up_sT": pc[c]["up_sT"],
            "down_sT": pc[c]["down_sT"],
            "gate_zT": pc[c]["gate_zT"], "up_zT": pc[c]["up_zT"],
            "down_zT": pc[c]["down_zT"],
            "sel4": shared["sel4"],
            "wqT": pc[c]["wqkvT"], "bq": pc[c]["bqkv"][None, :],
            "invf": shared["invf"][None, :],
            "ident": shared["ident"], "identf": shared["ident_f"],
            "tri": shared["tri_mask"], "half": shared["half_mask"],
            "wdTs": pc[c]["wdTs"], "bd": shared["b_dense"],
            "ipp": pc[c]["ipj_panels"],
            "conv_w": _c(pc[c]["conv_w"].reshape(12, 128, 4).transpose(1, 0, 2)
                         .reshape(128, 48)),
            "conv_b": _c(pc[c]["conv_b"].reshape(12, 128).T),
            "dtb": pc[c]["dt_bias"][:, None], "alog": pc[c]["A_log"][:, None],
            "Dp": pc[c]["D_ssm"][:, None], "nw": pc[c]["norm_w"][None, :],
            "opTs": pc[c]["opTs"],
            "routerT": shared["routerT"], "router_b": shared["router_b"][None, :],
            "expert_b": shared["expert_b"],
        })
    res = _run(em, in_maps)
    out = np.empty((2, TB, H), np.float32)
    for c in range(NCORE):
        b, q = divmod(c, 4)
        out[b, TC * q:TC * (q + 1), :] = res[c]["out_final"][IPERM_H, :].T
    return out


# version 13
# speedup vs baseline: 1.1025x; 1.1025x over previous
import numpy as np
import ml_dtypes
from contextlib import ExitStack
import concourse.bass as bass
import concourse.mybir as mybir
from concourse.bass_utils import run_bass_kernel_spmd

F32 = mybir.dt.float32
BF16 = mybir.dt.bfloat16
U8 = mybir.dt.uint8
I32 = mybir.dt.int32
AF = mybir.ActivationFunctionType
ALU = mybir.AluOpType
AX = mybir.AxisListType
BF = ml_dtypes.bfloat16

H = 4096
TB = 4096
TC = 1024
NCORE = 8
I_FFN = 11008
EPS_LN = 1e-5

ENGINES = ("sync", "vector", "scalar", "tensor", "gpsimd")


class Em:

    def __init__(self, name):
        self.nc = bass.Bass("TRN2", target_bir_lowering=False)
        self.es = ExitStack()
        self.q = {e: [] for e in ENGINES}
        self._sems = {}
        self._names = set()

    def _uname(self, base):
        i = 0
        nm = base
        while nm in self._names:
            i += 1
            nm = f"{base}_{i}"
        self._names.add(nm)
        return nm

    def dram_in(self, name, shape, dtype):
        return self.nc.dram_tensor(name, list(shape), dtype, kind="ExternalInput").ap()

    def dram_out(self, name, shape, dtype):
        return self.nc.dram_tensor(name, list(shape), dtype, kind="ExternalOutput").ap()

    def dram_tmp(self, name, shape, dtype, addr_space="Local"):
        return self.nc.dram_tensor(name, list(shape), dtype, addr_space=addr_space).ap()

    def sb(self, name, shape, dtype=F32):
        return self.es.enter_context(self.nc.sbuf_tensor(self._uname(name), list(shape), dtype))

    def ps(self, name, shape, dtype=F32):
        return self.es.enter_context(self.nc.psum_tensor(self._uname(name), list(shape), dtype))

    def ps_scope(self, name, shape, dtype=F32):
        cm = self.nc.psum_tensor(self._uname(name), list(shape), dtype)
        t = cm.__enter__()
        return t, (lambda cm=cm: cm.__exit__(None, None, None))

    def sb_scope(self, name, shape, dtype=F32):
        cm = self.nc.sbuf_tensor(self._uname(name), list(shape), dtype)
        t = cm.__enter__()
        return t, (lambda cm=cm: cm.__exit__(None, None, None))

    @staticmethod
    def _canon(name):
        if name.endswith("ct"):
            return "_T"
        if name.endswith("cs"):
            return "_S"
        if name.endswith("v"):
            return "_V"
        if name.endswith("gi"):
            return "_GI"
        return name

    def sem(self, name):
        name = self._canon(name)
        if name not in self._sems:
            h = self.es.enter_context(self.nc.semaphore(name))
            self._sems[name] = [h, 0]
        return self._sems[name]

    def count(self, name):
        return self.sem(name)[1]

    def do(self, eng, fn, wait=(), inc=None, incby=1):
        waits = []
        for w in wait:
            if isinstance(w, tuple):
                s = self.sem(w[0])
                waits.append((s[0], w[1]))
            else:
                s = self.sem(w)
                waits.append((s[0], s[1]))
        if inc is not None:
            s = self.sem(inc)
            s[1] += (incby if incby is not None else 1)
            hinc = s[0]
        else:
            hinc = None

        def cmd(engine, fn=fn, waits=waits, hinc=hinc, incby=incby):
            for h, v in waits:
                if v > 0:
                    engine.wait_ge(h, v)
            ins = fn(engine)
            if hinc is not None:
                if ins is None:
                    raise RuntimeError("inc on fn returning no instruction")
                if incby is None:
                    ins.then_inc(hinc)
                else:
                    ins.then_inc(hinc, incby)
        self.q[eng].append(cmd)

    def wait(self, eng, *sems):
        self.do(eng, lambda e: None, wait=sems)

    def phase_barrier(self, exclude=("ccA",)):
        snap = [(nm, s[1]) for nm, s in self._sems.items()
                if s[1] > 0 and nm not in exclude]
        for e in ENGINES:
            self.do(e, lambda _: None, wait=snap)

    def barrier_on(self, *sems):
        snap = [(nm, self.count(nm)) for nm in sems]
        for e in ENGINES:
            self.do(e, lambda _: None, wait=snap)

    def build(self):
        with self.nc.Block() as block:
            for e in ENGINES:
                cmds = self.q[e]
                if not cmds:
                    continue
                getattr(block, e)(lambda engine, cmds=cmds: [c(engine) for c in cmds])
        self.es.close()
        return self.nc


def _run(em, in_maps):
    nc = em.build()
    res = run_bass_kernel_spmd(nc, in_maps, core_ids=list(range(NCORE)))
    return res.results


PERM_H = np.concatenate([np.arange(0, H, 2), np.arange(1, H, 2)])
PERM_I = np.concatenate([np.arange(0, I_FFN, 2), np.arange(1, I_FFN, 2)])
IPERM_H = np.argsort(PERM_H)


def _c(a):
    return np.ascontiguousarray(a)


def host_prep(inp):
    pc = [dict() for _ in range(NCORE)]
    shared = {}

    x0 = np.asarray(inp["L0_attn_hidden_states"])
    for c in range(NCORE):
        b, q = divmod(c, 4)
        pc[c]["x0T"] = _c(x0[b, TC * q:TC * (q + 1), :].T[PERM_H, :])

    gate_pk = np.asarray(inp["L0_ffn_gate_Wq"]).astype(np.uint8)
    up_pk = np.asarray(inp["L0_ffn_up_Wq"]).astype(np.uint8)
    down_pk = np.asarray(inp["L0_ffn_down_Wq"]).astype(np.uint8)
    gate_pkT = gate_pk.T[:, PERM_I]
    up_pkT = up_pk.T[:, PERM_I]
    down_pkT = down_pk[PERM_H, :].T
    gsT = np.asarray(inp["L0_ffn_gate_scale"]).T[:, PERM_I]
    gzT = np.asarray(inp["L0_ffn_gate_zero"]).T[:, PERM_I]
    usT = np.asarray(inp["L0_ffn_up_scale"]).T[:, PERM_I]
    uzT = np.asarray(inp["L0_ffn_up_zero"]).T[:, PERM_I]
    dsT = np.asarray(inp["L0_ffn_down_scale"]).T[:, PERM_H]
    dzT = np.asarray(inp["L0_ffn_down_zero"]).T[:, PERM_H]
    ISL = I_FFN // 8
    HSL = H // 8

    def _pad1408(a):
        out = np.zeros((a.shape[0], 1408), a.dtype)
        out[:, :ISL] = a
        return out
    for c in range(NCORE):
        isl = slice(ISL * c, ISL * (c + 1))
        hsl = slice(HSL * c, HSL * (c + 1))
        pc[c]["gate_pk"] = _pad1408(gate_pkT[:, isl])
        pc[c]["up_pk"] = _pad1408(up_pkT[:, isl])
        pc[c]["down_pk"] = _c(down_pkT[:, hsl])
        pc[c]["gate_sT"] = _pad1408(gsT[:, isl])
        pc[c]["up_sT"] = _pad1408(usT[:, isl])
        pc[c]["down_sT"] = _c(dsT[:, hsl])
        pc[c]["gate_zT"] = _pad1408(gzT[:, isl])
        pc[c]["up_zT"] = _pad1408(uzT[:, isl])
        pc[c]["down_zT"] = _c(dzT[:, hsl])

    wq = np.asarray(inp["L0_attn_w_qkv"])
    bq = np.asarray(inp["L0_attn_b_qkv"])
    for c in range(NCORE):
        q = c % 4
        rows = np.arange(768 * 2 * q, 768 * (2 * q + 2))
        pc[c]["wqkvT"] = _c(wq[rows][:, PERM_H].T)
        pc[c]["bqkv"] = _c(bq[rows])
    sel = np.zeros((4, 128), np.float32)
    for g in range(4):
        sel[g, 32 * g:32 * (g + 1)] = 1.0
    shared["sel4"] = sel
    shared["invf"] = _c((1.0 / (1e6 ** (np.arange(0, 64) / 64.0))).astype(np.float32))

    tri = np.triu(np.full((128, 128), -1e30, np.float32), 1)
    shared["tri_mask"] = tri
    half = np.zeros((128, 64), np.float32)
    half[64:, :] = -1e30
    shared["half_mask"] = half
    shared["ident"] = np.eye(128, dtype=BF)
    shared["ident_f"] = np.eye(128, dtype=np.float32)

    wd = np.asarray(inp["L0_attn_w_dense"])[PERM_H, :]
    wdT = wd.T
    shared["wd_panels"] = _c(wdT.reshape(4096, 32, 128).transpose(1, 0, 2))
    shared["b_dense"] = _c(np.asarray(inp["L0_attn_b_dense"])[PERM_H].reshape(32, 128).T)
    for c in range(NCORE):
        q = c % 4
        pc[c]["wdTs"] = _c(wdT[1024 * q:1024 * (q + 1), :])

    ipw = np.asarray(inp["L1_attn_in_proj_w"])
    I_SSM = 4096
    GN = 1024
    for c in range(NCORE):
        q = c % 4
        hrows = np.arange(64 * 16 * q, 64 * 16 * (q + 1))
        gate_rows = hrows
        xbc_rows = I_SSM + hrows
        b_rows = I_SSM + I_SSM + 256 * q + np.arange(256)
        c_rows = I_SSM + I_SSM + GN + 256 * q + np.arange(256)
        dt_rows = I_SSM + I_SSM + 2 * GN + np.arange(16 * q, 16 * (q + 1))
        rows = np.concatenate([gate_rows, xbc_rows, b_rows, c_rows, dt_rows])
        wslice = ipw[rows][:, PERM_H].T.astype(np.float32)
        wpad = np.zeros((4096, 2688), np.float32)
        wpad[:, :2576] = wslice
        pc[c]["ipj_panels"] = _c(wpad.reshape(4096, 21, 128).transpose(1, 0, 2))
    cw = np.asarray(inp["L1_attn_conv_weight"])
    cb = np.asarray(inp["L1_attn_conv_bias"])
    dtb = np.asarray(inp["L1_attn_dt_bias"])
    alog = np.asarray(inp["L1_attn_A_log"])
    Dp = np.asarray(inp["L1_attn_D"])
    nw = np.asarray(inp["L1_attn_norm_weight"])
    for c in range(NCORE):
        q = c % 4
        hrows = np.arange(64 * 16 * q, 64 * 16 * (q + 1))
        b_rows = 4096 + 256 * q + np.arange(256)
        c_rows = 4096 + GN + 256 * q + np.arange(256)
        rows = np.concatenate([hrows, b_rows, c_rows])
        pc[c]["conv_w"] = _c(cw[rows])
        pc[c]["conv_b"] = _c(cb[rows])
        pc[c]["dt_bias"] = _c(dtb[16 * q:16 * (q + 1)])
        pc[c]["A_log"] = _c(alog[16 * q:16 * (q + 1)])
        pc[c]["D_ssm"] = _c(Dp[16 * q:16 * (q + 1)])
        pc[c]["norm_w"] = _c(nw[64 * 16 * q:64 * 16 * (q + 1)])

    opw = np.asarray(inp["L1_attn_out_proj_w"])[PERM_H, :]
    opT = opw.T
    shared["opj_panels"] = _c(opT.reshape(4096, 32, 128).transpose(1, 0, 2))
    for c in range(NCORE):
        q = c % 4
        pc[c]["opTs"] = _c(opT[1024 * q:1024 * (q + 1), :])
    shared["routerT"] = _c(np.asarray(inp["L1_ffn_router_weight"])[:, PERM_H].T)
    shared["router_b"] = _c(np.asarray(inp["L1_ffn_router_bias"]))
    shared["expert_b"] = _c(np.asarray(inp["L1_ffn_expert_bias"])[:, PERM_H])

    return pc, shared



def emit_ln(em, tag, src_hbm, xln_hbm, nt=TC, resid_in=None, pmajor=False):
    nc = em.nc
    CH = 512
    scopes = []
    def ssb(n, shp, dt=F32):
        t, f = em.sb_scope(n, shp, dt); scopes.append(f); return t
    sx = [ssb(f"{tag}_x", [128, CH]) for _ in range(2)]
    sbsq = [ssb(f"{tag}_bsq", [128, 2 * CH], BF16) for _ in range(2)]
    ones_b = ssb(f"{tag}_ones", [128, 1], BF16)
    ones_f = ssb(f"{tag}_onesf", [1, 128], F32)
    stat = ssb(f"{tag}_stat", [1, 3 * CH])
    rmr = ssb(f"{tag}_rmr", [1, 2 * CH])
    stmp = [ssb(f"{tag}_tmp", [128, CH]) for _ in range(2)]
    pscopes = [em.ps_scope(f"{tag}_psum", [1, CH]) for _ in range(2)] + \
              [em.ps_scope(f"{tag}_pbc", [128, CH]) for _ in range(2)]
    p_sum = [pscopes[0][0], pscopes[1][0]]
    p_bc = [pscopes[2][0], pscopes[3][0]]
    sxo = [ssb(f"{tag}_xo", [128, CH], BF16) for _ in range(2)]
    ld, cv, cs, ct, so = (f"{tag}{n}" for n in ("ld", "cv", "cs", "ct", "so"))
    em.do("vector", lambda e: e.memset(ones_b[:], 1.0), inc=cv)
    em.do("vector", lambda e: e.memset(ones_f[:], 1.0), inc=cv)
    n_chunk = nt // CH
    for t in range(n_chunk):
        lnmm = [0, 0]
        castm = [0, 0]
        for hh in range(32):
            buf = sx[hh % 2]
            bsq = sbsq[hh % 2]
            em.do("sync", lambda e, t=t, hh=hh, buf=buf: e.dma_start(
                out=buf[:], in_=src_hbm[128 * hh:128 * (hh + 1), CH * t:CH * (t + 1)]),
                wait=[(cv, castm[hh % 2])],
                inc=f"{tag}ld{hh % 2}", incby=16)
            em.do("vector", lambda e, buf=buf, bsq=bsq, hh=hh: e.tensor_copy(
                bsq[:, 0:CH], buf[:]),
                wait=[(f"{tag}ld{hh % 2}", em.count(f"{tag}ld{hh % 2}")),
                      (ct, lnmm[hh % 2]), (cs, em.count(cs))], inc=cv)
            castm[hh % 2] = em.count(cv)
            em.do("scalar", lambda e, bsq=bsq: e.activation(
                bsq[:, CH:2 * CH], bsq[:, 0:CH], AF.Square),
                wait=[(cv, em.count(cv))], inc=cs)
            for half in range(2):
                em.do("tensor", lambda e, hh=hh, half=half, bsq=bsq: e.matmul(
                    p_sum[half][0:1, :], ones_b[:], bsq[:, CH * half:CH * (half + 1)],
                    start=(hh == 0), stop=(hh == 31), skip_group_check=True),
                    wait=([(cs, em.count(cs))] if half == 0 else ()), inc=ct)
            lnmm[hh % 2] = em.count(ct)
        for half in range(2):
            em.do("vector", lambda e, half=half: e.tensor_scalar(
                stat[0:1, CH * half:CH * (half + 1)], p_sum[half][0:1, :],
                1.0 / H, None, ALU.mult),
                wait=[(ct, em.count(ct))], inc=cv)
        em.do("vector", lambda e: e.tensor_tensor(
            stat[0:1, 2 * CH:3 * CH], stat[0:1, 0:CH], stat[0:1, 0:CH], ALU.mult),
            wait=[(cv, em.count(cv))], inc=cv)
        em.do("vector", lambda e: e.tensor_tensor(
            stat[0:1, CH:2 * CH], stat[0:1, CH:2 * CH], stat[0:1, 2 * CH:3 * CH],
            ALU.subtract),
            wait=[(cv, em.count(cv))], inc=cv)
        em.do("vector", lambda e: e.tensor_scalar(
            stat[0:1, CH:2 * CH], stat[0:1, CH:2 * CH], EPS_LN, None, ALU.add),
            wait=[(cv, em.count(cv))], inc=cv)
        em.do("scalar", lambda e: e.activation(
            stat[0:1, CH:2 * CH], stat[0:1, CH:2 * CH], AF.Ln),
            wait=[(cv, em.count(cv))], inc=cs)
        em.do("scalar", lambda e: e.activation(
            rmr[0:1, 0:CH], stat[0:1, CH:2 * CH], AF.Exp, scale=-0.5),
            wait=[(cs, em.count(cs))], inc=cs)
        em.do("vector", lambda e: e.tensor_tensor(
            rmr[0:1, CH:2 * CH], rmr[0:1, 0:CH], stat[0:1, 0:CH], ALU.mult),
            wait=[(cs, em.count(cs))], inc=cv)
        for half in range(2):
            em.do("tensor", lambda e, half=half: e.matmul(
                p_bc[half][:, :], ones_f[0:1, :], rmr[0:1, CH * half:CH * (half + 1)],
                start=True, stop=True, skip_group_check=True),
                wait=[(cv, em.count(cv))], inc=ct)
        p2m = [0, 0]
        for hh in range(32):
            par = hh % 2
            buf = sx[par]
            em.do("sync", lambda e, t=t, hh=hh, buf=buf: e.dma_start(
                out=buf[:], in_=src_hbm[128 * hh:128 * (hh + 1), CH * t:CH * (t + 1)]),
                wait=[(cv, p2m[par]), (ct, em.count(ct))],
                inc=f"{tag}ld{par}", incby=16)
            em.do("vector", lambda e, buf=buf, par=par: e.tensor_tensor(
                stmp[par][:], buf[:], p_bc[0][:, :], ALU.mult),
                wait=[(f"{tag}ld{par}", em.count(f"{tag}ld{par}")),
                      (ct, em.count(ct))], inc=cv)
            em.do("vector", lambda e, par=par: e.tensor_tensor(
                sxo[par][:], stmp[par][:], p_bc[1][:, :], ALU.subtract),
                wait=[(cv, em.count(cv)), (so, em.count(so))], inc=cv)
            p2m[par] = em.count(cv)
            if pmajor:
                oap = xln_hbm[:, nt * hh + CH * t:nt * hh + CH * (t + 1)]
            else:
                oap = xln_hbm[128 * hh:128 * (hh + 1), CH * t:CH * (t + 1)]
            em.do("sync", lambda e, oap=oap, par=par: e.dma_start(
                out=oap, in_=sxo[par][:]),
                wait=[(cv, em.count(cv))], inc=so, incby=16)
    em.wait("sync", so)
    em.phase_barrier()
    for pf in reversed(pscopes):
        pf[1]()
    for f in reversed(scopes):
        f()


def _bcast_check(ap, shape):
    return ap



def emit_dequant(em, gate_pk, up_pk, down_pk, gate_sT, up_sT, down_sT,
                 gate_zT, up_zT, down_zT, gate_w, up_w, down_w,
                 ag_outs=None):
    sel4 = em.dram_in("sel4", [4, 128], F32)
    scopes = []
    def ssb(n, shp, dt=F32):
        t, f = em.sb_scope(n, shp, dt); scopes.append(f); return t
    spk = ssb("dq_pk", [128, 1408], U8)
    svhi = ssb("dq_vhi", [128, 1408], U8)
    svlo = ssb("dq_vlo", [128, 1408], U8)
    srow = ssb("dq_srow", [4, 1408])
    szrow = ssb("dq_zrow", [4, 1408])
    sself = ssb("dq_self", [4, 128], F32)
    sstile = ssb("dq_stile", [128, 1408])
    sztile = ssb("dq_ztile", [128, 1408])
    stmp = ssb("dq_tmp", [128, 1408])
    sw = ssb("dq_w", [128, 1408], BF16)
    zpad = ssb("dq_zpad", [32, 512], BF16)
    p_s, ps_free = em.ps_scope("dq_ps", [128, 512])
    dl, dv, ds, dtn, dso = ("dqld", "dqv", "dqs", "dqt", "dqso")
    em.do("sync", lambda e: e.dma_start(out=sself[:, :], in_=sel4), inc=dl, incby=16)
    em.do("vector", lambda e: e.memset(zpad[:, :], 0.0), inc=dv)
    _zpv = em.count(dv)
    for cc in range(8):
        em.do("sync", lambda e, cc=cc: e.dma_start(
            out=down_w[1408 * cc + 1376:1408 * cc + 1408, :], in_=zpad[:, :]),
            wait=[(dv, _zpv)], inc=dso, incby=16)

    def dq_matrix(pk_hbm, sT_hbm, zT_hbm, out_hbm, njt, ncol, ngrp,
                  tile_major=False):
        for jj in range(njt):
            g0 = 4 * jj
            em.do("sync", lambda e, jj=jj, pk_hbm=pk_hbm, ncol=ncol: e.dma_start(
                out=spk[:, 0:ncol], in_=pk_hbm[128 * jj:128 * (jj + 1), :]),
                wait=[(dv, em.count(dv))], inc=dl, incby=16)
            em.do("sync", lambda e, g0=g0, sT_hbm=sT_hbm, ncol=ncol: e.dma_start(
                out=srow[:, 0:ncol], in_=sT_hbm[g0:g0 + 4, :]),
                wait=[(dv, em.count(dv)), (dtn, em.count(dtn))], inc=dl, incby=16)
            em.do("sync", lambda e, g0=g0, zT_hbm=zT_hbm, ncol=ncol: e.dma_start(
                out=szrow[:, 0:ncol], in_=zT_hbm[g0:g0 + 4, :]),
                wait=[(dv, em.count(dv)), (dtn, em.count(dtn))], inc=dl, incby=16)
            for (row, tile) in ((srow, sstile), (szrow, sztile)):
                for n0 in range(0, ncol, 512):
                    n1 = min(n0 + 512, ncol)
                    em.do("tensor", lambda e, n0=n0, n1=n1, row=row: e.matmul(
                        p_s[:, 0:n1 - n0], sself[:, :], row[:, n0:n1],
                        start=True, stop=True, skip_group_check=True),
                        wait=[(dl, em.count(dl)), (dv, em.count(dv))], inc=dtn)
                    em.do("vector", lambda e, n0=n0, n1=n1, tile=tile: e.tensor_copy(
                        tile[:, n0:n1], p_s[:, 0:n1 - n0]),
                        wait=[(dtn, em.count(dtn))], inc=dv)
            em.do("vector", lambda e, ncol=ncol: e.tensor_single_scalar(
                svhi[:, 0:ncol], spk[:, 0:ncol], 4, ALU.logical_shift_right),
                wait=[(dl, em.count(dl))], inc=dv)
            em.do("vector", lambda e, ncol=ncol: e.tensor_single_scalar(
                svlo[:, 0:ncol], spk[:, 0:ncol], 15, ALU.bitwise_and),
                wait=[(dl, em.count(dl))], inc=dv)
            for (vv, rowbase) in ((svhi, 0), (svlo, njt)):
                em.do("vector", lambda e, vv=vv, ncol=ncol: e.tensor_tensor(
                    stmp[:, 0:ncol], vv[:, 0:ncol], sztile[:, 0:ncol], ALU.subtract),
                    wait=[(dv, em.count(dv)), (dso, em.count(dso))], inc=dv)
                em.do("vector", lambda e, ncol=ncol: e.tensor_tensor(
                    sw[:, 0:ncol], stmp[:, 0:ncol], sstile[:, 0:ncol], ALU.mult),
                    wait=[(dv, em.count(dv)), (dso, em.count(dso))], inc=dv)
                if tile_major:
                    bb = rowbase + jj
                    for s in range(11):
                        em.do("sync", lambda e, bb=bb, s=s, out_hbm=out_hbm:
                              e.dma_start(
                            out=out_hbm[128 * s:128 * (s + 1),
                                        128 * bb:128 * (bb + 1)],
                            in_=sw[:, 128 * s:128 * (s + 1)]),
                            wait=[(dv, em.count(dv))], inc=dso, incby=16)
                else:
                    q0 = 128 * (rowbase + jj)
                    c0, c1 = q0 // 1376, (q0 + 127) // 1376
                    runs = ([(q0, 128)] if c0 == c1 else
                            [(q0, 1376 * c1 - q0), (1376 * c1, q0 + 128 - 1376 * c1)])
                    for (qa, ln) in runs:
                        qq = qa + 32 * (qa // 1376)
                        em.do("sync", lambda e, qa=qa, ln=ln, qq=qq, q0=q0,
                              out_hbm=out_hbm, ncol=ncol: e.dma_start(
                            out=out_hbm[qq:qq + ln, :],
                            in_=sw[qa - q0:qa - q0 + ln, 0:ncol]),
                            wait=[(dv, em.count(dv))], inc=dso, incby=16)

    def _ag(src, dst):
        if ag_outs is None:
            return
        em.do("gpsimd", lambda e, src=src, dst=dst: e.collective_compute(
            "AllGather", ALU.bypass, replica_groups=[list(range(8))],
            ins=[src[:, :]], outs=[dst[:, :]]),
            wait=[(dso, em.count(dso))], inc="ccA", incby=None)

    dq_matrix(gate_pk, gate_sT, gate_zT, gate_w, 16, 1408, 64, tile_major=True)
    if ag_outs is not None:
        _ag(gate_w, ag_outs[0])
    dq_matrix(up_pk, up_sT, up_zT, up_w, 16, 1408, 64, tile_major=True)
    if ag_outs is not None:
        _ag(up_w, ag_outs[1])
    dq_matrix(down_pk, down_sT, down_zT, down_w, 43, 512, 172)
    if ag_outs is not None:
        _ag(down_w, ag_outs[2])
    em.wait("sync", dso)
    em.phase_barrier()
    ps_free()
    for f in reversed(scopes):
        f()


PI = float(np.pi)


def emit_qkv_rope(em, xg1p, wqT, bq, invf, qkvT, vout, ident):
    scopes = []
    def ssb(name, shape, dtype=F32):
        t, f = em.sb_scope(name, shape, dtype)
        scopes.append(f)
        return t
    swq = ssb("qk_w", [128, 32 * 1536], BF16)
    sxg = ssb("qk_xg", [128, 32 * 1024], BF16)
    sbq = ssb("qk_bq", [1, 1536], BF16)
    ones_b = ssb("qk_ones", [1, 128], BF16)
    stv = ssb("qk_tv", [1, 128])
    sinv = ssb("qk_inv", [1, 64])
    cos_all = ssb("qk_cos", [128, 2048])
    sin_all = ssb("qk_sin", [128, 2048])
    sfr = ssb("qk_fr", [128, 64])
    sk1 = ssb("qk_k1", [128, 64], I32)
    sk2 = ssb("qk_k2", [128, 64])
    scmp = ssb("qk_cmp", [128, 64])
    rope = ssb("qk_rope", [128, 5 * 128], BF16)
    t1 = ssb("qk_t1", [128, 64])
    t2 = ssb("qk_t2", [128, 64])
    vstage = ssb("qk_vst", [128, 128], BF16)
    pscopes = []
    def pps(name, shape, dtype=F32):
        t, f = em.ps_scope(name, shape, dtype)
        pscopes.append(f)
        return t
    p_fr = pps("qk_pfr", [128, 64])
    p_q = pps("qk_pq", [128, 512])
    p_kv = pps("qk_pkv", [128, 256])
    p_tr = pps("qk_ptr", [128, 128], BF16)
    sident = ssb("qk_id", [128, 128], BF16)
    gl, ld, cv, cs, ct, so = ("qkg", "qkld", "qkv", "qkcs", "qkct", "qkso")
    mm_mark = [0, 0]
    em.do("sync", lambda e: e.dma_start(out=sident[:, :], in_=ident), inc=ld, incby=16)

    em.do("gpsimd", lambda e: e.dma_start(
        out=swq.ap().rearrange("p (kk c) -> p kk c", c=1536),
        in_=wqT.rearrange("(kk p) c -> p kk c", p=128)), inc=gl, incby=16)
    em.do("gpsimd", lambda e: e.dma_start(out=sbq[:, :], in_=bq), inc=gl, incby=16)
    em.do("sync", lambda e: e.dma_start(out=sinv[:, :], in_=invf), inc=ld, incby=16)
    em.do("vector", lambda e: e.memset(ones_b[:], 1.0), inc=cv)
    for tt in range(32):
        em.do("gpsimd", lambda e, tt=tt: e.iota(
            stv[:, :], [[1, 128]], base=128 * tt, channel_multiplier=0,
            allow_small_or_imprecise_dtypes=True),
            wait=[(ct, em.count(ct))], inc="qkgi")
        em.do("tensor", lambda e: e.matmul(
            p_fr[:, :], stv[0:1, :], sinv[0:1, :],
            start=True, stop=True, skip_group_check=True),
            wait=[(gl, em.count(gl)), ("qkgi", em.count("qkgi")), (ld, em.count(ld)),
                  (cv, em.count(cv))], inc=ct)
        em.do("vector", lambda e: e.tensor_scalar(
            sk2[:, :], p_fr[:, :], 1.0 / (2 * PI), None, ALU.mult),
            wait=[(ct, em.count(ct))], inc=cv)
        em.do("vector", lambda e: e.tensor_copy(sk1[:, :], sk2[:, :]),
            wait=[(cv, em.count(cv))], inc=cv)
        em.do("vector", lambda e: e.tensor_copy(sk2[:, :], sk1[:, :]),
            wait=[(cv, em.count(cv))], inc=cv)
        em.do("vector", lambda e: e.scalar_tensor_tensor(
            sfr[:, :], sk2[:, :], -2 * PI, p_fr[:, :], ALU.mult, ALU.add),
            wait=[(cv, em.count(cv))], inc=cv)
        for _ in range(2):
            em.do("vector", lambda e: e.tensor_scalar(
                scmp[:, :], sfr[:, :], PI, None, ALU.is_gt),
                wait=[(cv, em.count(cv))], inc=cv)
            em.do("vector", lambda e: e.scalar_tensor_tensor(
                sfr[:, :], scmp[:, :], -2 * PI, sfr[:, :], ALU.mult, ALU.add),
                wait=[(cv, em.count(cv))], inc=cv)
            em.do("vector", lambda e: e.tensor_scalar(
                scmp[:, :], sfr[:, :], -PI, None, ALU.is_lt),
                wait=[(cv, em.count(cv))], inc=cv)
            em.do("vector", lambda e: e.scalar_tensor_tensor(
                sfr[:, :], scmp[:, :], 2 * PI, sfr[:, :], ALU.mult, ALU.add),
                wait=[(cv, em.count(cv))], inc=cv)
        em.do("scalar", lambda e, tt=tt: e.activation(
            sin_all[:, 64 * tt:64 * (tt + 1)], sfr[:, :], AF.Sin),
            wait=[(cv, em.count(cv))], inc=cs)
        em.do("scalar", lambda e: e.activation(t1[:, :], sfr[:, :], AF.Abs),
            wait=[(cv, em.count(cv))], inc=cs)
        em.do("vector", lambda e: e.tensor_scalar(
            t2[:, :], t1[:, :], -1.0, PI / 2, ALU.mult, ALU.add),
            wait=[(cs, em.count(cs))], inc=cv)
        em.do("scalar", lambda e, tt=tt: e.activation(
            cos_all[:, 64 * tt:64 * (tt + 1)], t2[:, :], AF.Sin),
            wait=[(cv, em.count(cv))], inc=cs)

    p_q2 = [p_q, pps("qk_pq2", [128, 512])]
    p_kv2 = [p_kv, pps("qk_pkv2", [128, 256])]
    rope2 = [rope, ssb("qk_rope2", [128, 5 * 128], BF16)]
    tiles = [(r, p, t2) for r in range(4) for p in range(2) for t2 in range(8)]
    rp_mark = [0, 0]
    ts_mark = [0, 0]
    st_mark = [0, 0]

    def emit_mm(idx):
        r, p, t2 = tiles[idx]
        par = idx % 2
        if p == 0 and t2 == 0:
            em.do("sync", lambda e, r=r: e.dma_start(
                out=sxg.ap(), in_=xg1p[128 * r:128 * (r + 1), :]),
                wait=[(ct, em.count(ct))], inc="qksl", incby=16)
        slv = em.count("qksl")
        for kk in range(32):
            lap = sxg[:, 1024 * kk + 128 * t2:1024 * kk + 128 * (t2 + 1)]
            em.do("tensor", lambda e, lap=lap, kk=kk, p=p, par=par: e.matmul(
                p_q2[par][:, :], lap,
                swq[:, 1536 * kk + 768 * p:1536 * kk + 768 * p + 512],
                start=(kk == 0), stop=False, skip_group_check=True),
                wait=([("qksl", slv), (gl, em.count(gl)), (cv, rp_mark[par]),
                       (cs, ts_mark[par])] if kk == 0 else ()), inc=ct)
            em.do("tensor", lambda e, lap=lap, kk=kk, p=p, par=par: e.matmul(
                p_kv2[par][:, :], lap,
                swq[:, 1536 * kk + 768 * p + 512:1536 * kk + 768 * p + 768],
                start=(kk == 0), stop=False, skip_group_check=True), inc=ct)
        em.do("tensor", lambda e, p=p, par=par: e.matmul(
            p_q2[par][:, :], ones_b[:, :], sbq[0:1, 768 * p:768 * p + 512],
            start=False, stop=True, skip_group_check=True), inc=ct)
        em.do("tensor", lambda e, p=p, par=par: e.matmul(
            p_kv2[par][:, :], ones_b[:, :], sbq[0:1, 768 * p + 512:768 * p + 768],
            start=False, stop=True, skip_group_check=True), inc=ct)
        mm_mark[par] = em.count(ct)

    def emit_rope(idx):
        r, p, u2 = tiles[idx]
        tt = 8 * r + u2
        par = idx % 2
        mmv = mm_mark[par]
        for s in range(5):
            ps_ = p_q2[par] if s < 4 else p_kv2[par]
            off = 128 * s if s < 4 else 0
            cosap = cos_all[:, 64 * tt:64 * (tt + 1)]
            sinap = sin_all[:, 64 * tt:64 * (tt + 1)]
            rp = rope2[par]
            em.do("vector", lambda e, ps_=ps_, off=off, cosap=cosap: e.tensor_tensor(
                t1[:, :], ps_[:, off:off + 64], cosap, ALU.mult),
                wait=[(ct, mmv), (cs, em.count(cs)), (cv, em.count(cv)),
                      (so, st_mark[idx % 2])], inc=cv)
            em.do("vector", lambda e, ps_=ps_, off=off, sinap=sinap: e.tensor_tensor(
                t2[:, :], ps_[:, off + 64:off + 128], sinap, ALU.mult),
                wait=[(cv, em.count(cv))], inc=cv)
            em.do("vector", lambda e, s=s, rp=rp: e.tensor_tensor(
                rp[:, 128 * s:128 * s + 64], t1[:, :], t2[:, :], ALU.subtract),
                wait=[(cv, em.count(cv))], inc=cv)
            em.do("vector", lambda e, ps_=ps_, off=off, cosap=cosap: e.tensor_tensor(
                t1[:, :], ps_[:, off + 64:off + 128], cosap, ALU.mult),
                wait=[(cv, em.count(cv))], inc=cv)
            em.do("vector", lambda e, ps_=ps_, off=off, sinap=sinap: e.tensor_tensor(
                t2[:, :], ps_[:, off:off + 64], sinap, ALU.mult),
                wait=[(cv, em.count(cv))], inc=cv)
            em.do("vector", lambda e, s=s, rp=rp: e.tensor_tensor(
                rp[:, 128 * s + 64:128 * (s + 1)], t1[:, :], t2[:, :], ALU.add),
                wait=[(cv, em.count(cv))], inc=cv)
        rp_mark[par] = em.count(cv)
        em.do("scalar", lambda e, par=par: e.activation(
            vstage[:, :], p_kv2[par][:, 128:256], AF.Copy),
            wait=[(ct, mmv), (so, em.count(so))], inc=cs)
        em.do("sync", lambda e, p=p, tt=tt: e.dma_start(
            out=vout[p, 128 * tt:128 * (tt + 1), :], in_=vstage[:, :]),
            wait=[(cs, em.count(cs))], inc=so, incby=16)

    def emit_tr(idx):
        r, p, u2 = tiles[idx]
        tt = 8 * r + u2
        par = idx % 2
        rp = rope2[par]
        for s in range(5):
            em.do("tensor", lambda e, s=s, rp=rp: e.transpose(
                p_tr[:, :], rp[:, 128 * s:128 * (s + 1)], sident[:, :]),
                wait=[(cv, rp_mark[par]), (cs, em.count(cs))], inc=ct)
            em.do("scalar", lambda e, s=s, rp=rp: e.activation(
                rp[:, 128 * s:128 * (s + 1)], p_tr[:, :], AF.Copy),
                wait=[(ct, em.count(ct))], inc=cs)
        ts_mark[par] = em.count(cs)
        em.do("sync", lambda e, p=p, tt=tt, rp=rp: e.dma_start(
            out=qkvT[p].rearrange("s d t -> d s t")[:, :, 128 * tt:128 * (tt + 1)],
            in_=rp.ap().rearrange("d (s t) -> d s t", t=128)),
            wait=[(cs, em.count(cs))], inc=so, incby=16)
        st_mark[par] = em.count(so)

    for idx in range(len(tiles) + 1):
        if idx < len(tiles):
            emit_mm(idx)
        if idx >= 1:
            emit_rope(idx - 1)
            emit_tr(idx - 1)
    em.wait("sync", so)
    em.phase_barrier()
    for f in reversed(scopes + pscopes):
        f()


def attn_regions(a):
    js = list(range(7, 2 * a - 15, 8))
    va = len(js)
    ks = 128 * max(0, a - 8)
    ke = 128 * (a + 1)
    L = ke - ks
    dead = 64 if a >= 8 else 0
    Wp = L - dead - 128
    p1 = min(512, Wp)
    p2 = Wp - p1
    need_half = (a >= 8) and ((2 * a - 14) % 8 != 0)
    return js, va, ks, ke, L, dead, Wp, p1, p2, need_half


def emit_attention(em, qkvT, vin, ctxT, tri_mask, half_mask, ident):
    scopes = []
    def asb(n, shp, dt=F32):
        t, f = em.sb_scope(n, shp, dt); scopes.append(f); return t
    skt = asb("at_kt", [128, 4096], BF16)
    sq = asb("at_qt", [128, 4096], BF16)
    sv = asb("at_v", [128, 32 * 128], BF16)
    ssq = asb("at_sq", [128, 4096], BF16)
    snrm = asb("at_nrm", [1, 4096])
    sm1 = asb("at_m1", [1, 8])
    ones_b = asb("at_ob", [128, 1], BF16)
    ones_f = asb("at_of", [1, 128])
    stri = asb("at_tri", [128, 128])
    shalf = asb("at_half", [128, 64])
    sident = asb("at_id", [128, 128], BF16)
    nb = asb("at_nb", [128, 1])
    P = [asb("at_P", [128, 2048], BF16) for _ in range(2)]
    acc = [asb("at_acc", [128, 4]) for _ in range(2)]
    ssum = [asb("at_ssum", [128, 1]) for _ in range(2)]
    rs = [asb("at_rs", [128, 1]) for _ in range(2)]
    pt_sb = [asb("at_pt", [128, 128], BF16) for _ in range(2)]
    svg = asb("at_vg", [128, 3 * 128], BF16)
    ctx_sb = asb("at_ctx", [128, 4096], BF16)
    psc, psc_free = zip(*[em.ps_scope("at_psc", [128, 1536]) for _ in range(2)])
    p_tr2, ptr_free = em.ps_scope("at_ptr", [128, 256], BF16)
    p_tr = [p_tr2[:, 0:128], p_tr2[:, 128:256]]
    pctx, pctx_free = em.ps_scope("at_pctx", [128, 128])
    ld, cv, cs, ct, so = ("atld", "atv", "atcs", "atct", "atso")
    em.do("sync", lambda e: e.dma_start(out=stri[:, :], in_=tri_mask), inc=ld, incby=16)
    em.do("sync", lambda e: e.dma_start(out=shalf[:, :], in_=half_mask), inc=ld, incby=16)
    em.do("sync", lambda e: e.dma_start(out=sident[:, :], in_=ident), inc=ld, incby=16)
    em.do("vector", lambda e: e.memset(ones_b[:], 1.0), inc=cv)
    em.do("vector", lambda e: e.memset(ones_f[:], 1.0), inc=cv)

    def norms_max(src_bf, out_idx):
        em.do("scalar", lambda e: e.activation(ssq[:, :], src_bf[:, :], AF.Square),
              wait=[(ld, em.count(ld)), (cv, em.count(cv)), (ct, em.count(ct))], inc=cs)
        for ch in range(8):
            em.do("tensor", lambda e, ch=ch: e.matmul(
                psc[0][0:1, 0:512], ones_b[:, :], ssq[:, 512 * ch:512 * (ch + 1)],
                start=True, stop=True, skip_group_check=True),
                wait=[(cs, em.count(cs)), (cv, em.count(cv))], inc=ct)
            em.do("vector", lambda e, ch=ch: e.tensor_copy(
                snrm[0:1, 512 * ch:512 * (ch + 1)], psc[0][0:1, 0:512]),
                wait=[(ct, em.count(ct))], inc=cv)
        em.do("vector", lambda e, out_idx=out_idx: e.reduce_max(
            sm1[0:1, out_idx:out_idx + 1], snrm[0:1, :], axis=AX.X),
            wait=[(cv, em.count(cv))], inc=cv)

    mask_mark = [0, 0]; exp_mark = [0, 0]; scl_mark = [0, 0]
    tr_mark = [0, 0]; sc_mark = [0, 0]; red_mark = [0, 0]
    ctx_evac_mark = [0]

    def emit_scores(a, par):
        js, va, ks, ke, L, dead, Wp, p1, p2, need_half = attn_regions(a)
        qsl = sq[:, 128 * a:128 * (a + 1)]
        mwait = [(cv, mask_mark[par]), (cs, exp_mark[par]), (ld, em.count(ld))]
        if va:
            vap = skt[:, 448:448 + 512 * va].rearrange(
                "p (m x) -> p m x", x=512)[:, :, 0:64]
            em.do("tensor", lambda e, qsl=qsl, vap=vap, va=va, par=par: e.matmul(
                psc[par][:, 0:64 * va], qsl, vap, start=True, stop=True,
                skip_group_check=True), wait=mwait, inc=ct)
            mwait = ()
        em.do("tensor", lambda e, qsl=qsl, ke=ke, par=par: e.matmul(
            psc[par][:, 384:512], qsl, skt[:, ke - 128:ke],
            start=True, stop=True, skip_group_check=True), wait=mwait, inc=ct)
        mwait = ()
        if p1:
            em.do("tensor", lambda e, qsl=qsl, ks=ks, dead=dead, p1=p1, par=par: e.matmul(
                psc[par][:, 512:512 + p1], qsl, skt[:, ks + dead:ks + dead + p1],
                start=True, stop=True, skip_group_check=True), wait=mwait, inc=ct)
        if p2:
            em.do("tensor", lambda e, qsl=qsl, ks=ks, dead=dead, p1=p1, p2=p2,
                  par=par: e.matmul(
                psc[par][:, 1024:1024 + p2], qsl,
                skt[:, ks + dead + p1:ks + dead + p1 + p2],
                start=True, stop=True, skip_group_check=True), wait=mwait, inc=ct)
        sc_mark[par] = em.count(ct)

    def emit_softmax(a, par):
        js, va, ks, ke, L, dead, Wp, p1, p2, need_half = attn_regions(a)
        em.do("vector", lambda e, par=par: e.tensor_tensor(
            psc[par][:, 384:512], psc[par][:, 384:512], stri[:, :], ALU.add),
            wait=[(ct, sc_mark[par])], inc=cv)
        if need_half:
            em.do("vector", lambda e, par=par: e.tensor_tensor(
                psc[par][:, 512:576], psc[par][:, 512:576], shalf[:, :], ALU.add),
                wait=[(cv, em.count(cv))], inc=cv)
        mask_mark[par] = em.count(cv)
        if dead:
            em.do("vector", lambda e, dead=dead, par=par: e.memset(
                P[par][:, 512:512 + dead], 0.0),
                wait=[(ct, tr_mark[par]), (cv, em.count(cv))], inc=cv)
        regions = []
        if va:
            regions.append((0, 0, 64 * va))
        if p1:
            regions.append((512, 512 + dead, p1))
        if p2:
            regions.append((1024, 512 + dead + p1, p2))
        regions.append((384, 512 + L - 128, 128))
        nacc = 0
        for (src_off, dst_off, w) in regions:
            em.do("scalar", lambda e, src_off=src_off, dst_off=dst_off, w=w,
                  nacc=nacc, par=par: e.activation(
                P[par][:, dst_off:dst_off + w], psc[par][:, src_off:src_off + w],
                AF.Exp, bias=nb[:, 0:1], scale=1.0 / 128.0,
                accum_out=acc[par][:, nacc:nacc + 1]),
                wait=[(cv, mask_mark[par]), (ct, tr_mark[par]),
                      (cv, red_mark[par])], inc=cs)
            nacc += 1
        exp_mark[par] = em.count(cs)
        em.do("vector", lambda e, nacc=nacc, par=par: e.reduce_sum(
            ssum[par][:, 0:1], acc[par][:, 0:nacc], axis=AX.X),
            wait=[(cs, exp_mark[par]), (cv, scl_mark[par])], inc=cv)
        red_mark[par] = em.count(cv)
        em.do("vector", lambda e, par=par: e.reciprocal(rs[par][:, 0:1],
              ssum[par][:, 0:1]), wait=[(cv, em.count(cv))], inc=cv)
        if va:
            em.do("vector", lambda e, va=va, par=par: e.tensor_scalar(
                P[par][:, 0:64 * va], P[par][:, 0:64 * va], rs[par][:, 0:1],
                None, ALU.mult),
                wait=[(cv, em.count(cv))], inc=cv)
        em.do("vector", lambda e, L=L, par=par: e.tensor_scalar(
            P[par][:, 512:512 + L], P[par][:, 512:512 + L], rs[par][:, 0:1],
            None, ALU.mult),
            wait=[(cv, em.count(cv))], inc=cv)
        scl_mark[par] = em.count(cv)

    def emit_pv(a, par):
        js, va, ks, ke, L, dead, Wp, p1, p2, need_half = attn_regions(a)
        chunks = []
        coff = 0
        while coff < 64 * va:
            w = min(128, 64 * va - coff)
            chunks.append(("v", coff, w))
            coff += w
        for c in range(L // 128):
            chunks.append(("l", 512 + 128 * c, 128))
        total_mm = len(chunks)
        cp_mark = [("atcs", em.count(cs)), ("atv", em.count(cv))]
        mmi = 0
        for ci, (kind, poff, w) in enumerate(chunks):
            cpar = ci % 2
            em.do("tensor", lambda e, poff=poff, w=w, par=par, cpar=cpar: e.transpose(
                p_tr[cpar][0:w, :], P[par][:, poff:poff + w], sident[:, :]),
                wait=[(cv, scl_mark[par]), cp_mark[cpar]], inc=ct)
            if cpar == 0:
                em.do("scalar", lambda e, w=w, cpar=cpar: e.activation(
                    pt_sb[cpar][0:w, :], p_tr[cpar][0:w, :], AF.Copy),
                    wait=[(ct, em.count(ct))], inc=cs)
                cp_mark[cpar] = ("atcs", em.count(cs))
            else:
                em.do("vector", lambda e, w=w, cpar=cpar: e.tensor_copy(
                    pt_sb[cpar][0:w, :], p_tr[cpar][0:w, :]),
                    wait=[(ct, em.count(ct))], inc=cv)
                cp_mark[cpar] = ("atv", em.count(cv))
            if kind == "l":
                tt = (poff - 512) // 128 + ks // 128
                em.do("tensor", lambda e, tt=tt, mmi=mmi, total_mm=total_mm,
                      cpar=cpar: e.matmul(
                    pctx[:, :], sv[:, 128 * tt:128 * (tt + 1)], pt_sb[cpar][:, :],
                    start=(mmi == 0), stop=(mmi == total_mm - 1),
                    skip_group_check=True),
                    wait=[cp_mark[cpar]] + ([(cs, ctx_evac_mark[0])]
                                            if mmi == 0 else []), inc=ct)
            else:
                vt = poff // 128
                em.do("tensor", lambda e, vt=vt, w=w, mmi=mmi, total_mm=total_mm,
                      cpar=cpar: e.matmul(
                    pctx[:, :], svg[0:w, 128 * vt:128 * (vt + 1)],
                    pt_sb[cpar][0:w, :],
                    start=(mmi == 0), stop=(mmi == total_mm - 1),
                    skip_group_check=True),
                    wait=[cp_mark[cpar], ("atgl", em.count("atgl"))] +
                         ([(cs, ctx_evac_mark[0])] if mmi == 0 else []), inc=ct)
            mmi += 1
        tr_mark[par] = em.count(ct)
        em.do("scalar", lambda e, a=a: e.activation(
            ctx_sb[:, 128 * a:128 * (a + 1)], pctx[:, :], AF.Copy),
            wait=[(ct, tr_mark[par]), (so, em.count(so))], inc=cs)
        ctx_evac_mark[0] = em.count(cs)

    for p in range(2):
        em.do("sync", lambda e, p=p: e.dma_start(out=skt[:, :], in_=qkvT[p, 4]),
              wait=[(ct, em.count(ct)), (cs, em.count(cs))], inc=ld, incby=16)
        em.do("sync", lambda e, p=p: e.dma_start(
            out=sv.ap().rearrange("q (tt d) -> q tt d", d=128),
            in_=vin[p].rearrange("(tt q) d -> q tt d", q=128)),
            wait=[(ct, em.count(ct)), (cs, em.count(cs))], inc=ld, incby=16)
        for i, j in enumerate((7, 15, 23, 31, 39)):
            em.do("gpsimd", lambda e, i=i, j=j: e.dma_start(
                out=svg[64 * (i % 2):64 * (i % 2) + 64,
                        128 * (i // 2):128 * (i // 2) + 128],
                in_=sv[64:128, 128 * (j // 2):128 * (j // 2) + 128]),
                wait=[(ld, em.count(ld)), (ct, em.count(ct))], inc="atgl", incby=16)
        norms_max(skt, 0)
        for s in range(4):
            em.do("sync", lambda e, p=p, s=s: e.dma_start(out=sq[:, :], in_=qkvT[p, s]),
                  wait=[(ct, em.count(ct)), (cs, em.count(cs))], inc=ld, incby=16)
            norms_max(sq, 1)
            em.do("vector", lambda e: e.tensor_tensor(
                sm1[0:1, 2:3], sm1[0:1, 0:1], sm1[0:1, 1:2], ALU.mult),
                wait=[(cv, em.count(cv))], inc=cv)
            em.do("scalar", lambda e: e.activation(sm1[0:1, 3:4], sm1[0:1, 2:3],
                  AF.Sqrt), wait=[(cv, em.count(cv))], inc=cs)
            em.do("vector", lambda e: e.tensor_scalar(
                sm1[0:1, 4:5], sm1[0:1, 3:4], -1.02 / 128.0, None, ALU.mult),
                wait=[(cs, em.count(cs))], inc=cv)
            em.do("tensor", lambda e: e.matmul(
                psc[0][:, 0:1], ones_f[0:1, :], sm1[0:1, 4:5], start=True, stop=True,
                skip_group_check=True), wait=[(cv, em.count(cv))], inc=ct)
            em.do("vector", lambda e: e.tensor_copy(nb[:, :], psc[0][:, 0:1]),
                  wait=[(ct, em.count(ct)), (cs, em.count(cs))], inc=cv)
            mask_mark[0] = mask_mark[1] = em.count(cv)
            for a in range(33):
                if a < 32:
                    emit_scores(a, a % 2)
                    emit_softmax(a, a % 2)
                if a >= 1:
                    emit_pv(a - 1, (a - 1) % 2)
            em.do("sync", lambda e, p=p, s=s: e.dma_start(
                out=ctxT[p, 128 * s:128 * (s + 1), :], in_=ctx_sb[:, :]),
                wait=[(cs, em.count(cs))], inc=so, incby=16)
    em.wait("sync", so)
    em.phase_barrier()
    pctx_free()
    ptr_free()
    for f in reversed(list(psc_free)):
        f()
    for f in reversed(scopes):
        f()



def emit_partial_gemm(em, tag, wTs, src_rows, rs_in, half_cb=None):
    scopes = []
    def ssb(n, shp, dt=BF16):
        t, f = em.sb_scope(n, shp, dt); scopes.append(f); return t
    wsb = ssb(f"{tag}_w", [128, 8 * 4096])
    rsb = [ssb(f"{tag}_r", [128, 8 * 512]) for _ in range(2)]
    ev = [ssb(f"{tag}_ev", [128, 512]) for _ in range(2)]
    pp, pp_free = zip(*[em.ps_scope(f"{tag}_pp", [128, 512]) for _ in range(8)])
    ld, gl, cv, cs, ct, so = (f"{tag}{n}" for n in ("ld", "gl", "cv", "cs", "ct", "so"))
    em.do("gpsimd", lambda e: e.dma_start(
        out=wsb.ap().rearrange("p (kc c) -> p kc c", c=4096),
        in_=wTs.rearrange("(kc p) c -> p kc c", p=128)), inc=gl, incby=16)
    rm = [0, 0]
    evm = [0, 0]
    tcs = [(tq, tch) for tch in range(2) for tq in range(4)]
    for ti, (tq, tch) in enumerate(tcs):
        tpar = ti % 2
        for kc in range(8):
            em.do("sync", lambda e, kc=kc, tq=tq, tch=tch, tpar=tpar: e.dma_start(
                out=rsb[tpar][:, 512 * kc:512 * (kc + 1)], in_=src_rows(kc, tq, tch)),
                wait=[(ct, rm[tpar])], inc=f"{tag}ld{tpar}", incby=16)
        ldv = em.count(f"{tag}ld{tpar}")
        for hg in range(8):
            hpar = hg % 2
            for kc in range(8):
                for i in range(4):
                    hh = 4 * hg + i
                    em.do("tensor", lambda e, kc=kc, hh=hh, i=i, hpar=hpar,
                          tpar=tpar: e.matmul(
                        pp[4 * hpar + i][:, :],
                        wsb[:, 4096 * kc + 128 * hh:4096 * kc + 128 * (hh + 1)],
                        rsb[tpar][:, 512 * kc:512 * (kc + 1)],
                        start=(kc == 0), stop=(kc == 7), skip_group_check=True),
                        wait=([(f"{tag}ld{tpar}", ldv), (gl, 16),
                               (cv, evm[hpar])] if (kc == 0 and i == 0) else ()),
                        inc=ct)
            mmv = em.count(ct)
            for i in range(4):
                hh = 4 * hg + i
                epar = i % 2
                em.do("vector", lambda e, i=i, hpar=hpar, epar=epar: e.tensor_copy(
                    ev[epar][:, :], pp[4 * hpar + i][:, :]),
                    wait=[(ct, mmv), (so, em.count(so))], inc=cv)
                em.do("sync", lambda e, tq=tq, tch=tch, hh=hh, epar=epar: e.dma_start(
                    out=rs_in[tch, tq, 128 * hh:128 * (hh + 1), :],
                    in_=ev[epar][:, :]),
                    wait=[(cv, em.count(cv))], inc=so, incby=16)
            evm[hpar] = em.count(cv)
        rm[tpar] = em.count(ct)
        if half_cb is not None and ti == 3:
            half_cb(0)
    if half_cb is not None:
        half_cb(1)
    em.wait("sync", so)
    em.phase_barrier()
    for f in reversed(list(pp_free)):
        f()
    for f in reversed(scopes):
        f()


def emit_add_resid_ln(em, tag, rs_out, resid_in, bias, resid_out, xbf_all):
    scopes = []
    def ssb(n, shp, dt=F32):
        t, f = em.sb_scope(n, shp, dt); scopes.append(f); return t
    ra = [ssb(f"{tag}_ra", [128, 512], BF16) for _ in range(2)]
    rb = [ssb(f"{tag}_rb", [128, 512]) for _ in range(2)]
    ev = [ssb(f"{tag}_ev", [128, 512]) for _ in range(2)]
    sbias = ssb(f"{tag}_bias", [128, 32]) if bias is not None else None
    ld, cv, so = (f"{tag}{n}" for n in ("ld", "cv", "so"))
    if bias is not None:
        em.do("sync", lambda e: e.dma_start(out=sbias[:, :], in_=bias),
              inc=f"{tag}lb", incby=16)
    evm = [0, 0]
    for hh in range(32):
        for th in range(2):
            par = (2 * hh + th) % 2
            em.do("sync", lambda e, hh=hh, th=th, par=par: e.dma_start(
                out=ra[par][:, :], in_=rs_out[th, 128 * hh:128 * (hh + 1), :]),
                wait=[(cv, evm[par])], inc=f"{tag}ld{par}", incby=16)
            em.do("sync", lambda e, hh=hh, th=th, par=par: e.dma_start(
                out=rb[par][:, :], in_=resid_in[128 * hh:128 * (hh + 1),
                                                512 * th:512 * (th + 1)]),
                wait=[(cv, evm[par])], inc=f"{tag}ld{par}", incby=16)
            ldv = em.count(f"{tag}ld{par}")
            if bias is not None:
                em.do("vector", lambda e, hh=hh, par=par: e.tensor_scalar(
                    ev[par][:, :], ra[par][:, :], sbias[:, hh:hh + 1], None, ALU.add),
                    wait=[(f"{tag}ld{par}", ldv), (f"{tag}lb", 16),
                          (so, em.count(so))],
                    inc=cv)
                em.do("vector", lambda e, par=par: e.tensor_tensor(
                    ev[par][:, :], ev[par][:, :], rb[par][:, :], ALU.add),
                    wait=[(cv, em.count(cv))], inc=cv)
            else:
                em.do("vector", lambda e, par=par: e.tensor_tensor(
                    ev[par][:, :], ra[par][:, :], rb[par][:, :], ALU.add),
                    wait=[(f"{tag}ld{par}", ldv), (so, em.count(so))], inc=cv)
            em.do("sync", lambda e, hh=hh, th=th, par=par: e.dma_start(
                out=resid_out[128 * hh:128 * (hh + 1), 512 * th:512 * (th + 1)],
                in_=ev[par][:, :]), wait=[(cv, em.count(cv))], inc=so, incby=16)
            em.do("vector", lambda e, hh=hh, th=th, par=par: e.tensor_copy(
                xbf_all[:, 1024 * hh + 512 * th:1024 * hh + 512 * th + 512],
                ev[par][:, :]), wait=[(cv, em.count(cv))], inc=cv)
            evm[par] = em.count(cv)
    em.wait("sync", so)
    em.phase_barrier()
    for f in reversed(scopes):
        f()



def emit_gemm_resid(em, tag, panels, actT, bias, resid_in, resid_out, xbf_all,
                    w_f32=True):
    scopes = []
    def ssb(n, shp, dt=BF16):
        t, f = em.sb_scope(n, shp, dt); scopes.append(f); return t
    act = ssb(f"{tag}_act", [128, 32 * 1024])
    wt = [ssb(f"{tag}_wt", [128, 32 * 128]) for _ in range(4)]
    sbias = ssb(f"{tag}_bias", [128, 32], F32)
    rt = [ssb(f"{tag}_rt", [128, 512], F32) for _ in range(2)]
    ev = ssb(f"{tag}_ev", [128, 512], F32)
    pd, pd_free = zip(*[em.ps_scope(f"{tag}_pd", [128, 512]) for i in range(4)])
    ld, gl, cv, cs, ct, so, la = (f"{tag}{n}" for n in
                                  ("ld", "gl", "cv", "cs", "ct", "so", "la"))
    em.do("sync", lambda e: e.dma_start(
        out=act.ap().rearrange("p (kk t) -> p kk t", t=1024),
        in_=actT.rearrange("(kk p) t -> p kk t", p=128)), inc=la, incby=16)
    em.do("sync", lambda e: e.dma_start(out=sbias[:, :], in_=bias), inc=la, incby=16)
    for hg in range(8):
        for i in range(4):
            if w_f32:
                em.do("gpsimd", lambda e, hg=hg, i=i: e.dma_start(
                    out=wt[i].ap().rearrange("p (kk c) -> p kk c", c=128),
                    in_=panels[4 * hg + i].rearrange("(kk p) c -> p kk c", p=128)),
                    wait=[(ct, em.count(ct))], inc=gl, incby=16)
            else:
                em.do("sync", lambda e, hg=hg, i=i: e.dma_start(
                    out=wt[i].ap().rearrange("p (kk c) -> p kk c", c=128),
                    in_=panels[4 * hg + i].rearrange("(kk p) c -> p kk c", p=128)),
                    wait=[(ct, em.count(ct))], inc=ld, incby=16)
        for th in range(2):
            mwait = [(la, em.count(la)), (ld, em.count(ld)), (gl, em.count(gl)),
                     (cv, em.count(cv))]
            for kk in range(32):
                for i in range(4):
                    em.do("tensor", lambda e, kk=kk, i=i, th=th: e.matmul(
                        pd[i][:, :], wt[i][:, 128 * kk:128 * (kk + 1)],
                        act[:, 1024 * kk + 512 * th:1024 * kk + 512 * th + 512],
                        start=(kk == 0), stop=(kk == 31), skip_group_check=True),
                        wait=(mwait if kk == 0 else ()), inc=ct)
            for i in range(4):
                hh = 4 * hg + i
                buf = rt[i % 2]
                em.do("sync", lambda e, hh=hh, th=th, buf=buf: e.dma_start(
                    out=buf[:, :], in_=resid_in[128 * hh:128 * (hh + 1),
                                             512 * th:512 * (th + 1)]),
                    wait=[(cv, em.count(cv))], inc=ld, incby=16)
                em.do("vector", lambda e, i=i, hh=hh: e.tensor_scalar(
                    ev[:, :], pd[i][:, :], sbias[:, hh:hh + 1], None, ALU.add),
                    wait=[(ct, em.count(ct)), (so, em.count(so)), (ld, em.count(ld))],
                    inc=cv)
                em.do("vector", lambda e, buf=buf: e.tensor_tensor(
                    ev[:, :], ev[:, :], buf[:, :], ALU.add),
                    wait=[(cv, em.count(cv))], inc=cv)
                em.do("sync", lambda e, hh=hh, th=th: e.dma_start(
                    out=resid_out[128 * hh:128 * (hh + 1), 512 * th:512 * (th + 1)],
                    in_=ev[:, :]),
                    wait=[(cv, em.count(cv))], inc=so, incby=16)
                em.do("vector", lambda e, hh=hh, th=th: e.tensor_copy(
                    xbf_all[:, 1024 * hh + 512 * th:1024 * hh + 512 * th + 512],
                    ev[:, :]), wait=[(cv, em.count(cv))], inc=cv)
    em.wait("sync", so)
    em.phase_barrier()
    for f in reversed(list(pd_free)):
        f()
    for f in reversed(scopes):
        f()


def emit_ln_sbuf(em, tag, xbf_all, xln_hbm):
    scopes = []
    def ssb(n, shp, dt=F32):
        t, f = em.sb_scope(n, shp, dt); scopes.append(f); return t
    sq = [ssb(f"{tag}_sq", [128, 512], BF16) for _ in range(2)]
    ones_b = ssb(f"{tag}_ones", [128, 1], BF16)
    ones_f = ssb(f"{tag}_onesf", [1, 128])
    stat = ssb(f"{tag}_stat", [1, 3 * 512])
    rmr = ssb(f"{tag}_rmr", [1, 2 * 512])
    stmp = [ssb(f"{tag}_tmp", [128, 512]) for _ in range(2)]
    sxo = [ssb(f"{tag}_xo", [128, 512], BF16) for _ in range(2)]
    p_sum = [em.ps_scope(f"{tag}_ps", [1, 512]) for _ in range(2)]
    p_bc = [em.ps_scope(f"{tag}_pbc", [128, 512]) for _ in range(2)]
    CH = 512
    ld, cv, cs, ct, so = (f"{tag}{n}" for n in ("ld", "cv", "cs", "ct", "so"))
    em.do("vector", lambda e: e.memset(ones_b[:], 1.0), inc=cv)
    em.do("vector", lambda e: e.memset(ones_f[:], 1.0), inc=cv)
    for t in range(2):
        sqm = [0, 0]
        for hh in range(32):
            par = hh % 2
            xap = xbf_all[:, 1024 * hh + CH * t:1024 * hh + CH * t + CH]
            em.do("scalar", lambda e, xap=xap, par=par: e.activation(
                sq[par][:, :], xap, AF.Square),
                wait=[(cv, em.count(cv)), (ct, sqm[par])], inc=cs)
            em.do("tensor", lambda e, xap=xap, hh=hh: e.matmul(
                p_sum[0][0][0:1, :], ones_b[:], xap,
                start=(hh == 0), stop=(hh == 31), skip_group_check=True),
                wait=[(cs, em.count(cs)), (cv, em.count(cv))], inc=ct)
            em.do("tensor", lambda e, hh=hh, par=par: e.matmul(
                p_sum[1][0][0:1, :], ones_b[:], sq[par][:, :],
                start=(hh == 0), stop=(hh == 31), skip_group_check=True), inc=ct)
            sqm[par] = em.count(ct)
        for half in range(2):
            em.do("vector", lambda e, half=half: e.tensor_scalar(
                stat[0:1, CH * half:CH * (half + 1)], p_sum[half][0][0:1, :],
                1.0 / H, None, ALU.mult),
                wait=[(ct, em.count(ct))], inc=cv)
        em.do("vector", lambda e: e.tensor_tensor(
            stat[0:1, 2 * CH:3 * CH], stat[0:1, 0:CH], stat[0:1, 0:CH], ALU.mult),
            wait=[(cv, em.count(cv))], inc=cv)
        em.do("vector", lambda e: e.tensor_tensor(
            stat[0:1, CH:2 * CH], stat[0:1, CH:2 * CH], stat[0:1, 2 * CH:3 * CH],
            ALU.subtract), wait=[(cv, em.count(cv))], inc=cv)
        em.do("vector", lambda e: e.tensor_scalar(
            stat[0:1, CH:2 * CH], stat[0:1, CH:2 * CH], EPS_LN, None, ALU.add),
            wait=[(cv, em.count(cv))], inc=cv)
        em.do("scalar", lambda e: e.activation(
            stat[0:1, CH:2 * CH], stat[0:1, CH:2 * CH], AF.Ln),
            wait=[(cv, em.count(cv))], inc=cs)
        em.do("scalar", lambda e: e.activation(
            rmr[0:1, 0:CH], stat[0:1, CH:2 * CH], AF.Exp, scale=-0.5),
            wait=[(cs, em.count(cs))], inc=cs)
        em.do("vector", lambda e: e.tensor_tensor(
            rmr[0:1, CH:2 * CH], rmr[0:1, 0:CH], stat[0:1, 0:CH], ALU.mult),
            wait=[(cs, em.count(cs))], inc=cv)
        for half in range(2):
            em.do("tensor", lambda e, half=half: e.matmul(
                p_bc[half][0][:, :], ones_f[0:1, :], rmr[0:1, CH * half:CH * (half + 1)],
                start=True, stop=True, skip_group_check=True),
                wait=[(cv, em.count(cv))], inc=ct)
        for hh in range(32):
            par = hh % 2
            xap = xbf_all[:, 1024 * hh + CH * t:1024 * hh + CH * t + CH]
            em.do("vector", lambda e, xap=xap, par=par: e.tensor_tensor(
                stmp[par][:], xap, p_bc[0][0][:, :], ALU.mult),
                wait=[(ct, em.count(ct))], inc=cv)
            em.do("vector", lambda e, par=par: e.tensor_tensor(
                sxo[par][:], stmp[par][:], p_bc[1][0][:, :], ALU.subtract),
                wait=[(cv, em.count(cv)), (so, em.count(so))], inc=cv)
            em.do("sync", lambda e, t=t, hh=hh, par=par: e.dma_start(
                out=xln_hbm[128 * hh:128 * (hh + 1), CH * t:CH * (t + 1)],
                in_=sxo[par][:]),
                wait=[(cv, em.count(cv))], inc=so, incby=16)
    em.wait("sync", so)
    em.phase_barrier()
    for pf in reversed(p_sum + p_bc):
        pf[1]()
    for f in reversed(scopes):
        f()


def emit_ffn(em, xln2_hbm, gate_tile, up_tile, down_pan, hp_hbm,
             resid2, resid3, xbf3_all):
    scopes = []
    def ssb(n, shp, dt=BF16):
        t, f = em.sb_scope(n, shp, dt); scopes.append(f); return t
    act = ssb("ff_act", [128, 32 * 1024])
    gp = [ssb("ff_gp", [128, 32 * 128]) for _ in range(2)]
    up = [ssb("ff_up", [128, 32 * 128]) for _ in range(2)]
    sig = ssb("ff_sig", [128, 512], F32)
    t2 = ssb("ff_t2", [128, 512], F32)
    hp = [ssb("ff_hp", [128, 512]) for _ in range(4)]
    pg, pg_free = zip(*[em.ps_scope("ff_pg", [128, 512]) for _ in range(4)])
    pu, pu_free = zip(*[em.ps_scope("ff_pu", [128, 512]) for _ in range(4)])
    ld, gl, cv, cs, ct, so = ("ffld", "ffgl", "ffv", "ffcs", "ffct", "ffso")
    em.do("sync", lambda e: e.dma_start(
        out=act.ap().rearrange("p (kk t) -> p kk t", t=1024),
        in_=xln2_hbm.rearrange("(kk p) t -> p kk t", p=128)), inc="ffla", incby=16)
    mm_mark = [0, 0]
    ev_mark = [0, 0]
    ld_mark = [0, 0]
    so_mark = [0, 0, 0, 0]
    for ii in range(88):
        par = ii % 2
        ldp = f"ffld{par}"
        em.do("sync", lambda e, sap=gate_tile(ii), par=par: e.dma_start(
            out=gp[par].ap(), in_=sap),
            wait=[(ct, mm_mark[par])], inc=ldp, incby=16)
        em.do("sync", lambda e, sap=up_tile(ii), par=par: e.dma_start(
            out=up[par].ap(), in_=sap),
            wait=[(ct, mm_mark[par])], inc=ldp, incby=16)
        ld_mark[par] = em.count(ldp)
        for kk in range(32):
            for th in range(2):
                rap = act[:, 1024 * kk + 512 * th:1024 * kk + 512 * th + 512]
                em.do("tensor", lambda e, kk=kk, th=th, rap=rap, par=par: e.matmul(
                    pg[2 * par + th][:, :], gp[par][:, 128 * kk:128 * (kk + 1)],
                    rap, start=(kk == 0), stop=(kk == 31), skip_group_check=True),
                    wait=([(f"ffld{par}", ld_mark[par]), ("ffla", 16),
                           (cv, ev_mark[par])]
                          if (kk == 0 and th == 0) else ()), inc=ct)
            for th in range(2):
                rap = act[:, 1024 * kk + 512 * th:1024 * kk + 512 * th + 512]
                em.do("tensor", lambda e, kk=kk, th=th, rap=rap, par=par: e.matmul(
                    pu[2 * par + th][:, :], up[par][:, 128 * kk:128 * (kk + 1)],
                    rap, start=(kk == 0), stop=(kk == 31), skip_group_check=True),
                    inc=ct)
        mm_mark[par] = em.count(ct)
        for th in range(2):
            em.do("scalar", lambda e, th=th, par=par: e.activation(
                sig[:, :], pg[2 * par + th][:, :], AF.Sigmoid),
                wait=[(ct, mm_mark[par]), (cv, em.count(cv))], inc=cs)
            em.do("vector", lambda e, th=th, par=par: e.tensor_tensor(
                t2[:, :], sig[:, :], pg[2 * par + th][:, :], ALU.mult),
                wait=[(cs, em.count(cs)), (cv, em.count(cv))], inc=cv)
            em.do("vector", lambda e, th=th, par=par: e.tensor_tensor(
                hp[2 * par + th][:, :], t2[:, :], pu[2 * par + th][:, :], ALU.mult),
                wait=[(cv, em.count(cv)), (so, em.count(so))], inc=cv)
            em.do("sync", lambda e, ii=ii, th=th, par=par: e.dma_start(
                out=hp_hbm[ii, :, 512 * th:512 * (th + 1)], in_=hp[2 * par + th][:, :]),
                wait=[(cv, em.count(cv))], inc=so, incby=16)
            so_mark[2 * par + th] = em.count(so)
        ev_mark[par] = em.count(cv)
    em.wait("sync", so)
    em.phase_barrier()
    for f in reversed(list(pg_free) + list(pu_free)):
        f()
    for f in reversed(scopes[1:]):
        f()
    scopes[0]()

    scopes2 = []
    def ssb2(n, shp, dt=BF16):
        t, f = em.sb_scope(n, shp, dt); scopes2.append(f); return t
    NDB = 4
    wdt = [ssb2("ff_wdt", [128, 512]) for _ in range(NDB)]
    hpc = [ssb2("ff_hpc", [128, 1024]) for _ in range(NDB)]
    rt = [ssb2("ff_rt", [128, 512], F32) for _ in range(2)]
    ev = [ssb2("ff_ev", [128, 512], F32) for _ in range(2)]
    pdn, pdn_free = zip(*[em.ps_scope("ff_pdn", [128, 512]) for _ in range(8)])
    mmk = [0] * NDB
    dso_mark = [0, 0]
    for hp8 in range(8):
        ldk = [0] * NDB
        for kk in range(88):
            par = kk % NDB
            ldp = f"ffldd{par}"
            em.do("sync", lambda e, hp8=hp8, kk=kk, par=par: e.dma_start(
                out=wdt[par][:, :], in_=down_pan[hp8, 128 * kk:128 * (kk + 1), :]),
                wait=[(ct, mmk[par])], inc=ldp, incby=16)
            em.do("sync", lambda e, kk=kk, par=par: e.dma_start(
                out=hpc[par][:, :], in_=hp_hbm[kk, :, :]),
                wait=[(ct, mmk[par])], inc=ldp, incby=16)
            ldk[par] = em.count(ldp)
            for i in range(4):
                for th in range(2):
                    em.do("tensor", lambda e, kk=kk, i=i, th=th, par=par: e.matmul(
                        pdn[4 * th + i][:, :], wdt[par][:, 128 * i:128 * (i + 1)],
                        hpc[par][:, 512 * th:512 * (th + 1)],
                        start=(kk == 0), stop=(kk == 87), skip_group_check=True),
                        wait=([(f"ffldd{par}", ldk[par]), (cv, em.count(cv))]
                              if (th == 0 and i == 0) else ()), inc=ct)
            mmk[par] = em.count(ct)
        for th in range(2):
            for i in range(4):
                hh = 4 * hp8 + i
                par = i % 2
                em.do("sync", lambda e, hh=hh, th=th, par=par: e.dma_start(
                    out=rt[par][:, :], in_=resid2[128 * hh:128 * (hh + 1),
                                                  512 * th:512 * (th + 1)]),
                    wait=[(cv, em.count(cv))], inc="ffrt", incby=16)
                em.do("vector", lambda e, i=i, th=th, par=par: e.tensor_tensor(
                    ev[par][:, :], pdn[4 * th + i][:, :], rt[par][:, :], ALU.add),
                    wait=[(ct, em.count(ct)), (so, em.count(so)),
                          ("ffrt", em.count("ffrt"))],
                    inc=cv)
                em.do("sync", lambda e, hh=hh, th=th, par=par: e.dma_start(
                    out=resid3[128 * hh:128 * (hh + 1), 512 * th:512 * (th + 1)],
                    in_=ev[par][:, :]),
                    wait=[(cv, em.count(cv))], inc=so, incby=16)
                dso_mark[par] = em.count(so)
                em.do("vector", lambda e, hh=hh, th=th, par=par: e.tensor_copy(
                    xbf3_all[:, 1024 * hh + 512 * th:1024 * hh + 512 * th + 512],
                    ev[par][:, :]), wait=[(cv, em.count(cv))], inc=cv)
        mmk = [em.count(ct)] * NDB
    em.wait("sync", so)
    em.phase_barrier()
    for f in reversed(list(pdn_free)):
        f()
    for f in reversed(scopes2):
        f()



def emit_inproj(em, xg3_tile, panels, gateT, hbc_out, dt_out, ident):
    scopes = []
    def ssb(n, shp, dt=BF16):
        t, f = em.sb_scope(n, shp, dt); scopes.append(f); return t
    sxg = [ssb("ip_xg", [128, 32 * 512]) for _ in range(2)]
    wt = [ssb("ip_wt", [128, 32 * 128]) for _ in range(2)]
    ev = [ssb("ip_ev", [128, 512], F32) for _ in range(2)]
    evb = [ssb("ip_evb", [128, 512]) for _ in range(2)]
    sident = ssb("ip_id", [128, 128])
    pj, pj_free = zip(*[em.ps_scope("ip_pj", [128, 512]) for _ in range(2)])
    ptr, ptr_free = em.ps_scope("ip_ptr", [128, 128], BF16)
    ld, gl, cv, cs, ct, so = ("ipld", "ipgl", "ipv", "ipcs", "ipct", "ipso")
    em.do("sync", lambda e: e.dma_start(out=sident[:, :], in_=ident), inc="ipid", incby=16)
    wm = [0, 0]
    xm = [0, 0]
    em_ = [0, 0]
    evm_cv = [0, 0]
    evm_cs = [0, 0]
    it = 0
    for ts in range(8):
        xpar = ts % 2
        em.do("sync", lambda e, ts=ts, xpar=xpar: e.dma_start(
            out=sxg[xpar].ap().rearrange("p (kk t) -> p kk t", t=512),
            in_=xg3_tile(ts)),
            wait=[(ct, xm[xpar])], inc=f"ipldx{xpar}", incby=16)
        xldv = em.count(f"ipldx{xpar}")
        for cc in range(21):
            par = it % 2
            wbuf = wt[par]
            em.do("gpsimd", lambda e, cc=cc, wbuf=wbuf: e.dma_start(
                out=wbuf.ap().rearrange("p (kk c) -> p kk c", c=128),
                in_=panels[cc].rearrange("(kk p) c -> p kk c", p=128)),
                wait=[(ct, wm[par])], inc=f"ipgl{par}", incby=16)
            wldv = em.count(f"ipgl{par}")
            for kk in range(32):
                em.do("tensor", lambda e, kk=kk, wbuf=wbuf, xpar=xpar, par=par: e.matmul(
                    pj[par][:, :], wbuf[:, 128 * kk:128 * (kk + 1)],
                    sxg[xpar][:, 512 * kk:512 * (kk + 1)],
                    start=(kk == 0), stop=(kk == 31), skip_group_check=True),
                    wait=([(f"ipldx{xpar}", xldv), (f"ipgl{par}", wldv),
                           (cv, evm_cv[par]), (cs, evm_cs[par]),
                           ("ipid", 16)] if kk == 0 else ()), inc=ct)
            wm[par] = em.count(ct)
            if ts == 7:
                xm[xpar] = em.count(ct)
            mmv = em.count(ct)
            if cc < 8:
                em.do("scalar", lambda e, par=par: e.activation(
                    evb[par][:, :], pj[par][:, :], AF.Copy),
                    wait=[(ct, mmv), (so, em.count(so)), (cs, em.count(cs))], inc=cs)
                for j in range(4):
                    em.do("tensor", lambda e, j=j, par=par: e.transpose(
                        ptr[:, :], evb[par][:, 128 * j:128 * (j + 1)], sident[:, :]),
                        wait=[(cs, em.count(cs))], inc=ct)
                    em.do("scalar", lambda e, j=j, par=par: e.activation(
                        evb[par][:, 128 * j:128 * (j + 1)], ptr[:, :], AF.Copy),
                        wait=[(ct, em.count(ct))], inc=cs)
                    em.do("sync", lambda e, ts=ts, cc=cc, j=j, par=par: e.dma_start(
                        out=gateT[512 * ts + 128 * j:512 * ts + 128 * (j + 1),
                                  128 * cc:128 * (cc + 1)],
                        in_=evb[par][:, 128 * j:128 * (j + 1)]),
                        wait=[(cs, em.count(cs))], inc=so, incby=16)
            elif cc < 20:
                rr = 128 * (cc - 8)
                em.do("vector", lambda e, par=par: e.tensor_copy(
                    ev[par][:, :], pj[par][:, :]),
                    wait=[(ct, mmv), (so, em.count(so))], inc=cv)
                em.do("sync", lambda e, ts=ts, rr=rr, par=par: e.dma_start(
                    out=hbc_out[rr:rr + 128, 3 + 512 * ts:3 + 512 * (ts + 1)],
                    in_=ev[par][:, :]), wait=[(cv, em.count(cv))], inc=so, incby=16)
            else:
                em.do("vector", lambda e, par=par: e.tensor_copy(
                    ev[par][0:16, :], pj[par][0:16, :]),
                    wait=[(ct, mmv), (so, em.count(so))], inc=cv)
                em.do("sync", lambda e, ts=ts, par=par: e.dma_start(
                    out=dt_out[0:16, 512 * ts:512 * (ts + 1)], in_=ev[par][0:16, :]),
                    wait=[(cv, em.count(cv))], inc=so, incby=16)
            evm_cv[par] = em.count(cv)
            evm_cs[par] = em.count(cs)
            it += 1
    em.wait("sync", so)
    em.phase_barrier()
    ptr_free()
    for f in reversed(list(pj_free)):
        f()
    for f in reversed(scopes):
        f()



def emit_ssm(em, hbc_pad, dt_in, gateT, conv_w, conv_b, dtb, alog, Dp, nw,
             yngT, ident, identf, ynT_loc=None, rowscr=None):
    if rowscr is None:
        rowscr = em.dram_tmp("sm_rowscr", [2, 16, 128], F32)
    """Conv + SSD scan + group-RMS gate (head shard, pipelined heads)."""
    NH, NCH = 16, 32
    scopes = []
    def ssb(n, shp, dt=BF16):
        t, f = em.sb_scope(n, shp, dt); scopes.append(f); return t
    xbc = ssb("sm_xbc", [128, 12 * 4096])
    stg = [ssb("sm_stg", [128, 515], F32) for _ in range(2)]
    acc = [ssb("sm_acc", [128, 512], F32) for _ in range(2)]
    scw = ssb("sm_cw", [128, 12 * 4], F32)
    scb = ssb("sm_cb", [128, 12], F32)
    sdt = ssb("sm_dt", [16, 4096], F32)
    sd2 = ssb("sm_d2", [16, 4096], F32)
    cum = ssb("sm_cum", [16, 4096], F32)
    s16 = ssb("sm_s16", [16, 4], F32)
    cumT = ssb("sm_cumT", [128, 32 * 16], F32)
    ncumT = ssb("sm_ncumT", [128, 16], F32)
    dtpT = ssb("sm_dtpT", [128, 32 * 16], F32)
    fT = ssb("sm_fT", [128, 16], F32)
    wT = ssb("sm_wT", [128, 16], F32)
    dtot = ssb("sm_dtot", [128, 16], F32)
    Dbc = ssb("sm_Dbc", [128, 16], F32)
    rowcat = [ssb("sm_rowcat", [1, 2048], F32) for _ in range(2)]
    rrow0 = ssb("sm_rrow0", [1, 16], F32)
    sident = ssb("sm_id", [128, 128])
    sidf = ssb("sm_idf", [128, 128], F32)
    tri01 = ssb("sm_tri", [128, 128])
    xT = ssb("sm_xT", [128, 8 * 128])
    Bt = ssb("sm_Bt", [128, 2 * 128])
    dtx = [ssb("sm_dtx", [128, 64]) for _ in range(2)]
    wdtx = [ssb("sm_wdtx", [128, 64]) for _ in range(2)]
    Gs = [ssb("sm_Gs", [128, 128]) for _ in range(2)]
    Ls = [ssb("sm_Ls", [128, 128], F32) for _ in range(2)]
    Ms = [ssb("sm_Ms", [128, 128]) for _ in range(2)]
    Ssb = ssb("sm_S", [128, 16 * 64], F32)
    Sbf = ssb("sm_Sbf", [128, 16 * 64])
    ysb = ssb("sm_y", [128, 16 * 64], F32)
    yev = [ssb("sm_yev", [128, 64], F32) for _ in range(2)]
    ysq = ssb("sm_ysq", [128, 1024], F32)
    var2 = ssb("sm_var2", [128, 2], F32)
    nwbc = ssb("sm_nwbc", [128, 1024])
    nwrow = ssb("sm_nwrow", [1, 1024], F32)
    gt = ssb("sm_gt", [128, 1024])
    gsil = ssb("sm_gsil", [128, 1024])
    yng = ssb("sm_yng", [128, 1024])
    ytr = ssb("sm_ytr", [128, 8 * 128])
    ones_f = ssb("sm_onef", [1, 128], F32)
    pG, pG_f = em.ps_scope("sm_pG", [128, 128])
    pL, pL_f = zip(*[em.ps_scope("sm_pL", [128, 128]) for _ in range(2)])
    pY, pY_f = em.ps_scope("sm_pY", [128, 128])
    pS, pS_f = em.ps_scope("sm_pS", [128, 64])
    pT2, pT_f = em.ps_scope("sm_pT", [128, 256], BF16)
    pT = [pT2[:, 0:128], pT2[:, 128:256]]
    pTf, pTf_f = em.ps_scope("sm_pTf", [128, 16])
    pR, pR_f = em.ps_scope("sm_pR", [16, 128])
    ld, cv, cs, ct, so, gl = ("smld", "smv", "smcs", "smct", "smso", "smgl")

    em.do("sync", lambda e: e.dma_start(out=sident[:, :], in_=ident), inc=ld, incby=16)
    em.do("sync", lambda e: e.dma_start(out=sidf[:, :], in_=identf), inc=ld, incby=16)
    em.do("sync", lambda e: e.dma_start(out=scw[:, :], in_=conv_w), inc=ld, incby=16)
    em.do("sync", lambda e: e.dma_start(out=scb[:, :], in_=conv_b), inc=ld, incby=16)
    em.do("sync", lambda e: e.dma_start(out=sdt[:, :], in_=dt_in), inc=ld, incby=16)
    em.do("sync", lambda e: e.dma_start(out=s16[:, 0:1], in_=dtb), inc=ld, incby=16)
    em.do("sync", lambda e: e.dma_start(out=s16[:, 1:2], in_=alog), inc=ld, incby=16)
    em.do("sync", lambda e: e.dma_start(out=s16[:, 2:3], in_=Dp), inc=ld, incby=16)
    em.do("sync", lambda e: e.dma_start(out=nwrow[:, :], in_=nw), inc=ld, incby=16)
    em.do("gpsimd", lambda e: e.iota(ysq[:, 0:128], [[1, 128]], base=0,
        channel_multiplier=0, allow_small_or_imprecise_dtypes=True), inc=gl)
    em.do("gpsimd", lambda e: e.iota(ysq[:, 128:256], [[0, 128]], base=0,
        channel_multiplier=1, allow_small_or_imprecise_dtypes=True), inc=gl)
    em.do("vector", lambda e: e.tensor_tensor(tri01[:, :], ysq[:, 0:128],
        ysq[:, 128:256], ALU.is_ge), wait=[(gl, em.count(gl))], inc=cv)
    em.do("vector", lambda e: e.memset(ones_f[:, :], 1.0), inc=cv)

    cvm = [0, 0]
    for it, (ct12, ts) in enumerate([(a, b) for a in range(12) for b in range(8)]):
        par = it % 2
        em.do("sync", lambda e, ct12=ct12, ts=ts, par=par: e.dma_start(
            out=stg[par][:, :], in_=hbc_pad[128 * ct12:128 * (ct12 + 1),
                                            512 * ts:512 * ts + 515]),
            wait=[(cv, cvm[par])], inc=f"smld{par}", incby=16)
        ldv = em.count(f"smld{par}")
        em.do("vector", lambda e, ct12=ct12, par=par: e.tensor_scalar(
            acc[par][:, :], stg[par][:, 0:512], scw[:, 4 * ct12:4 * ct12 + 1],
            scb[:, ct12:ct12 + 1], ALU.mult, ALU.add),
            wait=[(f"smld{par}", ldv), (ld, em.count(ld)), (cs, em.count(cs))], inc=cv)
        for k in range(1, 4):
            em.do("vector", lambda e, ct12=ct12, k=k, par=par: e.scalar_tensor_tensor(
                acc[par][:, :], stg[par][:, k:k + 512],
                scw[:, 4 * ct12 + k:4 * ct12 + k + 1],
                acc[par][:, :], ALU.mult, ALU.add),
                wait=[(cv, em.count(cv))], inc=cv)
        cvm[par] = em.count(cv)
        em.do("scalar", lambda e, ct12=ct12, ts=ts, par=par: e.activation(
            xbc[:, 4096 * ct12 + 512 * ts:4096 * ct12 + 512 * (ts + 1)],
            acc[par][:, :], AF.Silu), wait=[(cv, cvm[par])], inc=cs)

    rcuse_mark = [0, 0]
    cum_ready = [0]
    em.do("vector", lambda e: e.tensor_scalar(
        sdt[:, :], sdt[:, :], s16[:, 0:1], None, ALU.add),
        wait=[(ld, em.count(ld))], inc=cv)
    em.do("scalar", lambda e: e.activation(sd2[:, :], sdt[:, :], AF.Abs),
          wait=[(cv, em.count(cv)), (cs, em.count(cs))], inc=cs)
    em.do("scalar", lambda e: e.activation(sd2[:, :], sd2[:, :], AF.Exp, scale=-1.0),
          wait=[(cs, em.count(cs))], inc=cs)
    em.do("vector", lambda e: e.tensor_scalar(sd2[:, :], sd2[:, :], 1.0, None, ALU.add),
          wait=[(cs, em.count(cs))], inc=cv)
    em.do("scalar", lambda e: e.activation(sd2[:, :], sd2[:, :], AF.Ln),
          wait=[(cv, em.count(cv))], inc=cs)
    em.do("vector", lambda e: e.tensor_scalar(sdt[:, :], sdt[:, :], 0.0, None, ALU.max),
          wait=[(cs, em.count(cs))], inc=cv)
    em.do("vector", lambda e: e.tensor_tensor(sdt[:, :], sd2[:, :], sdt[:, :], ALU.add),
          wait=[(cv, em.count(cv)), (cs, em.count(cs))], inc=cv)
    em.do("scalar", lambda e: e.activation(s16[:, 3:4], s16[:, 1:2], AF.Exp),
          wait=[(ld, em.count(ld))], inc=cs)
    em.do("vector", lambda e: e.tensor_scalar(
        s16[:, 3:4], s16[:, 3:4], -1.0, None, ALU.mult),
        wait=[(cs, em.count(cs))], inc=cv)
    em.do("vector", lambda e: e.tensor_scalar(
        sd2[:, :], sdt[:, :], s16[:, 3:4], None, ALU.mult),
        wait=[(cv, em.count(cv))], inc=cv)
    cur, oth = sd2, cum
    v3 = lambda t: t.ap().rearrange("p (c x) -> p c x", x=128)
    for st_ in (1, 2, 4, 8, 16, 32, 64):
        em.do("vector", lambda e, st_=st_, cur=cur, oth=oth: e.tensor_copy(
            v3(oth)[:, :, 0:st_], v3(cur)[:, :, 0:st_]),
            wait=[(cv, em.count(cv))], inc=cv)
        em.do("vector", lambda e, st_=st_, cur=cur, oth=oth: e.tensor_tensor(
            v3(oth)[:, :, st_:128], v3(cur)[:, :, st_:128],
            v3(cur)[:, :, 0:128 - st_], ALU.add),
            wait=[(cv, em.count(cv))], inc=cv)
        cur, oth = oth, cur
    assert cur is cum
    cum_ready[0] = em.count(cv)
    for c in range(NCH):
        em.do("tensor", lambda e, c=c: e.transpose(
            pTf[:, :], sdt[:, 128 * c:128 * (c + 1)], sidf[0:16, 0:16]),
            wait=[(cv, em.count(cv)), (cs, em.count(cs)), (ld, em.count(ld))], inc=ct)
        em.do("scalar", lambda e, c=c: e.activation(
            dtpT[:, 16 * c:16 * (c + 1)], pTf[:, :], AF.Copy),
            wait=[(ct, em.count(ct))], inc=cs)
    for c in range(NCH):
        em.do("tensor", lambda e, c=c: e.transpose(
            pTf[:, :], cum[:, 128 * c:128 * (c + 1)], sidf[0:16, 0:16]),
            wait=[(cv, em.count(cv)), (cs, em.count(cs))], inc=ct)
        em.do("scalar", lambda e, c=c: e.activation(
            cumT[:, 16 * c:16 * (c + 1)], pTf[:, :], AF.Copy),
            wait=[(ct, em.count(ct))], inc=cs)
    em.do("tensor", lambda e: e.transpose(pR[0:1, 0:16], s16[:, 2:3], sidf[0:16, 0:16]),
          wait=[(ld, em.count(ld)), (cs, em.count(cs))], inc=ct)
    em.do("scalar", lambda e: e.activation(rrow0[0:1, :], pR[0:1, 0:16], AF.Copy),
          wait=[(ct, em.count(ct))], inc=cs)
    em.do("tensor", lambda e: e.matmul(pG[:, 0:16], ones_f[0:1, :],
          rrow0[0:1, :], start=True, stop=True, skip_group_check=True),
          wait=[(cs, em.count(cs)), (cv, em.count(cv))], inc=ct)
    em.do("vector", lambda e: e.tensor_copy(Dbc[:, :], pG[:, 0:16]),
          wait=[(ct, em.count(ct))], inc=cv)
    for seg in range(8):
        em.do("tensor", lambda e, seg=seg: e.matmul(
            pG[:, :], ones_f[0:1, :], nwrow[0:1, 128 * seg:128 * (seg + 1)],
            start=True, stop=True, skip_group_check=True),
            wait=[(cv, em.count(cv))], inc=ct)
        em.do("vector", lambda e, seg=seg: e.tensor_copy(
            nwbc[:, 128 * seg:128 * (seg + 1)], pG[:, :]),
            wait=[(ct, em.count(ct))], inc=cv)
    em.do("vector", lambda e: e.memset(Ssb[:, :], 0.0), inc=cv)
    em.do("vector", lambda e: e.memset(Sbf[:, :], 0.0), inc=cv)

    lb_mark = [0, 0]
    ms_mark = [0, 0]
    dtx_mark = [0, 0]
    yev_mark = [0, 0]
    msw_mark = [0, 0]
    yevD_mark = [0, 0]
    pY_cs = [0]
    pY_cv = [0]
    pS_cv = [0]
    for c in range(NCH):
        tsl = slice(128 * c, 128 * (c + 1))
        ctsl = slice(16 * c, 16 * (c + 1))
        for j in range(8):
            jpar = j % 2
            em.do("tensor", lambda e, j=j, tsl=tsl, jpar=jpar: e.transpose(
                pT[jpar][:, :], xbc[:, 4096 * j:4096 * (j + 1)][:, tsl], sident[:, :]),
                wait=[(cv, em.count(cv)), (cs, em.count(cs))], inc=ct)
            em.do("scalar", lambda e, j=j, jpar=jpar: e.activation(
                xT[:, 128 * j:128 * (j + 1)], pT[jpar][:, :], AF.Copy),
                wait=[(ct, em.count(ct))], inc=cs)
        for g in range(2):
            gpar = g % 2
            em.do("tensor", lambda e, g=g, tsl=tsl, gpar=gpar: e.transpose(
                pT[gpar][:, :], xbc[:, 4096 * (8 + g):4096 * (9 + g)][:, tsl],
                sident[:, :]),
                wait=[(cv, em.count(cv)), (cs, em.count(cs))], inc=ct)
            em.do("scalar", lambda e, g=g, gpar=gpar: e.activation(
                Bt[:, 128 * g:128 * (g + 1)], pT[gpar][:, :], AF.Copy),
                wait=[(ct, em.count(ct))], inc=cs)
        for g in range(2):
            em.do("tensor", lambda e, g=g, tsl=tsl: e.matmul(
                pG[:, :], xbc[:, 4096 * (8 + g):4096 * (9 + g)][:, tsl],
                xbc[:, 4096 * (10 + g):4096 * (11 + g)][:, tsl],
                start=True, stop=True, skip_group_check=True),
                wait=[(cv, em.count(cv)), (cs, em.count(cs))], inc=ct)
            em.do("vector", lambda e, g=g: e.tensor_tensor(
                Gs[g][:, :], pG[:, :], tri01[:, :], ALU.mult),
                wait=[(ct, em.count(ct))], inc=cv)
        em.do("vector", lambda e, ctsl=ctsl: e.tensor_scalar(
            ncumT[:, :], cumT[:, ctsl], -1.0, None, ALU.mult),
            wait=[(cs, em.count(cs)), (cv, em.count(cv))], inc=cv)
        em.do("scalar", lambda e, ctsl=ctsl: e.activation(
            fT[:, :], cumT[:, ctsl], AF.Exp),
            wait=[(cv, em.count(cv))], inc=cs)
        cpar = c % 2
        em.do("sync", lambda e, tsl=tsl, cpar=cpar: e.dma_start(
            out=rowscr[cpar], in_=cum[:, tsl]),
            wait=[(ct, rcuse_mark[cpar]), (cv, cum_ready[0])],
            inc="smrb", incby=16)
        em.do("sync", lambda e, cpar=cpar: e.dma_start(
            out=rowcat[cpar].ap().rearrange("p (h t) -> p h t", t=128),
            in_=rowscr[cpar].unsqueeze(0)),
            wait=[("smrb", em.count("smrb"))], inc="smrc", incby=16)
        rcv = em.count("smrc")
        em.do("tensor", lambda e, cpar=cpar: e.matmul(
            pG[:, 0:16], ones_f[0:1, :],
            rowcat[cpar][0:1, 127:2048:128],
            start=True, stop=True, skip_group_check=True),
            wait=[("smrc", rcv), (cv, em.count(cv))], inc=ct)

        em.do("vector", lambda e, ctsl=ctsl: e.scalar_tensor_tensor(
            wT[:, :], cumT[:, ctsl], -1.0, pG[:, 0:16], ALU.mult, ALU.add),
            wait=[(ct, em.count(ct))], inc=cv)
        em.do("scalar", lambda e: e.activation(wT[:, :], wT[:, :], AF.Exp),
              wait=[(cv, em.count(cv))], inc=cs)
        em.do("scalar", lambda e: e.activation(dtot[:, :], pG[:, 0:16], AF.Exp),
              wait=[(cv, em.count(cv)), (ct, em.count(ct))], inc=cs)
        pre_cs = em.count(cs)
        pre_cv = em.count(cv)

        def stage1(h):
            par = h % 2
            g = h // 8
            em.do("tensor", lambda e, h=h, par=par, cpar=c % 2: e.matmul(
                pL[par][:, :], ones_f[0:1, :],
                rowcat[cpar][0:1, 128 * h:128 * (h + 1)],
                start=True, stop=True, skip_group_check=True),
                wait=[("smrc", rcv), (cv, lb_mark[par])], inc=ct)
            lbv = em.count(ct)
            em.do("vector", lambda e, h=h, par=par: e.tensor_scalar(
                Ls[par][:, :], pL[par][:, :], ncumT[:, h:h + 1], 0.0,
                ALU.add, ALU.min),
                wait=[(ct, lbv), (cv, msw_mark[par])], inc=cv)
            lb_mark[par] = em.count(cv)
            em.do("scalar", lambda e, par=par: e.activation(
                Ls[par][:, :], Ls[par][:, :], AF.Exp),
                wait=[(cv, lb_mark[par])], inc=cs)
            lse = em.count(cs)
            em.do("vector", lambda e, g=g, par=par: e.tensor_tensor(
                Ms[par][:, :], Ls[par][:, :], Gs[g][:, :], ALU.mult),
                wait=[(cs, lse), (ct, ms_mark[par])], inc=cv)
            em.do("vector", lambda e, h=h, c=c, par=par: e.tensor_scalar(
                dtx[par][:, :], xT[:, 64 * h:64 * h + 64],
                dtpT[:, 16 * c + h:16 * c + h + 1], None, ALU.mult),
                wait=[(ct, dtx_mark[par]), (cs, pre_cs)], inc=cv)
            em.do("vector", lambda e, h=h, par=par: e.tensor_scalar(
                wdtx[par][:, :], dtx[par][:, :], wT[:, h:h + 1], None, ALU.mult),
                wait=[(cv, em.count(cv))], inc=cv)
            msw_mark[par] = em.count(cv)

        def stage2(h):
            par = h % 2
            g = h // 8
            em.do("tensor", lambda e, par=par: e.matmul(
                pY[:, 0:64], Ms[par][:, :], dtx[par][:, :],
                start=True, stop=True, skip_group_check=True),
                wait=[(cv, msw_mark[par]), (cs, pY_cs[0]), (cv, pY_cv[0])], inc=ct)
            em.do("tensor", lambda e, g=g, tsl=tsl, h=h: e.matmul(
                pY[:, 64:128], xbc[:, 4096 * (10 + g):4096 * (11 + g)][:, tsl],
                Sbf[:, 64 * h:64 * (h + 1)],
                start=True, stop=True, skip_group_check=True), inc=ct)
            em.do("tensor", lambda e, g=g, par=par: e.matmul(
                pS[:, :], Bt[:, 128 * g:128 * (g + 1)], wdtx[par][:, :],
                start=True, stop=True, skip_group_check=True),
                wait=[(cv, pS_cv[0])], inc=ct)
            ms_mark[par] = em.count(ct)
            dtx_mark[par] = em.count(ct)
            ymm = em.count(ct)
            em.do("scalar", lambda e, par=par: e.activation(
                yev[par][:, :], pY[:, 0:64], AF.Copy),
                wait=[(ct, ymm), (cv, yevD_mark[par])], inc=cs)
            yev_mark[par] = em.count(cs)
            pY_cs[0] = em.count(cs)
            em.do("vector", lambda e, h=h, par=par: e.scalar_tensor_tensor(
                yev[par][:, :], pY[:, 64:128], fT[:, h:h + 1], yev[par][:, :],
                ALU.mult, ALU.add),
                wait=[(cs, yev_mark[par])], inc=cv)
            pY_cv[0] = em.count(cv)
            em.do("vector", lambda e, h=h, par=par: e.scalar_tensor_tensor(
                ysb[:, 64 * h:64 * (h + 1)], xT[:, 64 * h:64 * h + 64],
                Dbc[:, h:h + 1], yev[par][:, :], ALU.mult, ALU.add),
                wait=[(cv, em.count(cv))], inc=cv)
            yevD_mark[par] = em.count(cv)
            em.do("vector", lambda e, h=h: e.scalar_tensor_tensor(
                Ssb[:, 64 * h:64 * (h + 1)], Ssb[:, 64 * h:64 * (h + 1)],
                dtot[:, h:h + 1], pS[:, :], ALU.mult, ALU.add),
                wait=[(ct, ymm), (cv, pS_cv[0])], inc=cv)
            pS_cv[0] = em.count(cv)

        for h in range(NH + 1):
            if h < NH:
                stage1(h)
            if h >= 1:
                stage2(h - 1)
        rcuse_mark[cpar] = em.count(ct)
        em.do("vector", lambda e: e.tensor_copy(Sbf[:, :], Ssb[:, :]),
              wait=[(cv, em.count(cv)), (ct, em.count(ct))], inc=cv)
        em.do("sync", lambda e, tsl=tsl: e.dma_start(out=gt[:, :], in_=gateT[tsl, :]),
              wait=[(cs, em.count(cs)), (cv, em.count(cv))], inc="smgt", incby=16)
        em.do("scalar", lambda e: e.activation(ysq[:, :], ysb[:, :], AF.Square),
              wait=[(cv, em.count(cv))], inc=cs)
        for g in range(2):
            em.do("vector", lambda e, g=g: e.reduce_sum(
                var2[:, g:g + 1], ysq[:, 512 * g:512 * (g + 1)], axis=AX.X),
                wait=[(cs, em.count(cs))], inc=cv)
        em.do("vector", lambda e: e.tensor_scalar(
            var2[:, :], var2[:, :], 1.0 / 512.0, EPS_LN, ALU.mult, ALU.add),
            wait=[(cv, em.count(cv))], inc=cv)
        em.do("scalar", lambda e: e.activation(var2[:, :], var2[:, :], AF.Ln),
              wait=[(cv, em.count(cv))], inc=cs)
        em.do("scalar", lambda e: e.activation(var2[:, :], var2[:, :], AF.Exp,
              scale=-0.5), wait=[(cs, em.count(cs))], inc=cs)
        for g in range(2):
            em.do("vector", lambda e, g=g: e.tensor_scalar(
                ysq[:, 512 * g:512 * (g + 1)], ysb[:, 512 * g:512 * (g + 1)],
                var2[:, g:g + 1], None, ALU.mult),
                wait=[(cs, em.count(cs))], inc=cv)
        em.do("vector", lambda e: e.tensor_tensor(ysq[:, :], ysq[:, :], nwbc[:, :],
              ALU.mult), wait=[(cv, em.count(cv))], inc=cv)
        em.do("scalar", lambda e: e.activation(gsil[:, :], gt[:, :], AF.Silu),
              wait=[("smgt", em.count("smgt")), (cv, em.count(cv))], inc=cs)
        em.do("vector", lambda e: e.tensor_tensor(yng[:, :], ysq[:, :], gsil[:, :],
              ALU.mult), wait=[(cs, em.count(cs)), (so, em.count(so))], inc=cv)
        if yngT is not None:
            em.do("sync", lambda e, tsl=tsl: e.dma_start(out=yngT[tsl, :],
                  in_=yng[:, :]), wait=[(cv, em.count(cv))], inc=so, incby=16)
        if ynT_loc is not None:
            for j in range(8):
                jpar = j % 2
                em.do("tensor", lambda e, j=j, jpar=jpar: e.transpose(
                    pT[jpar][:, :], yng[:, 128 * j:128 * (j + 1)], sident[:, :]),
                    wait=[(cv, em.count(cv)), (cs, em.count(cs))], inc=ct)
                em.do("scalar", lambda e, j=j, jpar=jpar: e.activation(
                    ytr[:, 128 * j:128 * (j + 1)], pT[jpar][:, :], AF.Copy),
                    wait=[(ct, em.count(ct))], inc=cs)
            em.do("sync", lambda e, c=c: e.dma_start(
                out=ynT_loc.rearrange("(j p) t -> p j t", p=128)[:, :,
                     128 * c:128 * (c + 1)],
                in_=ytr.ap().rearrange("p (j t) -> p j t", t=128)),
                wait=[(cs, em.count(cs))], inc=so, incby=16)
    em.wait("sync", so)
    em.phase_barrier()
    for f in (pR_f, pTf_f, pT_f, pS_f, pY_f) + tuple(reversed(pL_f)) + (pG_f,):
        f()
    for f in reversed(scopes):
        f()



def emit_l1ffn(em, xln4_hbm, routerT, router_b, expert_b, resid4, out_final,
               ident, identf):
    scopes = []
    def ssb(n, shp, dt=BF16):
        t, f = em.sb_scope(n, shp, dt); scopes.append(f); return t
    act = ssb("lf_act", [128, 32 * 1024])
    rsb = ssb("lf_r", [128, 32 * 64])
    ebT = ssb("lf_eb", [64, 4096])
    rbb = ssb("lf_rbb", [128, 64], F32)
    ones_f = ssb("lf_onef", [1, 128], F32)
    lg = ssb("lf_lg", [128, 64], F32)
    wrk = ssb("lf_wrk", [128, 64], F32)
    mwrk = ssb("lf_mwrk", [128, 64], F32)
    m1 = ssb("lf_m1", [128, 4], F32)
    m1f = ssb("lf_m1f", [128, 1], F32)
    msk = ssb("lf_msk", [128, 64])
    mskT = ssb("lf_mskT", [64, 1024])
    wsr = ssb("lf_wsr", [1, 1024], F32)
    sident = ssb("lf_id", [128, 128])
    sidf = ssb("lf_idf", [128, 128], F32)
    rt = ssb("lf_rt", [128, 512], F32)
    ev = ssb("lf_ev", [128, 512], F32)
    ev2 = ssb("lf_ev2", [128, 512], F32)
    plg, plg_f = em.ps_scope("lf_plg", [128, 64])
    pt_, pt_f = em.ps_scope("lf_pt", [128, 128], BF16)
    ptf, ptf_f = em.ps_scope("lf_ptf", [1, 128])
    pwb = [em.ps_scope("lf_pwb", [128, 512]) for _ in range(2)]
    pmx = [em.ps_scope("lf_pmx", [128, 512]) for _ in range(2)]
    ld, gl, cv, cs, ct, so = ("lfld", "lfgl", "lfv", "lfcs", "lfct", "lfso")
    em.do("sync", lambda e: e.dma_start(
        out=act.ap().rearrange("p (kk t) -> p kk t", t=1024),
        in_=xln4_hbm.rearrange("(kk p) t -> p kk t", p=128)), inc=ld, incby=16)
    em.do("sync", lambda e: e.dma_start(out=sident[:, :], in_=ident), inc=ld, incby=16)
    em.do("sync", lambda e: e.dma_start(out=sidf[:, :], in_=identf), inc=ld, incby=16)
    em.do("gpsimd", lambda e: e.dma_start(
        out=rsb.ap().rearrange("p (kk c) -> p kk c", c=64),
        in_=routerT.rearrange("(kk p) c -> p kk c", p=128)), inc=gl, incby=16)
    em.do("gpsimd", lambda e: e.dma_start(out=ebT[:, :], in_=expert_b), inc=gl, incby=16)
    em.do("gpsimd", lambda e: e.dma_start(out=lg[0:1, :], in_=router_b), inc=gl, incby=16)
    em.do("vector", lambda e: e.memset(ones_f[:, :], 1.0), inc=cv)
    em.do("tensor", lambda e: e.matmul(plg[:, :], ones_f[0:1, :], lg[0:1, :],
          start=True, stop=True, skip_group_check=True),
          wait=[(gl, em.count(gl)), (cv, em.count(cv))], inc=ct)
    em.do("vector", lambda e: e.tensor_copy(rbb[:, :], plg[:, :]),
          wait=[(ct, em.count(ct))], inc=cv)
    for tt in range(8):
        for kk in range(32):
            em.do("tensor", lambda e, kk=kk, tt=tt: e.matmul(
                plg[:, :], act[:, 1024 * kk + 128 * tt:1024 * kk + 128 * tt + 128],
                rsb[:, 64 * kk:64 * (kk + 1)],
                start=(kk == 0), stop=(kk == 31), skip_group_check=True),
                wait=([(ld, em.count(ld)), (gl, em.count(gl)), (cv, em.count(cv))]
                      if kk == 0 else ()), inc=ct)
        em.do("vector", lambda e: e.tensor_tensor(lg[:, :], plg[:, :], rbb[:, :],
              ALU.add), wait=[(ct, em.count(ct)), (cs, em.count(cs))], inc=cv)
        em.do("vector", lambda e: e.reduce_max(m1[:, 0:1], lg[:, :], axis=AX.X),
              wait=[(cv, em.count(cv))], inc=cv)
        em.do("vector", lambda e: e.tensor_scalar(m1[:, 1:2], m1[:, 0:1], -1.0, None,
              ALU.mult), wait=[(cv, em.count(cv))], inc=cv)
        em.do("scalar", lambda e: e.activation(lg[:, :], lg[:, :], AF.Exp,
              bias=m1[:, 1:2], accum_out=m1[:, 2:3]),
              wait=[(cv, em.count(cv))], inc=cs)
        em.do("vector", lambda e: e.reciprocal(m1[:, 3:4], m1[:, 2:3]),
              wait=[(cs, em.count(cs))], inc=cv)
        em.do("vector", lambda e: e.tensor_scalar(lg[:, :], lg[:, :], m1[:, 3:4],
              None, ALU.mult), wait=[(cv, em.count(cv))], inc=cv)
        em.do("vector", lambda e: e.tensor_copy(wrk[:, :], lg[:, :]),
              wait=[(cv, em.count(cv))], inc=cv)
        for it in range(8):
            em.do("vector", lambda e: e.reduce_max(m1[:, 0:1], wrk[:, :], axis=AX.X),
                  wait=[(cv, em.count(cv))], inc=cv)
            if it < 7:
                em.do("vector", lambda e: e.tensor_scalar(
                    mwrk[:, :], wrk[:, :], m1[:, 0:1], None, ALU.is_ge),
                    wait=[(cv, em.count(cv))], inc=cv)
                em.do("vector", lambda e: e.scalar_tensor_tensor(
                    wrk[:, :], mwrk[:, :], -4.0, wrk[:, :], ALU.mult, ALU.add),
                    wait=[(cv, em.count(cv))], inc=cv)
        em.do("vector", lambda e: e.tensor_scalar(msk[:, :], lg[:, :], m1[:, 0:1],
              None, ALU.is_ge), wait=[(cv, em.count(cv))], inc=cv)
        em.do("vector", lambda e: e.tensor_tensor(wrk[:, :], lg[:, :], msk[:, :],
              ALU.mult), wait=[(cv, em.count(cv))], inc=cv)
        em.do("vector", lambda e: e.reduce_sum(m1[:, 2:3], wrk[:, :], axis=AX.X),
              wait=[(cv, em.count(cv))], inc=cv)
        em.do("vector", lambda e: e.tensor_copy(msk[:, :], wrk[:, :]),
              wait=[(cv, em.count(cv))], inc=cv)
        em.do("vector", lambda e: e.tensor_copy(m1f[:, 0:1], m1[:, 2:3]),
              wait=[(cv, em.count(cv))], inc=cv)
        em.do("tensor", lambda e: e.transpose(pt_[0:64, :], msk[:, :], sident[:, :]),
              wait=[(cv, em.count(cv)), (cs, em.count(cs))], inc=ct)
        em.do("scalar", lambda e, tt=tt: e.activation(
            mskT[:, 128 * tt:128 * (tt + 1)], pt_[0:64, :], AF.Copy),
            wait=[(ct, em.count(ct))], inc=cs)
        em.do("tensor", lambda e: e.transpose(ptf[0:1, :], m1f[:, 0:1], sidf[:, :]),
              wait=[(cv, em.count(cv)), (cs, em.count(cs))], inc=ct)
        em.do("scalar", lambda e, tt=tt: e.activation(
            wsr[0:1, 128 * tt:128 * (tt + 1)], ptf[0:1, :], AF.Copy),
            wait=[(ct, em.count(ct))], inc=cs)
    for th in range(2):
        em.do("tensor", lambda e, th=th: e.matmul(
            pwb[th][0][:, :], ones_f[0:1, :], wsr[0:1, 512 * th:512 * (th + 1)],
            start=True, stop=True, skip_group_check=True),
            wait=[(cs, em.count(cs)), (cv, em.count(cv))], inc=ct)
    for hh in range(32):
        for th in range(2):
            em.do("tensor", lambda e, hh=hh, th=th: e.matmul(
                pmx[th][0][:, :], ebT[:, 128 * hh:128 * (hh + 1)],
                mskT[:, 512 * th:512 * (th + 1)],
                start=True, stop=True, skip_group_check=True),
                wait=[(cs, em.count(cs)), (cv, em.count(cv))], inc=ct)
            em.do("sync", lambda e, hh=hh, th=th: e.dma_start(
                out=rt[:, :], in_=resid4[128 * hh:128 * (hh + 1),
                                         512 * th:512 * (th + 1)]),
                wait=[(cv, em.count(cv))], inc="lfrt", incby=16)
            em.do("vector", lambda e, hh=hh, th=th: e.tensor_tensor(
                ev[:, :], act[:, 1024 * hh + 512 * th:1024 * hh + 512 * th + 512],
                pwb[th][0][:, :], ALU.mult),
                wait=[(ct, em.count(ct)), (so, em.count(so)), (cv, em.count(cv))], inc=cv)
            em.do("vector", lambda e, th=th: e.tensor_tensor(
                ev2[:, :], ev[:, :], pmx[th][0][:, :], ALU.add),
                wait=[(cv, em.count(cv))], inc=cv)
            em.do("vector", lambda e: e.tensor_tensor(
                ev[:, :], ev2[:, :], rt[:, :], ALU.add),
                wait=[(cv, em.count(cv)), ("lfrt", em.count("lfrt"))], inc=cv)
            em.do("sync", lambda e, hh=hh, th=th: e.dma_start(
                out=out_final[128 * hh:128 * (hh + 1), 512 * th:512 * (th + 1)],
                in_=ev[:, :]), wait=[(cv, em.count(cv))], inc=so, incby=16)
    em.wait("sync", so)
    em.phase_barrier()
    for pf in reversed([plg_f, pt_f, ptf_f] + [p[1] for p in pwb] + [p[1] for p in pmx]):
        pf()
    for f in reversed(scopes):
        f()



def _cc(em, kind, rg, in_ap, out_ap, sem):
    em.do("gpsimd", lambda e: e.collective_compute(
        kind, ALU.add if kind in ("AllReduce", "ReduceScatter") else ALU.bypass,
        replica_groups=rg, ins=[in_ap], outs=[out_ap]), inc=sem, incby=None)


G4 = [[0, 1, 2, 3], [4, 5, 6, 7]]
G8 = [list(range(8))]


def build_fused():
    em = Em("FU")
    x0 = em.dram_in("x0T", [H, TC], F32)
    gate_pk = em.dram_in("gate_pk", [2048, 1408], U8)
    up_pk = em.dram_in("up_pk", [2048, 1408], U8)
    down_pk = em.dram_in("down_pk", [5504, 512], U8)
    gate_sT = em.dram_in("gate_sT", [64, 1408], F32)
    up_sT = em.dram_in("up_sT", [64, 1408], F32)
    down_sT = em.dram_in("down_sT", [172, 512], F32)
    gate_zT = em.dram_in("gate_zT", [64, 1408], F32)
    up_zT = em.dram_in("up_zT", [64, 1408], F32)
    down_zT = em.dram_in("down_zT", [172, 512], F32)
    sel4 = None
    wqT = em.dram_in("wqT", [4096, 1536], F32)
    bq = em.dram_in("bq", [1, 1536], F32)
    invf = em.dram_in("invf", [1, 64], F32)
    ident = em.dram_in("ident", [128, 128], BF16)
    identf = em.dram_in("identf", [128, 128], F32)
    tri = em.dram_in("tri", [128, 128], F32)
    half = em.dram_in("half", [128, 64], F32)
    wdTs = em.dram_in("wdTs", [1024, 4096], F32)
    bd = em.dram_in("bd", [128, 32], F32)
    ipp = em.dram_in("ipp", [21, 4096, 128], F32)
    conv_w = em.dram_in("conv_w", [128, 48], F32)
    conv_b = em.dram_in("conv_b", [128, 12], F32)
    dtb = em.dram_in("dtb", [16, 1], F32)
    alog = em.dram_in("alog", [16, 1], F32)
    Dp = em.dram_in("Dp", [16, 1], F32)
    nw = em.dram_in("nw", [1, 1024], F32)
    opTs = em.dram_in("opTs", [1024, 4096], F32)
    routerT = em.dram_in("routerT", [4096, 64], F32)
    router_b = em.dram_in("router_b", [1, 64], F32)
    expert_b = em.dram_in("expert_b", [64, 4096], F32)
    out_final = em.dram_out("out_final", [H, TC], F32)
    xln1 = em.dram_tmp("xln1", [128, 32 * TC], BF16)
    gate_w = em.dram_tmp("gate_w", [1408, 4096], BF16)
    up_w = em.dram_tmp("up_w", [1408, 4096], BF16)
    down_w = em.dram_tmp("down_w", [11264, 512], BF16)
    xg1 = em.dram_tmp("xg1", [512, 32 * TC], BF16)
    gate_ag = em.dram_tmp("gate_ag", [8 * 1408, 4096], BF16)
    up_ag = em.dram_tmp("up_ag", [8 * 1408, 4096], BF16)
    down_ag = em.dram_tmp("down_ag", [8 * 11264, 512], BF16)
    qkvT = em.dram_tmp("qkvT", [2, 5, 128, 4096], BF16)
    vout = em.dram_tmp("vout", [2, 4096, 128], BF16)
    ctxT = em.dram_tmp("ctxT", [2, 512, 4096], BF16)
    rs1_in = em.dram_tmp("rs1_in", [2, 4, 4096, 512], BF16)
    rs1_out = em.dram_tmp("rs1_out", [2, 4096, 512], BF16)
    resid2 = em.dram_tmp("resid2", [H, TC], F32)
    xln2 = em.dram_tmp("xln2", [H, TC], BF16)
    hp_hbm = em.dram_tmp("hp", [88, 128, 1024], BF16)
    resid3 = em.dram_tmp("resid3", [H, TC], F32)
    xln3 = em.dram_tmp("xln3", [H, TC], BF16)
    xg3 = em.dram_tmp("xg3", [4 * H, TC], BF16)
    gateT = em.dram_tmp("gateT", [4096, 1024], BF16)
    hbc = em.dram_tmp("hbc", [1536, 4099], F32)
    dt_o = em.dram_tmp("dt_o", [16, 4096], F32)
    ynT_loc = em.dram_tmp("ynT_loc", [1024, 4096], BF16)
    rs2_in = em.dram_tmp("rs2_in", [2, 4, 4096, 512], BF16)
    rs2_out = em.dram_tmp("rs2_out", [2, 4096, 512], BF16)
    resid4 = em.dram_tmp("resid4", [H, TC], F32)
    xln4 = em.dram_tmp("xln4", [H, TC], BF16)

    emit_ln(em, "ln1", x0, xln1, pmajor=True)
    em.do("gpsimd", lambda e: e.collective_compute(
        "AllGather", ALU.bypass, replica_groups=G4,
        ins=[xln1[:, :]], outs=[xg1[:, :]]),
        wait=[("ln1so", em.count("ln1so"))], inc="ccX", incby=None)
    emit_dequant(em, gate_pk, up_pk, down_pk, gate_sT, up_sT, down_sT,
                 gate_zT, up_zT, down_zT, gate_w, up_w, down_w,
                 ag_outs=(gate_ag, up_ag, down_ag))
    em.do("gpsimd", lambda e: None, wait=[("ccX", em.count("ccX"))])
    em.phase_barrier()

    emit_qkv_rope(em, xg1, wqT, bq, invf, qkvT, vout, ident)
    emit_attention(em, qkvT, vout, ctxT, tri, half, ident)

    def ctx_rows(kc, tq, tch):
        pair, c2 = kc // 4, kc % 4
        return ctxT[pair, 128 * c2:128 * (c2 + 1),
                    1024 * tq + 512 * tch:1024 * tq + 512 * (tch + 1)]
    def _rsb(h):
        em.do("gpsimd", lambda e, h=h: e.collective_compute(
            "ReduceScatter", ALU.add, replica_groups=G4,
            ins=[rs1_in[h].rearrange("a b c -> (a b) c")],
            outs=[rs1_out[h]]),
            wait=[("dnso", em.count("dnso"))], inc="ccB", incby=None)
    emit_partial_gemm(em, "dn", wdTs, ctx_rows, rs1_in, half_cb=_rsb)
    em.do("gpsimd", lambda e: None, wait=[("ccB", em.count("ccB"))])
    em.phase_barrier()

    xbf_all, xbf_free = em.sb_scope("xbf", [128, 32 * 1024], BF16)
    emit_add_resid_ln(em, "r2", rs1_out, x0, bd, resid2, xbf_all)
    emit_ln_sbuf(em, "l2", xbf_all, xln2)
    em.barrier_on("ccA")

    def _wtile(ag, ii):
        r0 = 128 * ii
        return ag[r0:r0 + 128, :]
    emit_ffn(em, xln2, lambda ii: _wtile(gate_ag, ii), lambda ii: _wtile(up_ag, ii),
             down_ag.rearrange("(r i) c -> r i c", i=11264), hp_hbm,
             resid2, resid3, xbf_all)
    emit_ln_sbuf(em, "l3", xbf_all, xln3)
    xbf_free()
    em.phase_barrier()
    _cc(em, "AllGather", G4, xln3[:, :], xg3[:, :], "ccC")
    em.do("gpsimd", lambda e: None, wait=[("ccC", em.count("ccC"))])
    em.phase_barrier()

    zi, zi_free = em.sb_scope("zi", [128, 3], F32)
    em.do("vector", lambda e: e.memset(zi[:, :], 0.0), inc="zv")
    _zvc = em.count("zv")
    for rr in range(12):
        em.do("sync", lambda e, rr=rr: e.dma_start(
            out=hbc[128 * rr:128 * (rr + 1), 0:3], in_=zi[:, :]),
            wait=[("zv", _zvc)], inc="zs", incby=16)

    def xg3_tile(ts):
        r, t2 = ts // 2, ts % 2
        return xg3[4096 * r:4096 * (r + 1), :].rearrange(
            "(kk p) t -> p kk t", p=128)[:, :, 512 * t2:512 * (t2 + 1)]
    emit_inproj(em, xg3_tile, ipp, gateT, hbc, dt_o, ident)
    emit_ssm(em, hbc, dt_o, gateT, conv_w, conv_b, dtb, alog, Dp, nw,
             None, ident, identf, ynT_loc=ynT_loc)

    def yn_rows(kc, tq, tch):
        return ynT_loc[128 * kc:128 * (kc + 1),
                       1024 * tq + 512 * tch:1024 * tq + 512 * (tch + 1)]
    def _rsd(h):
        em.do("gpsimd", lambda e, h=h: e.collective_compute(
            "ReduceScatter", ALU.add, replica_groups=G4,
            ins=[rs2_in[h].rearrange("a b c -> (a b) c")],
            outs=[rs2_out[h]]),
            wait=[("opso", em.count("opso"))], inc="ccD", incby=None)
    emit_partial_gemm(em, "op", opTs, yn_rows, rs2_in, half_cb=_rsd)
    em.do("gpsimd", lambda e: None, wait=[("ccD", em.count("ccD"))])
    em.phase_barrier()
    zi_free()

    xbf2, xbf2_free = em.sb_scope("xbf2", [128, 32 * 1024], BF16)
    emit_add_resid_ln(em, "r4", rs2_out, resid3, None, resid4, xbf2)
    emit_ln_sbuf(em, "l4", xbf2, xln4)
    xbf2_free()
    emit_l1ffn(em, xln4, routerT, router_b, expert_b, resid4, out_final,
               ident, identf)
    return em


def kernel(**inputs):
    pc, shared = host_prep(inputs)
    em = build_fused()
    in_maps = []
    for c in range(NCORE):
        q = c % 4
        in_maps.append({
            "x0T": pc[c]["x0T"],
            "gate_pk": pc[c]["gate_pk"], "up_pk": pc[c]["up_pk"],
            "down_pk": pc[c]["down_pk"],
            "gate_sT": pc[c]["gate_sT"], "up_sT": pc[c]["up_sT"],
            "down_sT": pc[c]["down_sT"],
            "gate_zT": pc[c]["gate_zT"], "up_zT": pc[c]["up_zT"],
            "down_zT": pc[c]["down_zT"],
            "sel4": shared["sel4"],
            "wqT": pc[c]["wqkvT"], "bq": pc[c]["bqkv"][None, :],
            "invf": shared["invf"][None, :],
            "ident": shared["ident"], "identf": shared["ident_f"],
            "tri": shared["tri_mask"], "half": shared["half_mask"],
            "wdTs": pc[c]["wdTs"], "bd": shared["b_dense"],
            "ipp": pc[c]["ipj_panels"],
            "conv_w": _c(pc[c]["conv_w"].reshape(12, 128, 4).transpose(1, 0, 2)
                         .reshape(128, 48)),
            "conv_b": _c(pc[c]["conv_b"].reshape(12, 128).T),
            "dtb": pc[c]["dt_bias"][:, None], "alog": pc[c]["A_log"][:, None],
            "Dp": pc[c]["D_ssm"][:, None], "nw": pc[c]["norm_w"][None, :],
            "opTs": pc[c]["opTs"],
            "routerT": shared["routerT"], "router_b": shared["router_b"][None, :],
            "expert_b": shared["expert_b"],
        })
    res = _run(em, in_maps)
    out = np.empty((2, TB, H), np.float32)
    for c in range(NCORE):
        b, q = divmod(c, 4)
        out[b, TC * q:TC * (q + 1), :] = res[c]["out_final"][IPERM_H, :].T
    return out
